# version 06
# baseline (speedup 1.0000x reference)
import os as _os
import numpy as np
import ml_dtypes
import concourse.bass as bass
import concourse.mybir as mybir
from concourse.bass_utils import run_bass_kernel_spmd

F32 = mybir.dt.float32
BF16 = mybir.dt.bfloat16
ALU = mybir.AluOpType
AF = mybir.ActivationFunctionType
AX = mybir.AxisListType

C1 = float(3.0 * 2.0 ** 22)
EPS_G = float(2.0 ** -26)
SC_G = float(2.0 ** 26)
M29 = float(2.0 ** 29 - 1)
WMAX = 384

D = 1024
DH = 4096
NDC = D // 128
NHC = DH // 128

ENGINES = ("sync", "vector", "scalar", "tensor", "gpsimd")


def plan(T):
    slab = ((T + 4) + 127) // 128 * 128
    tiles = slab // 128

    def windows(h0, h1):
        res = []
        a = h0
        while True:
            w = min(WMAX, h1 - a)
            res.append((a, w))
            if a + w >= h1:
                break
            a = a + w - 2
        return res

    def btiles(s0, s1):
        res = []
        s = s0
        while s < s1:
            tw = min(128, s1 - s)
            res.append((s, tw))
            s += tw
        return res

    return {
        "T": T, "slab": slab, "tiles": tiles,
        "win1": windows(0, T + 4), "win2": windows(1, T + 3),
        "bt1": btiles(1, T + 3), "bt2": btiles(2, T + 2),
        "gcols": T + 2,
    }


class Step:
    __slots__ = ("eng", "kind", "fn", "deps", "inc", "seq", "small", "drain")

    def __init__(self, eng, kind, fn, deps, inc):
        self.eng, self.kind, self.fn, self.deps, self.inc = \
            eng, kind, fn, deps, inc
        self.seq = None
        self.drain = False


class Sched:
    def __init__(self, nc, scalar_auto_drain=True):
        self.nc = nc
        self.scalar_auto_drain = scalar_auto_drain
        self.sems = {}
        for e in ENGINES:
            self.sems[(e, "c")] = nc.ctx.enter_context(nc.semaphore(f"{e}_c"))
            self.sems[(e, "q")] = nc.ctx.enter_context(nc.semaphore(f"{e}_q"))
        self.sems[("gpsimd", "x")] = nc.ctx.enter_context(nc.semaphore("gp_x"))
        self.count = {k: 0 for k in self.sems}
        self.water = {e: {k: 0 for k in self.sems} for e in ENGINES}
        self.steps = []

    def step(self, eng, fn, deps=(), dma=False, small=False, async_c=False,
             drain=False):
        kind = "x" if async_c else ("q" if dma else "c")
        st = Step(eng, kind, fn,
                  [d for d in deps if d is not None], 16 if dma else 1)
        st.small = small and not dma
        st.drain = drain and not dma
        self.steps.append(st)
        return st

    def flush(self):
        steps, self.steps = self.steps, []
        if not steps:
            return
        for st in steps:
            key = (st.eng, st.kind)
            self.count[key] += st.inc
            st.seq = self.count[key]
        with self.nc.Block() as blk:
            for e in ENGINES:
                mine = [s for s in steps if s.eng == e]
                if not mine:
                    continue

                def body(engine, mine=mine, e=e):
                    for st in mine:
                        for d in st.deps:
                            dk = (d.eng, d.kind)
                            if d.eng == e and d.kind == "c":
                                continue
                            if d.seq > self.water[e][dk]:
                                engine.wait_ge(self.sems[dk], d.seq)
                                self.water[e][dk] = d.seq
                        inst = st.fn(engine)
                        if st.small or st.drain or (
                                self.scalar_auto_drain
                                and st.kind == "c" and e == "scalar"):
                            inst = engine.drain()
                        inst.then_inc(self.sems[(st.eng, st.kind)], st.inc)

                getattr(blk, e)(body)


def build_program(T, n_cores, cores_per_batch, use_cc=True, last_call=2):
    P = plan(T)
    tiles, slab, gcols = P["tiles"], P["slab"], P["gcols"]
    nc = bass.Bass("TRN2", target_bir_lowering=False,
                   detect_race_conditions=False)

    xslab = nc.dram_tensor("xslab", [slab, D], F32, kind="ExternalInput").ap()
    w1Td = nc.dram_tensor("w1T", [D, DH], F32, kind="ExternalInput").ap()
    w2Td = nc.dram_tensor("w2T", [DH, D], F32, kind="ExternalInput").ap()
    convw = nc.dram_tensor("convw", [128, 3 * NHC], F32, kind="ExternalInput").ap()
    convb = nc.dram_tensor("convb", [128, NHC], F32, kind="ExternalInput").ap()
    wvTd = nc.dram_tensor("wvT", [D, D], F32, kind="ExternalInput").ap()
    woTd = nc.dram_tensor("woT", [D, D], F32, kind="ExternalInput").ap()
    onecol = nc.dram_tensor("onecol", [1, 128], F32, kind="ExternalInput").ap()
    onescol = nc.dram_tensor("onescol", [128, 1], F32, kind="ExternalInput").ap()
    masksd = nc.dram_tensor("masks", [128, tiles], F32, kind="ExternalInput").ap()
    identf = nc.dram_tensor("identf", [128, 128], F32, kind="ExternalInput").ap()
    zpad = nc.dram_tensor("zpad", [128, D], F32, kind="ExternalInput").ap()
    halom = nc.dram_tensor("halomask", [128, tiles], F32, kind="ExternalInput").ap()
    mdin = (nc.dram_tensor("mdin", [128, 4], F32, kind="ExternalInput").ap()
            if last_call == -1 else None)
    outd = nc.dram_tensor("out", [T, D], F32, kind="ExternalOutput").ap()

    _dbg = dict(kind="ExternalOutput") if _os.environ.get("KDBG") else {}
    t2hbm = nc.dram_tensor("t2hbm", [DH, D], BF16)
    n2hbm = nc.dram_tensor("n2hbm", [128, NHC, gcols], BF16, **_dbg)
    x1hbm = nc.dram_tensor("x1hbm", [slab, D], F32, **_dbg)
    x2hbm = nc.dram_tensor("x2hbm", [slab, D], F32, **_dbg)
    ccin = nc.dram_tensor("ccin", [1, D], F32)
    ccout = nc.dram_tensor("ccout", [1, D], F32)
    avdbg = (nc.dram_tensor("avdbg", [128, D], F32, kind="ExternalOutput").ap()
             if _dbg else None)
    t1dbg = (nc.dram_tensor("t1dbg", [128, NDC * DH], BF16, kind="ExternalOutput").ap()
             if _dbg else None)
    n1dbg = (nc.dram_tensor("n1dbg", [128, NDC * slab], BF16, kind="ExternalOutput").ap()
             if _dbg else None)
    f1dbg = (nc.dram_tensor("f1dbg", [1, slab], F32, kind="ExternalOutput").ap()
             if _dbg else None)
    csdbg = (nc.dram_tensor("csdbg", [128, 4], F32, kind="ExternalOutput").ap()
             if _dbg else None)
    saccdbg = (nc.dram_tensor("saccdbg", [1, D], F32, kind="ExternalOutput").ap()
               if _dbg else None)
    stdbg = (nc.dram_tensor("stdbg", [1, D], F32, kind="ExternalOutput").ap()
             if _dbg else None)
    vsdbg = (nc.dram_tensor("vsdbg", [1, D], F32, kind="ExternalOutput").ap()
             if _dbg else None)
    s1dbg = (nc.dram_tensor("s1dbg", [128, D + 16], F32, kind="ExternalOutput").ap()
             if _dbg else None)

    groups = [list(range(b * cores_per_batch, (b + 1) * cores_per_batch))
              for b in range(max(1, n_cores // cores_per_batch))]

    S = Sched(nc)
    st = {}

    from contextlib import ExitStack
    with ExitStack() as _es:
        t1T = _es.enter_context(nc.sbuf_tensor("t1T", [128, NDC * DH], BF16))
        onec = _es.enter_context(nc.sbuf_tensor("onec", [1, 128], F32))
        onesc = _es.enter_context(nc.sbuf_tensor("onesc", [128, 1], F32))
        maskc = _es.enter_context(nc.sbuf_tensor("maskc", [128, tiles], F32))
        hmc = _es.enter_context(nc.sbuf_tensor("hmc", [128, tiles], F32))
        idf = _es.enter_context(nc.sbuf_tensor("idf", [128, 128], F32))
        wcs = _es.enter_context(nc.sbuf_tensor("wcs", [128, 3 * NHC], F32))
        wbs = _es.enter_context(nc.sbuf_tensor("wbs", [128, NHC], F32))
        cstar = _es.enter_context(nc.sbuf_tensor("cstar", [128, 1], F32))
        ncstar = _es.enter_context(nc.sbuf_tensor("ncstar", [128, 1], F32))
        epsc = _es.enter_context(nc.sbuf_tensor("epsc", [128, 1], F32))
        m1d = _es.enter_context(nc.sbuf_tensor("m1d", [128, 1], F32))
        m2d = _es.enter_context(nc.sbuf_tensor("m2d", [128, 1], F32))
        swb = _es.enter_context(nc.sbuf_tensor("swb", [128, 2], F32))
        f1row = _es.enter_context(nc.sbuf_tensor("f1row", [1, slab], F32))
        f2row = _es.enter_context(nc.sbuf_tensor("f2row", [1, slab + 128], F32))
        rm2row = _es.enter_context(nc.sbuf_tensor("rm2row", [1, WMAX], F32))
        s2row = _es.enter_context(nc.sbuf_tensor("s2row", [1, WMAX], F32))
        avtile = _es.enter_context(nc.sbuf_tensor("avtile", [128, D], F32))
        sTc = _es.enter_context(nc.sbuf_tensor("sTc", [128, NDC], F32))
        sacc = _es.enter_context(nc.sbuf_tensor("sacc", [1, D], F32))
        csstage = _es.enter_context(nc.sbuf_tensor("csstage", [128, 4], F32))
        dsmall = _es.enter_context(nc.sbuf_tensor("dsmall", [128, 4], F32))
        dstage = _es.enter_context(nc.sbuf_tensor("dstage", [128, D + 16], F32))
        qdbg = _es.enter_context(nc.sbuf_tensor("qdbg", [128, 2], F32))
        vsT = _es.enter_context(nc.sbuf_tensor("vsT", [128, NDC], F32))
        psA = _es.enter_context(nc.psum_tensor("psA", [128, 512], F32))
        psB = _es.enter_context(nc.psum_tensor("psB", [128, 512], F32))
        psX = _es.enter_context(nc.psum_tensor("psX", [128, 512], F32))
        psM0 = _es.enter_context(nc.psum_tensor("psM0", [128, 1024], F32))
        psM1 = _es.enter_context(nc.psum_tensor("psM1", [128, 1024], F32))
        consts = []
        for dst_sb, src_hbm in [(onec, onecol), (onesc, onescol),
                                (maskc, masksd), (idf, identf), (wcs, convw),
                                (wbs, convb), (hmc, halom)]:
            consts.append(S.step(
                "sync", lambda e, d=dst_sb, s=src_hbm:
                e.dma_start(out=d[:, :], in_=s), dma=True))

        def _cstar(e):
            e.tensor_scalar(cstar[:, :], onesc[:, :], EPS_G, 1.0,
                            ALU.add, ALU.subtract)
            e.drain()
            e.tensor_scalar(cstar[:, :], cstar[:, :], SC_G, None, ALU.mult)
            e.drain()
            e.tensor_scalar(cstar[:, :], cstar[:, :], M29, 1.0,
                            ALU.mult, ALU.add)
            e.drain()
            e.tensor_scalar(cstar[:, :], cstar[:, :], C1, None, ALU.mult)
            e.drain()
            e.tensor_scalar(ncstar[:, :], cstar[:, :], -1.0, None, ALU.mult)
            e.tensor_scalar(epsc[:, :], onesc[:, :], 1e-5, None, ALU.mult)
            e.drain()
            return e.memset(f2row[:, :], 0.0)
        st["cstar"] = S.step("vector", _cstar, deps=consts)
        S.flush()

        def ternarize(wTd, nchunks, ccols, widx, tdst_sb, spill_hbm, gate=None):
            tg = f"_t{widx}"
            done = []
            nsub = ccols // 1024
            with (
                nc.sbuf_tensor("wfa" + tg, [128, 2 * ccols], F32) as wfa,
                nc.sbuf_tensor("ub" + tg, [128, 2 * 1024], F32) as ub,
                nc.sbuf_tensor("tch" + tg, [128, 2 * ccols], BF16) as tch,
                nc.sbuf_tensor("cacc" + tg, [128, 1], F32) as cacc,
                nc.sbuf_tensor("cx" + tg, [128, 1], F32) as cx,
                nc.sbuf_tensor("mt" + tg, [1, 1], F32) as mt,
            ):
                ld1 = {}
                sm = {}
                for c in range(nchunks):
                    ld1[c] = S.step("sync", lambda e, c=c: e.dma_start(
                        out=wfa[:, (c % 2) * ccols:(c % 2 + 1) * ccols],
                        in_=wTd[c * 128:(c + 1) * 128, :]),
                        deps=[sm.get(c - 2), gate if c < 2 else None],
                        dma=True)

                    def _s(e, c=c):
                        e.reduce_sum(cx[:, :],
                                     wfa[:, (c % 2) * ccols:(c % 2 + 1) * ccols],
                                     AX.X, apply_absolute_value=True)
                        e.drain()
                        if c == 0:
                            return e.tensor_copy(cacc[:, :], cx[:, :])
                        return e.tensor_add(cacc[:, :], cacc[:, :], cx[:, :])
                    sm[c] = S.step("vector", _s, deps=[ld1[c]], small=True)

                cs = S.step("tensor", lambda e: e.matmul(
                    psA[0:1, 0:1], cacc[:, :], onesc[:, :],
                    start=True, stop=True), deps=[sm[nchunks - 1]] + consts)

                def _mean(e):
                    e.tensor_scalar(mt[:, :], psA[0:1, 0:1],
                                    1.0 / (nchunks * 128 * ccols), None,
                                    ALU.mult)
                    e.drain()
                    return e.tensor_scalar(mt[:, :], mt[:, :], 1e-5, None,
                                           ALU.max)
                mean = S.step("vector", _mean, deps=[cs], small=True)

                bc = S.step("tensor", lambda e: e.matmul(
                    psB[:, 0:1], onec[:, :], mt[:, :], start=True, stop=True),
                    deps=[mean])

                def _mc(e):
                    e.tensor_copy(cx[:, :], psB[:, 0:1])
                    e.drain()
                    e.reciprocal(swb[:, widx:widx + 1], cx[:, :])
                    md = m1d if widx == 0 else m2d
                    sc = (1.0 / 127.0) if widx == 0 else (0.5 / 127.0)
                    e.tensor_scalar(md[:, :], cx[:, :], sc, None, ALU.mult)
                    return e.drain()
                mc = S.step("vector", _mc, deps=[bc, st["cstar"]])

                ld2 = {}
                clip = {}
                spills = {}
                for c in range(nchunks):
                    lastclip = clip.get((c - 2, nsub - 1))
                    ld2[c] = S.step("sync", lambda e, c=c: e.dma_start(
                        out=wfa[:, (c % 2) * ccols:(c % 2 + 1) * ccols],
                        in_=wTd[c * 128:(c + 1) * 128, :]),
                        deps=[mc, lastclip,
                              sm.get(nchunks - 1) if c < 2 else None],
                        dma=True)
                    for u in range(nsub):
                        L = c * nsub + u
                        pL = L - 2
                        pclip = (clip.get((pL // nsub, pL % nsub))
                                 if pL >= 0 else None)

                        def _sc(e, c=c, u=u, L=L):
                            src = wfa[:, (c % 2) * ccols + u * 1024:
                                      (c % 2) * ccols + (u + 1) * 1024]
                            uu = ub[:, (L % 2) * 1024:(L % 2 + 1) * 1024]
                            e.activation(uu, src, AF.Identity,
                                         scale=swb[:, widx:widx + 1])
                            e.activation(uu, uu, AF.Identity, bias=cstar[:, :])
                            return e.activation(uu, uu, AF.Identity,
                                                bias=ncstar[:, :])
                        scs = S.step("scalar", _sc, deps=[ld2[c], mc, pclip])

                        def _cl(e, c=c, u=u, L=L):
                            uu = ub[:, (L % 2) * 1024:(L % 2 + 1) * 1024]
                            if tdst_sb is not None:
                                dstv = tdst_sb[:, c * ccols + u * 1024:
                                               c * ccols + (u + 1) * 1024]
                            else:
                                dstv = tch[:, (c % 2) * ccols + u * 1024:
                                           (c % 2) * ccols + (u + 1) * 1024]
                            return e.tensor_scalar(dstv, uu, 1.0, -1.0,
                                                   ALU.min, ALU.max)
                        clip[(c, u)] = S.step(
                            "vector", _cl,
                            deps=[scs, spills.get(c - 2)])

                    if spill_hbm is not None:
                        spills[c] = S.step("scalar", lambda e, c=c:
                                           e.dma_start(
                                               out=spill_hbm[c * 128:
                                                             (c + 1) * 128, :],
                                               in_=tch[:, (c % 2) * ccols:
                                                       (c % 2 + 1) * ccols]),
                                           deps=[clip[(c, nsub - 1)]],
                                           dma=True)
                        done.append(spills[c])
                    else:
                        done.append(clip[(c, nsub - 1)])
                S.flush()
            return done

        if last_call == -1:
            def _md(e):
                return e.dma_start(out=dsmall[:, :], in_=mdin)
            _l = S.step("sync", _md, dma=True)
            def _md2(e):
                e.tensor_copy(m1d[:, :], dsmall[:, 0:1])
                e.tensor_copy(m2d[:, :], dsmall[:, 1:2])
                e.tensor_copy(swb[:, :], dsmall[:, 2:4])
                return e.drain()
            _c = S.step("vector", _md2, deps=[_l, st["cstar"]])
            S.flush()
            t1_done = [_c]
            t2_done = [_c]
        else:
            t1_done = ternarize(w1Td, NDC, DH, 0, t1T, None)
            t2_done = ternarize(w2Td, NHC, D, 1, None, t2hbm,
                                gate=t1_done[-1])

        zsteps = []
        with nc.sbuf_tensor("zt", [128, D], F32) as zt:
            zld = S.step("sync", lambda e: e.dma_start(out=zt[:, :], in_=zpad),
                         deps=[t2_done[-1]], dma=True)
            zs = [(0, 1)]
            r = T + 3
            while r < slab:
                h = min(128, slab - r)
                zs.append((r, h))
                r += h
            for (r, h) in zs:
                zsteps.append(S.step("gpsimd", lambda e, r=r, h=h: e.dma_start(
                    out=x1hbm[r:r + h, :], in_=zt[0:h, :]),
                    deps=[zld], dma=True))
            S.flush()

        st["psx_free"] = st["cstar"]

        def s1_phase(call, n1T):
            tg = f"_s{call}"
            src = xslab if call == 1 else x1hbm
            with (
                nc.sbuf_tensor("xr" + tg, [128, 3 * D], F32) as xr,
                nc.sbuf_tensor("x2r" + tg, [128, 2 * D], F32) as x2r,
                nc.sbuf_tensor("lnb" + tg, [128, D], F32) as lnb,
                nc.sbuf_tensor("tqb" + tg, [128, D], F32) as tqb,
                nc.sbuf_tensor("n1r" + tg, [128, 2 * D], BF16) as n1r,
                nc.sbuf_tensor("bns" + tg, [128, 12], F32) as bns,
                nc.sbuf_tensor("agg" + tg, [128, 2], F32) as agg,
                nc.sbuf_tensor("rstd" + tg, [128, 1], F32) as rstd,
                nc.sbuf_tensor("nrs" + tg, [128, 1], F32) as nrs,
                nc.sbuf_tensor("nmr" + tg, [128, 1], F32) as nmr,
                nc.sbuf_tensor("rmc" + tg, [128, 1], F32) as rmc,
                nc.sbuf_tensor("s1c" + tg, [128, 1], F32) as s1c,
                nc.sbuf_tensor("f1c" + tg, [128, 1], F32) as f1c,
            ):
                ld, x2s, x2sp, stat, lng, pres, rnd, trn, f1t, f1e, qsd = \
                    {}, {}, {}, {}, {}, {}, {}, {}, {}, {}, {}
                extra = ([st["x1w"], st["av_done"]] + zsteps
                         if call == 2 else [t2_done[-1]])
                for k in range(tiles):
                    ld[k] = S.step("sync", lambda e, k=k: e.dma_start(
                        out=xr[:, (k % 3) * D:(k % 3 + 1) * D],
                        in_=src[k * 128:(k + 1) * 128, :]),
                        deps=[(lng.get(k - 3) if call == 1 else x2s.get(k - 3))]
                        + (extra if k < 3 else []), dma=True)

                    if call == 2:
                        def _x2(e, k=k):
                            dst = x2r[:, (k % 2) * D:(k % 2 + 1) * D]
                            e.tensor_add(dst,
                                         xr[:, (k % 3) * D:(k % 3 + 1) * D],
                                         avtile[:, :])
                            return e.tensor_scalar(dst, dst, hmc[:, k:k + 1],
                                                   None, ALU.mult)
                        x2s[k] = S.step("vector", _x2,
                                        deps=[ld[k], st["av_done"],
                                              x2sp.get(k - 2), lng.get(k - 2)])
                        x2sp[k] = S.step("scalar", lambda e, k=k: e.dma_start(
                            out=x2hbm[k * 128:(k + 1) * 128, :],
                            in_=x2r[:, (k % 2) * D:(k % 2 + 1) * D]),
                            deps=[x2s[k]], dma=True)

                    def xt(k):
                        return (x2r[:, (k % 2) * D:(k % 2 + 1) * D]
                                if call == 2
                                else xr[:, (k % 3) * D:(k % 3 + 1) * D])

                    def _st(e, k=k):
                        e.bn_stats(bns[:, 0:6], xt(k)[:, 0:512])
                        e.bn_stats(bns[:, 6:12], xt(k)[:, 512:1024])
                        e.drain()
                        return e.bn_aggr(agg[:, :], bns[:, :])
                    stat[k] = S.step("vector", _st,
                                     deps=[(ld[k] if call == 1 else x2s[k]),
                                           lng.get(k - 1)], small=True)

                    sq = S.step("scalar", lambda e: e.activation(
                        rstd[:, :], agg[:, 1:2], AF.Sqrt, bias=epsc[:, :]),
                        deps=[stat[k], pres.get(k - 1)], small=True)

                    def _rv(e):
                        e.reciprocal(rstd[:, :], rstd[:, :])
                        e.drain()
                        e.tensor_scalar(nrs[:, :], rstd[:, :], -1.0,
                                        None, ALU.mult)
                        return e.drain()
                    rv = S.step("vector", _rv, deps=[sq])

                    def _ln(e, k=k):
                        e.activation(nmr[:, :], agg[:, 0:1], AF.Identity,
                                     scale=nrs[:, :])
                        e.drain()
                        return e.activation(lnb[:, :], xt(k), AF.Identity,
                                            scale=rstd[:, :], bias=nmr[:, :])
                    lng[k] = S.step("scalar", _ln, deps=[rv])

                    def _qs(e, k=k):
                        e.reduce_max(rmc[:, :], lnb[:, :], AX.X,
                                     apply_absolute_value=True)
                        e.drain()
                        e.tensor_scalar(rmc[:, :], rmc[:, :], 1e-5, None,
                                        ALU.max)
                        e.drain()
                        e.reciprocal(s1c[:, :], rmc[:, :])
                        e.drain()
                        e.tensor_scalar(s1c[:, :], s1c[:, :], 127.0, None,
                                        ALU.mult)
                        e.tensor_tensor(f1c[:, :], rmc[:, :],
                                        m1d[:, :], ALU.mult)
                        return e.drain()
                    qsd[k] = S.step("vector", _qs,
                                    deps=[lng[k], f1t.get(k - 1)])

                    pres[k] = S.step("scalar", lambda e, k=k: e.activation(
                        tqb[:, :], lnb[:, :], AF.Identity, scale=s1c[:, :]),
                        deps=[qsd[k], rnd.get(k - 1)])

                    def _rnd(e, k=k):
                        e.tensor_scalar(tqb[:, :], tqb[:, :], cstar[:, :],
                                        None, ALU.add)
                        return e.tensor_scalar(
                            n1r[:, (k % 2) * D:(k % 2 + 1) * D], tqb[:, :],
                            cstar[:, :], None, ALU.subtract)
                    rnd[k] = S.step("vector", _rnd,
                                    deps=[pres[k], trn.get(k - 2)])

                    for i in range(NDC):
                        t = S.step(
                            "scalar", lambda e, k=k, i=i:
                            e.dma_start_transpose(
                                n1T[:, i * slab + k * 128:
                                    i * slab + (k + 1) * 128],
                                n1r[:, (k % 2) * D + i * 128:
                                    (k % 2) * D + (i + 1) * 128]),
                            deps=[rnd[k]], dma=True)
                    trn[k] = t

                    f1t[k] = S.step("tensor", lambda e: e.transpose(
                        psX[0:1, 0:128], f1c[:, :], idf[:, :]),
                        deps=[qsd[k], f1e.get(k - 1), st["psx_free"]])
                    f1e[k] = S.step("vector", lambda e, k=k: e.tensor_copy(
                        f1row[:, k * 128:(k + 1) * 128], psX[0:1, 0:128]),
                        deps=[f1t[k]], small=True)
                S.flush()
                st["f1row_done"] = f1e[tiles - 1]
                st["n1T_done"] = trn[tiles - 1]
                st["psx_free"] = f1e[tiles - 1]
                if call == 2:
                    st["x2sp_last"] = x2sp[tiles - 1]

        def a_phase(call, n1T):
            tg = f"_a{call}"
            wins = P["win1"] if call == 1 else P["win2"]
            coff = 1 if call == 1 else 2
            gw = WMAX - 2
            spills = []
            with (
                nc.sbuf_tensor("Gwin" + tg, [128, NHC * gw], F32) as Gwin,
                nc.sbuf_tensor("n2w" + tg, [128, 2 * gw], BF16) as n2w,
                nc.sbuf_tensor("Hb" + tg, [128, 2 * WMAX], F32) as Hb,
                nc.sbuf_tensor("Cb" + tg, [128, 2 * WMAX], F32) as Cb,
                nc.sbuf_tensor("absb" + tg, [128, WMAX], F32) as absb,
                nc.sbuf_tensor("n2tmp" + tg, [128, WMAX], F32) as n2tmp,
                nc.sbuf_tensor("Macc" + tg, [128, WMAX], F32) as Macc,
                nc.sbuf_tensor("f1b" + tg, [128, WMAX], F32) as f1b,
                nc.sbuf_tensor("s2b" + tg, [128, WMAX], F32) as s2b,
            ):
                deq, a1, a23, sab, mx, mul, rnds, spl = \
                    {}, {}, {}, {}, {}, {}, {}, {}
                prev_s2e = None
                for w, (a, wd) in enumerate(wins):
                    gl = wd - 2
                    bc = S.step("tensor", lambda e, a=a, wd=wd: e.matmul(
                        psX[:, 0:wd], onec[:, :], f1row[:, a:a + wd],
                        start=True, stop=True),
                        deps=[st["f1row_done"], st["psx_free"], prev_s2e])
                    f1be = S.step("vector", lambda e, wd=wd: e.tensor_copy(
                        f1b[:, 0:wd], psX[:, 0:wd]), deps=[bc])

                    for j in range(NHC):
                        ps = psA if j % 2 == 0 else psB
                        pdq = (deq.get((w, j - 2)) if j >= 2
                               else deq.get((w - 1, NHC + j - 2)))

                        def _mm(e, a=a, wd=wd, j=j, ps=ps):
                            last = None
                            for i in range(NDC):
                                last = e.matmul(
                                    ps[:, 0:wd],
                                    t1T[:, i * DH + j * 128:
                                        i * DH + (j + 1) * 128],
                                    n1T[:, i * slab + a:i * slab + a + wd],
                                    start=(i == 0), stop=(i == NDC - 1))
                            return last
                        mm = S.step("tensor", _mm,
                                    deps=[st["n1T_done"], t1_done[-1], pdq])

                        deq[(w, j)] = S.step(
                            "vector", lambda e, wd=wd, j=j, ps=ps:
                            e.tensor_mul(
                                Hb[:, (j % 2) * WMAX:(j % 2) * WMAX + wd],
                                ps[:, 0:wd], f1b[:, 0:wd]),
                            deps=[mm, f1be, a23.get((w, j - 2)) if j >= 2
                                  else a23.get((w - 1, NHC + j - 2))])

                        a1[(w, j)] = S.step(
                            "scalar", lambda e, gl=gl, j=j: e.activation(
                                Cb[:, (j % 2) * WMAX:(j % 2) * WMAX + gl],
                                Hb[:, (j % 2) * WMAX:(j % 2) * WMAX + gl],
                                AF.Identity, scale=wcs[:, 3 * j:3 * j + 1]),
                            deps=[deq[(w, j)],
                                  sab.get((w, j - 2)) if j >= 2
                                  else sab.get((w - 1, NHC + j - 2))])

                        def _a23(e, gl=gl, j=j):
                            H0 = Hb[:, (j % 2) * WMAX:]
                            Cj = Cb[:, (j % 2) * WMAX:]
                            e.scalar_tensor_tensor(
                                Cj[:, 0:gl], H0[:, 1:1 + gl],
                                wcs[:, 3 * j + 1:3 * j + 2], Cj[:, 0:gl],
                                ALU.mult, ALU.add)
                            return e.scalar_tensor_tensor(
                                Cj[:, 0:gl], H0[:, 2:2 + gl],
                                wcs[:, 3 * j + 2:3 * j + 3], Cj[:, 0:gl],
                                ALU.mult, ALU.add)
                        a23[(w, j)] = S.step("vector", _a23, deps=[a1[(w, j)]])

                        def _sab(e, gl=gl, j=j):
                            e.activation(Gwin[:, j * gw:j * gw + gl],
                                         Cb[:, (j % 2) * WMAX:
                                            (j % 2) * WMAX + gl],
                                         AF.Silu, bias=wbs[:, j:j + 1])
                            return e.activation(absb[:, 0:gl],
                                                Gwin[:, j * gw:j * gw + gl],
                                                AF.Abs)
                        sab[(w, j)] = S.step(
                            "scalar", _sab,
                            deps=[a23[(w, j)], mul.get((w - 1, j)),
                                  mx.get((w, j - 1))])

                        def _mx(e, gl=gl, j=j):
                            if j == 0:
                                return e.tensor_copy(Macc[:, 0:gl],
                                                     absb[:, 0:gl])
                            return e.tensor_max(Macc[:, 0:gl], Macc[:, 0:gl],
                                                absb[:, 0:gl])
                        mx[(w, j)] = S.step("vector", _mx, deps=[sab[(w, j)]])

                    rm = S.step("gpsimd", lambda e, gl=gl: e.tensor_reduce(
                        rm2row[:, 0:gl], Macc[:, 0:gl], AX.C, ALU.max),
                        deps=[mx[(w, NHC - 1)]], small=True)

                    def _s2(e, a=a, gl=gl):
                        e.tensor_scalar(rm2row[:, 0:gl], rm2row[:, 0:gl],
                                        1e-5, None, ALU.max)
                        e.drain()
                        e.reciprocal(s2row[:, 0:gl], rm2row[:, 0:gl])
                        e.drain()
                        e.tensor_scalar(s2row[:, 0:gl], s2row[:, 0:gl],
                                        127.0, None, ALU.mult)
                        e.tensor_scalar(f2row[:, a + 1:a + 1 + gl],
                                        rm2row[:, 0:gl],
                                        m2d[0:1, 0:1], None, ALU.mult)
                        return e.drain()
                    s2st = S.step("vector", _s2, deps=[rm])

                    s2bc = S.step("tensor", lambda e, gl=gl: e.matmul(
                        psX[:, 0:gl], onec[:, :], s2row[:, 0:gl],
                        start=True, stop=True), deps=[s2st, f1be])
                    s2e = S.step("vector", lambda e, gl=gl: e.tensor_copy(
                        s2b[:, 0:gl], psX[:, 0:gl]), deps=[s2bc])
                    prev_s2e = s2e
                    st["psx_free"] = s2e

                    for j in range(NHC):
                        mul[(w, j)] = S.step(
                            "vector", lambda e, gl=gl, j=j: e.tensor_mul(
                                n2tmp[:, 0:gl], Gwin[:, j * gw:j * gw + gl],
                                s2b[:, 0:gl]),
                            deps=[s2e, sab[(w, j)], rnds.get((w, j - 1))])

                        def _rn(e, gl=gl, j=j):
                            e.activation(n2tmp[:, 0:gl], n2tmp[:, 0:gl],
                                         AF.Identity, bias=cstar[:, :])
                            return e.activation(
                                n2w[:, (j % 2) * gw:(j % 2) * gw + gl],
                                n2tmp[:, 0:gl], AF.Identity,
                                bias=ncstar[:, :])
                        rnds[(w, j)] = S.step(
                            "scalar", _rn,
                            deps=[mul[(w, j)], spl.get((w, j - 2)) if j >= 2
                                  else spl.get((w - 1, NHC + j - 2))])

                        spl[(w, j)] = S.step(
                            "scalar", lambda e, a=a, gl=gl, j=j: e.dma_start(
                                out=n2hbm[:, j, a + 1 - coff:
                                          a + 1 - coff + gl],
                                in_=n2w[:, (j % 2) * gw:(j % 2) * gw + gl]),
                            deps=[rnds[(w, j)]], dma=True)
                        spills.append(spl[(w, j)])
                S.flush()
            st["spills_last"] = spills[-1]
            st["f2row_done"] = spills[-1]

        def b_phase(call):
            tg = f"_b{call}"
            bts = P["bt1"] if call == 1 else P["bt2"]
            coff = 1 if call == 1 else 2
            resid = xslab if call == 1 else x2hbm
            nbt = len(bts)
            with (
                nc.sbuf_tensor("t2T" + tg, [128, NHC * D], BF16) as t2T,
                nc.sbuf_tensor("n2t" + tg, [128, 2 * NHC * 128], BF16) as n2t,
                nc.sbuf_tensor("xrb" + tg, [128, 2 * D], F32) as xrb,
                nc.sbuf_tensor("odb" + tg, [128, D], F32) as odb,
                nc.sbuf_tensor("x1b" + tg, [128, 2 * D], F32) as x1b,
                nc.sbuf_tensor("f2c" + tg, [128, len(bts)], F32) as f2c,
                nc.sbuf_tensor("ln2b" + tg, [128, D], F32) as ln2b,
                nc.sbuf_tensor("bns2" + tg, [128, 12], F32) as bns2,
                nc.sbuf_tensor("agg2" + tg, [128, 2], F32) as agg2,
                nc.sbuf_tensor("rstd2" + tg, [128, 1], F32) as rstd2,
                nc.sbuf_tensor("nrs2" + tg, [128, 1], F32) as nrs2,
                nc.sbuf_tensor("nmr2" + tg, [128, 1], F32) as nmr2,
            ):
                t2ld = []
                for c in range(NHC):
                    t2ld.append(S.step("sync", lambda e, c=c: e.dma_start(
                        out=t2T[:, c * D:(c + 1) * D],
                        in_=t2hbm[c * 128:(c + 1) * 128, :]),
                        deps=[t2_done[-1], st["spills_last"]], dma=True))

                n2ld, xld, f2t, f2e, mm2, od, x1s, ln2g, msum, x1dma = \
                    {}, {}, {}, {}, {}, {}, {}, {}, {}, {}
                vacc = {}
                for k, (sb0, tw) in enumerate(bts):
                    f2t[k] = S.step("tensor", lambda e, sb0=sb0: e.transpose(
                        psX[0:128, 0:1], f2row[:, sb0:sb0 + 128],
                        idf[0:1, 0:1]),
                        deps=[st["f2row_done"], st["psx_free"],
                              f2e.get(k - 1)])
                    f2e[k] = S.step("scalar", lambda e, k=k: e.copy(
                        f2c[:, k:k + 1], psX[0:128, 0:1]), deps=[f2t[k]],
                        small=True)
                    st["psx_free"] = f2e[k]
                for k, (sb0, tw) in enumerate(bts):
                    n2ld[k] = S.step(
                        "gpsimd", lambda e, k=k, sb0=sb0, tw=tw: e.dma_start(
                            out=n2t[:, (k % 2) * NHC * 128:
                                    (k % 2 + 1) * NHC * 128]
                            .rearrange("p (c t) -> p c t", c=NHC)[:, :, 0:tw],
                            in_=n2hbm[:, :, sb0 - coff:sb0 - coff + tw]),
                        deps=[st["spills_last"], mm2.get(k - 2)], dma=True)

                    xld[k] = S.step(
                        "sync", lambda e, k=k, sb0=sb0, tw=tw: e.dma_start(
                            out=xrb[:, (k % 2) * D:(k % 2 + 1) * D][0:tw, :],
                            in_=resid[sb0:sb0 + tw, :]),
                        deps=[x1s.get(k - 2)] +
                        (([st["x2sp_last"]] if call == 2
                          else [st["spills_last"]]) if k < 2 else []),
                        dma=True)

                    def _m2(e, k=k, tw=tw):
                        psM = psM0 if k % 2 == 0 else psM1
                        last = None
                        for j in range(NHC):
                            stt = n2t[:, (k % 2) * NHC * 128 + j * 128:
                                      (k % 2) * NHC * 128 + j * 128 + tw]
                            for h in range(2):
                                last = e.matmul(
                                    psM[0:tw, h * 512:(h + 1) * 512], stt,
                                    t2T[:, j * D + h * 512:
                                        j * D + (h + 1) * 512],
                                    start=(j == 0), stop=(j == NHC - 1))
                        return last
                    mm2[k] = S.step("tensor", _m2,
                                    deps=[n2ld[k], od.get(k - 2), t2ld[-1]])

                    od[k] = S.step("scalar", lambda e, k=k, tw=tw:
                                   e.activation(
                                       odb[0:tw, :],
                                       (psM0 if k % 2 == 0 else psM1)[0:tw, :],
                                       AF.Identity,
                                       scale=f2c[0:tw, k:k + 1]),
                                   deps=[mm2[k], f2e[len(bts) - 1],
                                         x1s.get(k - 1)])

                    def _x1(e, k=k, tw=tw):
                        xv = xrb[:, (k % 2) * D:(k % 2 + 1) * D][0:tw, :]
                        x1v = x1b[:, (k % 2) * D:(k % 2 + 1) * D][0:tw, :]
                        inst = e.tensor_add(x1v, xv, odb[0:tw, :])
                        if call == 1:
                            e.bn_stats(bns2[0:tw, 0:6], x1v[:, 0:512])
                            e.bn_stats(bns2[0:tw, 6:12], x1v[:, 512:1024])
                            e.drain()
                            inst = e.bn_aggr(agg2[0:tw, :], bns2[0:tw, :])
                        return inst
                    x1s[k] = S.step("vector", _x1,
                                    deps=[od[k], xld[k], x1dma.get(k - 2),
                                          ln2g.get(k - 1)],
                                    small=(call == 1))

                    if call == 1:
                        sq2 = S.step("scalar", lambda e: e.activation(
                            rstd2[:, :], agg2[:, 1:2], AF.Sqrt, bias=epsc[:, :]),
                            deps=[x1s[k], msum.get(k - 1)], small=True)

                        def _rv2(e):
                            e.reciprocal(rstd2[:, :], rstd2[:, :])
                            e.drain()
                            e.tensor_scalar(nrs2[:, :], rstd2[:, :],
                                            -1.0, None, ALU.mult)
                            return e.drain()
                        rv2 = S.step("vector", _rv2, deps=[sq2])

                        def _l2(e, k=k, tw=tw):
                            e.activation(nmr2[:, :], agg2[:, 0:1], AF.Identity,
                                         scale=nrs2[:, :])
                            e.drain()
                            return e.activation(
                                ln2b[0:tw, :],
                                x1b[:, (k % 2) * D:(k % 2 + 1) * D][0:tw, :],
                                AF.Identity, scale=rstd2[0:tw, :],
                                bias=nmr2[0:tw, :])
                        ln2g[k] = S.step("scalar", _l2, deps=[rv2])

                        def _msa(e, k=k, tw=tw):
                            return e.matmul(
                                psX[0:1, 0:512], maskc[0:tw, k:k + 1],
                                ln2b[0:tw, 0:512], start=True, stop=True)
                        msa = S.step("tensor", _msa,
                                     deps=[ln2g[k], vacc.get(k - 1),
                                           f2e[len(bts) - 1]])

                        def _va(e, k=k):
                            if k == 0:
                                return e.tensor_copy(sacc[0:1, 0:512],
                                                     psX[0:1, 0:512])
                            return e.tensor_add(sacc[0:1, 0:512],
                                                sacc[0:1, 0:512],
                                                psX[0:1, 0:512])
                        va = S.step("vector", _va, deps=[msa])

                        def _msb(e, k=k, tw=tw):
                            return e.matmul(
                                psX[0:1, 0:512], maskc[0:tw, k:k + 1],
                                ln2b[0:tw, 512:1024], start=True, stop=True)
                        msb = S.step("tensor", _msb, deps=[va])

                        def _vb(e, k=k):
                            if k == 0:
                                return e.tensor_copy(sacc[0:1, 512:1024],
                                                     psX[0:1, 0:512])
                            return e.tensor_add(sacc[0:1, 512:1024],
                                                sacc[0:1, 512:1024],
                                                psX[0:1, 0:512])
                        vacc[k] = S.step("vector", _vb, deps=[msb])
                        msum[k] = vacc[k]

                    def _dma(e, k=k, sb0=sb0, tw=tw):
                        if call == 1:
                            return e.dma_start(
                                out=x1hbm[sb0:sb0 + tw, :],
                                in_=x1b[:, (k % 2) * D:(k % 2 + 1) * D]
                                [0:tw, :])
                        return e.dma_start(
                            out=outd[sb0 - 2:sb0 - 2 + tw, :],
                            in_=x1b[:, (k % 2) * D:(k % 2 + 1) * D][0:tw, :])
                    x1dma[k] = S.step("gpsimd", _dma, deps=[x1s[k]], dma=True)
                S.flush()
                st["x1w"] = x1dma[nbt - 1]
                st["outs"] = [x1dma[k] for k in range(nbt)]
                if call == 1:
                    st["pss_done"] = msum[nbt - 1]
                    st["psx_free"] = msum[nbt - 1]

        def attn_phase():
            with (
                nc.sbuf_tensor("wch", [128, 2 * D], F32) as wch,
                nc.sbuf_tensor("vsrow", [1, D], F32) as vsrow,
                nc.sbuf_tensor("avrow", [1, D], F32) as avrow,
                nc.sbuf_tensor("sTrow", [1, D], F32) as sTrow,
            ):
                d1 = S.step("gpsimd", lambda e: e.dma_start(
                    out=ccin[:, :], in_=sacc[0:1, :]),
                    deps=[st["pss_done"]], dma=True)
                if saccdbg is not None:
                    S.step("gpsimd", lambda e: e.dma_start(
                        out=saccdbg, in_=sacc[0:1, :]),
                        deps=[st["pss_done"]], dma=True)
                if use_cc:
                    cc = S.step("gpsimd", lambda e: e.collective_compute(
                        "AllReduce", ALU.add, replica_groups=groups,
                        ins=[ccin.ap().opt()], outs=[ccout.ap().opt()]),
                        deps=[d1], async_c=True)
                    d2 = S.step("gpsimd", lambda e: e.dma_start(
                        out=sTrow[0:1, :], in_=ccout[:, :]), deps=[cc],
                        dma=True)
                else:
                    d2 = S.step("gpsimd", lambda e: e.dma_start(
                        out=sTrow[0:1, :], in_=ccin[:, :]), deps=[d1],
                        dma=True)
                stt = {}
                for i in range(NDC):
                    stt[i] = S.step("tensor", lambda e, i=i: e.transpose(
                        psX[0:128, i:i + 1],
                        sTrow[0:1, i * 128:(i + 1) * 128], idf[0:1, 0:1]),
                        deps=[d2, st["psx_free"]])
                ste = S.step("vector", lambda e: e.tensor_copy(
                    sTc[:, :], psX[0:128, 0:NDC]), deps=[stt[NDC - 1]],
                    small=True)
                if stdbg is not None:
                    S.step("scalar", lambda e: e.dma_start(
                        out=stdbg, in_=sTrow[0:1, :]), deps=[d2], dma=True)
                st["psx_free"] = ste

                wld, vmm = {}, {}
                for i in range(NDC):
                    wld[i] = S.step("sync", lambda e, i=i: e.dma_start(
                        out=wch[:, (i % 2) * D:(i % 2 + 1) * D],
                        in_=wvTd[i * 128:(i + 1) * 128, :]),
                        deps=[d2, vmm.get(i - 2), st["x1w"]], dma=True)

                    def _v(e, i=i):
                        e.matmul(psA[0:1, 0:512], sTc[:, i:i + 1],
                                 wch[:, (i % 2) * D:(i % 2) * D + 512],
                                 start=(i == 0), stop=(i == NDC - 1))
                        return e.matmul(
                            psB[0:1, 0:512], sTc[:, i:i + 1],
                            wch[:, (i % 2) * D + 512:(i % 2 + 1) * D],
                            start=(i == 0), stop=(i == NDC - 1))
                    vmm[i] = S.step("tensor", _v, deps=[wld[i], ste])

                def _vse(e):
                    e.tensor_copy(vsrow[0:1, 0:512], psA[0:1, 0:512])
                    e.tensor_copy(vsrow[0:1, 512:1024],
                                  psB[0:1, 0:512])
                    return e.drain()
                vse = S.step("vector", _vse, deps=[vmm[NDC - 1]])
                if vsdbg is not None:
                    S.step("scalar", lambda e: e.dma_start(
                        out=vsdbg, in_=vsrow[0:1, :]), deps=[vse], dma=True)

                vst = {}
                for i in range(NDC):
                    vst[i] = S.step("tensor", lambda e, i=i: e.transpose(
                        psX[0:128, i:i + 1],
                        vsrow[0:1, i * 128:(i + 1) * 128], idf[0:1, 0:1]),
                        deps=[vse, st["psx_free"]])
                vste = S.step("vector", lambda e: e.tensor_copy(
                    vsT[:, :], psX[0:128, 0:NDC]), deps=[vst[NDC - 1]],
                    small=True)
                st["psx_free"] = vste

                wld2, amm = {}, {}
                for i in range(NDC):
                    wld2[i] = S.step("sync", lambda e, i=i: e.dma_start(
                        out=wch[:, (i % 2) * D:(i % 2 + 1) * D],
                        in_=woTd[i * 128:(i + 1) * 128, :]),
                        deps=[vmm[NDC - 1], amm.get(i - 2)], dma=True)

                    def _a(e, i=i):
                        e.matmul(psA[0:1, 0:512], vsT[:, i:i + 1],
                                 wch[:, (i % 2) * D:(i % 2) * D + 512],
                                 start=(i == 0), stop=(i == NDC - 1))
                        return e.matmul(
                            psB[0:1, 0:512], vsT[:, i:i + 1],
                            wch[:, (i % 2) * D + 512:(i % 2 + 1) * D],
                            start=(i == 0), stop=(i == NDC - 1))
                    amm[i] = S.step("tensor", _a, deps=[wld2[i], vste, vse])

                def _ave(e):
                    e.tensor_copy(avrow[0:1, 0:512], psA[0:1, 0:512])
                    return e.tensor_copy(avrow[0:1, 512:1024],
                                         psB[0:1, 0:512])
                ave = S.step("vector", _ave, deps=[amm[NDC - 1]], drain=True)

                def _avbc(e):
                    e.matmul(psM0[:, 0:512], onec[:, :], avrow[0:1, 0:512],
                             start=True, stop=True)
                    return e.matmul(psM0[:, 512:1024], onec[:, :],
                                    avrow[0:1, 512:1024],
                                    start=True, stop=True)
                avbc = S.step("tensor", _avbc, deps=[ave])
                avt = S.step("vector", lambda e: e.tensor_copy(
                    avtile[:, :], psM0[:, :]), deps=[avbc])
                if avdbg is not None:
                    S.step("gpsimd", lambda e: e.dma_start(
                        out=avdbg, in_=avtile[:, :]), deps=[avt], dma=True)
                S.flush()
                st["av_done"] = avt

        with nc.sbuf_tensor("n1Ta", [128, NDC * slab], BF16) as n1Ta:
            s1_phase(1, n1Ta)
            if last_call <= 0:
                S.step("gpsimd", lambda e: e.dma_start(
                    out=n1dbg, in_=n1Ta[:, :]), deps=[st["n1T_done"]], dma=True)
                S.step("scalar", lambda e: e.dma_start(
                    out=f1dbg, in_=f1row[:, :]),
                    deps=[st["f1row_done"]], dma=True)
                fin0 = S.step("gpsimd", lambda e: e.dma_start(
                    out=t1dbg, in_=t1T[:, :]), deps=[t1_done[-1]], dma=True)
                sdmp = S.step("scalar", lambda e: e.dma_start(
                    out=s1dbg, in_=dstage[:, :]), deps=[st["n1T_done"]],
                    dma=True)
                S.step("gpsimd", lambda e: e.engine_nop(),
                       deps=[fin0, sdmp])
                S.flush()
            if last_call > 0 and t1dbg is not None:
                def _css(e):
                    e.tensor_copy(csstage[:, 0:1], cstar[:, :])
                    e.tensor_copy(csstage[:, 1:2], ncstar[:, :])
                    e.tensor_copy(csstage[:, 2:3], epsc[:, :])
                    return e.tensor_copy(csstage[:, 3:4], m1d[:, :])
                _c1 = S.step("vector", _css, deps=[t1_done[-1]])
                S.step("scalar", lambda e: e.dma_start(out=csdbg,
                                                       in_=csstage[:, :]),
                       deps=[_c1], dma=True)
                S.step("gpsimd", lambda e: e.dma_start(out=t1dbg, in_=t1T[:, :]),
                       deps=[t1_done[-1]], dma=True)
                S.step("gpsimd", lambda e: e.dma_start(out=n1dbg, in_=n1Ta[:, :]),
                       deps=[st["n1T_done"]], dma=True)
                S.step("gpsimd", lambda e: e.dma_start(out=f1dbg, in_=f1row[:, :]),
                       deps=[st["f1row_done"]], dma=True)
            a_phase(1, n1Ta)
        b_phase(1)
        if last_call <= 0:
            return nc, P
        if last_call == 2:
            attn_phase()
            with nc.sbuf_tensor("n1Tb", [128, NDC * slab], BF16) as n1Tb:
                s1_phase(2, n1Tb)
                a_phase(2, n1Tb)
            b_phase(2)

        S.step("gpsimd", lambda e: e.engine_nop(),
               deps=st["outs"] + [st["x1w"]])
        S.flush()

    return nc, P


WIN = 256
HW = WIN + 2
HWS = 272


def _sil_mk(j, gl, Gwin, Cb, wbs):
    def _f(e):
        return e.activation(
            Gwin[:, j * WIN:j * WIN + gl],
            Cb[:, (j % 2) * WIN:(j % 2) * WIN + gl],
            AF.Silu, bias=wbs[:, j:j + 1])
    return _f


def plan2(T):
    assert T % WIN == 0
    slab = ((T + 4) + 127) // 128 * 128
    tiles = slab // 128
    nw = T // WIN
    wins = []
    for w in range(nw):
        wins.append({"g0": 1 + WIN * w, "gl": WIN, "h0": WIN * w, "hw": HW,
                     "bts": [2 * w, 2 * w + 1], "pair": w})
    wins.append({"g0": T + 1, "gl": 2, "h0": T, "hw": 4,
                 "bts": [2 * nw], "pair": nw})
    bts = []
    s = 1
    while s < T + 3:
        tw = min(128, T + 3 - s)
        bts.append((s, tw))
        s += tw
    return {"T": T, "slab": slab, "tiles": tiles, "wins": wins, "bts": bts,
            "nw": nw}


def build_program2(T, n_cores, cores_per_batch, use_cc=True, last_call=2):
    P = plan2(T)
    tiles, slab, nw = P["tiles"], P["slab"], P["nw"]
    wins, btlist = P["wins"], P["bts"]
    nbt = len(btlist)
    nc = bass.Bass("TRN2", target_bir_lowering=False,
                   detect_race_conditions=False)

    xslab = nc.dram_tensor("xslab", [slab, D], F32, kind="ExternalInput").ap()
    w1Td = nc.dram_tensor("w1T", [D, DH], F32, kind="ExternalInput").ap()
    w2Td = nc.dram_tensor("w2T", [DH, D], F32, kind="ExternalInput").ap()
    convw = nc.dram_tensor("convw", [128, 3 * NHC], F32,
                           kind="ExternalInput").ap()
    convb = nc.dram_tensor("convb", [128, NHC], F32,
                           kind="ExternalInput").ap()
    wvTd = nc.dram_tensor("wvT", [D, D], F32, kind="ExternalInput").ap()
    woTd = nc.dram_tensor("woT", [D, D], F32, kind="ExternalInput").ap()
    onecol = nc.dram_tensor("onecol", [1, 128], F32,
                            kind="ExternalInput").ap()
    onescol = nc.dram_tensor("onescol", [128, 1], F32,
                             kind="ExternalInput").ap()
    masksd = nc.dram_tensor("masks", [128, tiles], F32,
                            kind="ExternalInput").ap()
    identf = nc.dram_tensor("identf", [128, 128], F32,
                            kind="ExternalInput").ap()
    zpad = nc.dram_tensor("zpad", [128, D], F32, kind="ExternalInput").ap()
    halom = nc.dram_tensor("halomask", [128, tiles], F32,
                           kind="ExternalInput").ap()
    outd = nc.dram_tensor("out", [T, D], F32, kind="ExternalOutput").ap()

    _dbg = dict(kind="ExternalOutput") if _os.environ.get("KDBG") else {}
    t2hbm = nc.dram_tensor("t2hbm", [DH, D], BF16)
    x1hbm = nc.dram_tensor("x1hbm", [slab, D], F32, **_dbg)
    x2hbm = nc.dram_tensor("x2hbm", [slab, D], F32, **_dbg)
    ccin = nc.dram_tensor("ccin", [1, D], F32)
    ccout = nc.dram_tensor("ccout", [1, D], F32)
    avdbg = (nc.dram_tensor("avdbg", [128, D], F32,
                            kind="ExternalOutput").ap() if _dbg else None)
    saccdbg = (nc.dram_tensor("saccdbg", [1, D], F32,
                              kind="ExternalOutput").ap() if _dbg else None)
    stdbg = (nc.dram_tensor("stdbg", [1, D], F32,
                            kind="ExternalOutput").ap() if _dbg else None)

    groups = [list(range(b * cores_per_batch, (b + 1) * cores_per_batch))
              for b in range(max(1, n_cores // cores_per_batch))]

    S = Sched(nc, scalar_auto_drain=False)
    st = {}

    from contextlib import ExitStack
    with ExitStack() as _es:
        t1T = _es.enter_context(nc.sbuf_tensor("t1T", [128, NDC * DH], BF16))
        n1Tr = _es.enter_context(
            nc.sbuf_tensor("n1Tr", [128, 3 * NDC * HWS], BF16))
        t2s = _es.enter_context(nc.sbuf_tensor("t2s", [128, 3 * D], BF16))
        f1row = _es.enter_context(nc.sbuf_tensor("f1row", [1, slab], F32))
        onec = _es.enter_context(nc.sbuf_tensor("onec", [1, 128], F32))
        onesc = _es.enter_context(nc.sbuf_tensor("onesc", [128, 1], F32))
        maskc = _es.enter_context(nc.sbuf_tensor("maskc", [128, tiles], F32))
        hmc = _es.enter_context(nc.sbuf_tensor("hmc", [128, tiles], F32))
        idf = _es.enter_context(nc.sbuf_tensor("idf", [128, 128], F32))
        wcs = _es.enter_context(nc.sbuf_tensor("wcs", [128, 3 * NHC], F32))
        wbs = _es.enter_context(nc.sbuf_tensor("wbs", [128, NHC], F32))
        cstar = _es.enter_context(nc.sbuf_tensor("cstar", [128, 1], F32))
        ncstar = _es.enter_context(nc.sbuf_tensor("ncstar", [128, 1], F32))
        epsc = _es.enter_context(nc.sbuf_tensor("epsc", [128, 1], F32))
        m1d = _es.enter_context(nc.sbuf_tensor("m1d", [128, 1], F32))
        m2d = _es.enter_context(nc.sbuf_tensor("m2d", [128, 1], F32))
        swb = _es.enter_context(nc.sbuf_tensor("swb", [128, 2], F32))
        sTc = _es.enter_context(nc.sbuf_tensor("sTc", [128, NDC], F32))
        vsT = _es.enter_context(nc.sbuf_tensor("vsT", [128, NDC], F32))
        avtile = _es.enter_context(nc.sbuf_tensor("avtile", [128, D], F32))
        sacc = avtile
        xr = _es.enter_context(nc.sbuf_tensor("xr", [128, 2 * D], F32))
        x2r = _es.enter_context(nc.sbuf_tensor("x2r", [128, D], F32))
        lnb = _es.enter_context(nc.sbuf_tensor("lnb", [128, D], F32))
        n1r = _es.enter_context(nc.sbuf_tensor("n1r", [128, 2 * D], BF16))
        bns = _es.enter_context(nc.sbuf_tensor("bns", [128, 12], F32))
        agg = _es.enter_context(nc.sbuf_tensor("agg", [128, 2], F32))
        rstd = _es.enter_context(nc.sbuf_tensor("rstd", [128, 1], F32))
        nrs = _es.enter_context(nc.sbuf_tensor("nrs", [128, 1], F32))
        nmr = _es.enter_context(nc.sbuf_tensor("nmr", [128, 1], F32))
        rmc = _es.enter_context(nc.sbuf_tensor("rmc", [128, 1], F32))
        s1c = _es.enter_context(nc.sbuf_tensor("s1c", [128, 1], F32))
        f1c = _es.enter_context(nc.sbuf_tensor("f1c", [128, 1], F32))
        psH = _es.enter_context(nc.psum_tensor("psH", [128, 1024], F32))
        psO = _es.enter_context(nc.psum_tensor("psO", [128, 2048], F32))
        psm = _es.enter_context(nc.psum_tensor("psm", [128, 512], F32))
        psX = _es.enter_context(nc.psum_tensor("psX", [128, 512], F32))

        consts = []
        for dst_sb, src_hbm in [(onec, onecol), (onesc, onescol),
                                (maskc, masksd), (idf, identf), (wcs, convw),
                                (wbs, convb), (hmc, halom)]:
            consts.append(S.step(
                "sync", lambda e, d=dst_sb, s=src_hbm:
                e.dma_start(out=d[:, :], in_=s), dma=True))

        def _cstar(e):
            e.tensor_scalar(cstar[:, :], onesc[:, :], EPS_G, 1.0,
                            ALU.add, ALU.subtract)
            e.drain()
            e.tensor_scalar(cstar[:, :], cstar[:, :], SC_G, None, ALU.mult)
            e.drain()
            e.tensor_scalar(cstar[:, :], cstar[:, :], M29, 1.0,
                            ALU.mult, ALU.add)
            e.drain()
            e.tensor_scalar(cstar[:, :], cstar[:, :], C1, None, ALU.mult)
            e.drain()
            e.tensor_scalar(ncstar[:, :], cstar[:, :], -1.0, None, ALU.mult)
            e.tensor_scalar(epsc[:, :], onesc[:, :], 1e-5, None, ALU.mult)
            return e.drain()
        st["cstar"] = S.step("vector", _cstar, deps=consts)
        st["psx"] = st["cstar"]
        st["psm"] = st["cstar"]

        def ternarize(wTd, nchunks, ccols, widx, tdst_sb, spill_hbm,
                      gate=None):
            tg = f"_t{widx}"
            done = []
            nsub = ccols // 1024
            with (
                nc.sbuf_tensor("wfa" + tg, [128, 2 * ccols], F32) as wfa,
                nc.sbuf_tensor("ub" + tg, [128, 2 * 1024], F32) as ub,
                nc.sbuf_tensor("tch" + tg, [128, 2 * ccols], BF16) as tch,
                nc.sbuf_tensor("cacc" + tg, [128, 1], F32) as cacc,
                nc.sbuf_tensor("cx" + tg, [128, 1], F32) as cx,
                nc.sbuf_tensor("mt" + tg, [1, 1], F32) as mt,
            ):
                ld1, sm = {}, {}
                for c in range(nchunks):
                    ld1[c] = S.step("sync", lambda e, c=c: e.dma_start(
                        out=wfa[:, (c % 2) * ccols:(c % 2 + 1) * ccols],
                        in_=wTd[c * 128:(c + 1) * 128, :]),
                        deps=[sm.get(c - 2), gate if c < 2 else None],
                        dma=True)

                    def _s(e, c=c):
                        e.reduce_sum(cx[:, :],
                                     wfa[:, (c % 2) * ccols:
                                         (c % 2 + 1) * ccols],
                                     AX.X, apply_absolute_value=True)
                        e.drain()
                        if c == 0:
                            return e.tensor_copy(cacc[:, :], cx[:, :])
                        return e.tensor_add(cacc[:, :], cacc[:, :], cx[:, :])
                    sm[c] = S.step("vector", _s, deps=[ld1[c]], small=True)

                cs = S.step("tensor", lambda e: e.matmul(
                    psX[0:1, 0:1], cacc[:, :], onesc[:, :],
                    start=True, stop=True),
                    deps=[sm[nchunks - 1], st["psx"]] + consts)

                def _mean(e):
                    e.tensor_scalar(mt[:, :], psX[0:1, 0:1],
                                    1.0 / (nchunks * 128 * ccols), None,
                                    ALU.mult)
                    e.drain()
                    return e.tensor_scalar(mt[:, :], mt[:, :], 1e-5, None,
                                           ALU.max)
                mean = S.step("vector", _mean, deps=[cs], small=True)

                bc = S.step("tensor", lambda e: e.matmul(
                    psX[0:128, 0:1], onec[:, :], mt[:, :],
                    start=True, stop=True), deps=[mean])

                def _mc(e):
                    e.tensor_copy(cx[:, :], psX[0:128, 0:1])
                    e.drain()
                    e.reciprocal(swb[:, widx:widx + 1], cx[:, :])
                    md = m1d if widx == 0 else m2d
                    sc = (1.0 / 127.0) if widx == 0 else (0.5 / 127.0)
                    e.tensor_scalar(md[:, :], cx[:, :], sc, None, ALU.mult)
                    return e.drain()
                mc = S.step("vector", _mc, deps=[bc, st["cstar"]])
                st["psx"] = mc
                st[f"mc{widx}"] = mc

                ld2, clip, spills = {}, {}, {}
                for c in range(nchunks):
                    lastclip = clip.get((c - 2, nsub - 1))
                    ld2[c] = S.step("sync", lambda e, c=c: e.dma_start(
                        out=wfa[:, (c % 2) * ccols:(c % 2 + 1) * ccols],
                        in_=wTd[c * 128:(c + 1) * 128, :]),
                        deps=[mc, lastclip,
                              sm.get(nchunks - 1) if c < 2 else None],
                        dma=True)
                    for u in range(nsub):
                        L = c * nsub + u
                        pL = L - 2
                        pclip = (clip.get((pL // nsub, pL % nsub))
                                 if pL >= 0 else None)

                        def _sc(e, c=c, u=u, L=L):
                            src = wfa[:, (c % 2) * ccols + u * 1024:
                                      (c % 2) * ccols + (u + 1) * 1024]
                            uu = ub[:, (L % 2) * 1024:(L % 2 + 1) * 1024]
                            e.activation(uu, src, AF.Identity,
                                         scale=swb[:, widx:widx + 1])
                            e.activation(uu, uu, AF.Identity,
                                         bias=cstar[:, :])
                            return e.activation(uu, uu, AF.Identity,
                                                bias=ncstar[:, :])
                        scs = S.step("scalar", _sc, deps=[ld2[c], mc, pclip],
                                     drain=True)

                        def _cl(e, c=c, u=u, L=L):
                            uu = ub[:, (L % 2) * 1024:(L % 2 + 1) * 1024]
                            if tdst_sb is not None:
                                dstv = tdst_sb[:, c * ccols + u * 1024:
                                               c * ccols + (u + 1) * 1024]
                            else:
                                dstv = tch[:, (c % 2) * ccols + u * 1024:
                                           (c % 2) * ccols + (u + 1) * 1024]
                            return e.tensor_scalar(dstv, uu, 1.0, -1.0,
                                                   ALU.min, ALU.max)
                        clip[(c, u)] = S.step(
                            "vector", _cl, deps=[scs, spills.get(c - 2)],
                            drain=True)

                    if spill_hbm is not None:
                        spills[c] = S.step(
                            "scalar", lambda e, c=c: e.dma_start(
                                out=spill_hbm[c * 128:(c + 1) * 128, :],
                                in_=tch[:, (c % 2) * ccols:
                                        (c % 2 + 1) * ccols]),
                            deps=[clip[(c, nsub - 1)]], dma=True)
                        done.append(spills[c])
                    else:
                        done.append(clip[(c, nsub - 1)])
            return done

        mm1last = {}
        attn_done = {"av": None}
        x1w_steps = []

        def emit_call(call):
            tg = f"_c{call}"
            src = xslab if call == 1 else x1hbm
            resid = xslab if call == 1 else x2hbm

            sld, sx2, sx2p, sstat, slng, sqsd, spres, srnd, strn = \
                {}, {}, {}, {}, {}, {}, {}, {}, {}
            sf1t, sf1e, shalo = {}, {}, {}

            def emit_s1_tile(k):
                extra = []
                if k < 2 and call == 2:
                    extra = [st["x1w"], attn_done["av"]] + st["zs"]
                sld[k] = S.step("sync", lambda e, k=k: e.dma_start(
                    out=xr[:, (k % 2) * D:(k % 2 + 1) * D],
                    in_=src[k * 128:(k + 1) * 128, :]),
                    deps=[(slng.get(k - 2) if call == 1
                           else sx2.get(k - 2))] + extra, dma=True)

                if call == 2:
                    def _x2(e, k=k):
                        e.tensor_add(x2r[:, :],
                                     xr[:, (k % 2) * D:(k % 2 + 1) * D],
                                     avtile[:, :])
                        return e.tensor_scalar(x2r[:, :], x2r[:, :],
                                               hmc[:, k:k + 1], None,
                                               ALU.mult)
                    sx2[k] = S.step("vector", _x2,
                                    deps=[sld[k], attn_done["av"],
                                          sx2p.get(k - 1), slng.get(k - 1)],
                                    drain=True)
                    sx2p[k] = S.step("scalar", lambda e, k=k: e.dma_start(
                        out=x2hbm[k * 128:(k + 1) * 128, :],
                        in_=x2r[:, :]), deps=[sx2[k]], dma=True)

                def xt(k):
                    return (x2r[:, :] if call == 2
                            else xr[:, (k % 2) * D:(k % 2 + 1) * D])

                def _st(e, k=k):
                    e.bn_stats(bns[:, 0:6], xt(k)[:, 0:512])
                    e.bn_stats(bns[:, 6:12], xt(k)[:, 512:1024])
                    e.drain()
                    return e.bn_aggr(agg[:, :], bns[:, :])
                sstat[k] = S.step("vector", _st,
                                  deps=[(sld[k] if call == 1 else sx2[k]),
                                        slng.get(k - 1)], small=True)

                sq = S.step("scalar", lambda e: e.activation(
                    rstd[:, :], agg[:, 1:2], AF.Sqrt, bias=epsc[:, :]),
                    deps=[sstat[k], spres.get(k - 1)], small=True)

                def _rv(e):
                    e.reciprocal(rstd[:, :], rstd[:, :])
                    e.drain()
                    e.tensor_scalar(nrs[:, :], rstd[:, :], -1.0,
                                    None, ALU.mult)
                    return e.drain()
                rv = S.step("vector", _rv, deps=[sq])

                def _ln(e, k=k):
                    e.activation(nmr[:, :], agg[:, 0:1], AF.Identity,
                                 scale=nrs[:, :])
                    e.drain()
                    return e.activation(lnb[:, :], xt(k), AF.Identity,
                                        scale=rstd[:, :], bias=nmr[:, :])
                slng[k] = S.step("scalar", _ln,
                                 deps=[rv, srnd.get(k - 1)], drain=True)

                def _qs(e, k=k):
                    e.reduce_max(rmc[:, :], lnb[:, :], AX.X,
                                 apply_absolute_value=True)
                    e.drain()
                    e.tensor_scalar(rmc[:, :], rmc[:, :], 1e-5, None,
                                    ALU.max)
                    e.drain()
                    e.reciprocal(s1c[:, :], rmc[:, :])
                    e.drain()
                    e.tensor_scalar(s1c[:, :], s1c[:, :], 127.0, None,
                                    ALU.mult)
                    e.tensor_tensor(f1c[:, :], rmc[:, :],
                                    m1d[:, :], ALU.mult)
                    return e.drain()
                sqsd[k] = S.step("vector", _qs,
                                 deps=[slng[k], sf1t.get(k - 1), st["mc0"]])

                spres[k] = S.step("scalar", lambda e: e.activation(
                    lnb[:, :], lnb[:, :], AF.Identity, scale=s1c[:, :]),
                    deps=[sqsd[k]], drain=True)

                def _rnd(e, k=k):
                    e.tensor_scalar(lnb[:, :], lnb[:, :], cstar[:, :],
                                    None, ALU.add)
                    return e.tensor_scalar(
                        n1r[:, (k % 2) * D:(k % 2 + 1) * D], lnb[:, :],
                        cstar[:, :], None, ALU.subtract)
                srnd[k] = S.step("vector", _rnd,
                                 deps=[spres[k], strn.get(k - 2)],
                                 drain=True)

                p = k // 2
                soff = (p % 3) * NDC * HWS
                pos = k % 2
                slot_gate = mm1last.get((call, p - 3))
                t = None
                for i in range(NDC):
                    t = S.step(
                        "scalar",
                        lambda e, k=k, i=i, soff=soff, pos=pos:
                        e.dma_start_transpose(
                            n1Tr[:, soff + i * HWS + pos * 128:
                                 soff + i * HWS + pos * 128 + 128],
                            n1r[:, (k % 2) * D + i * 128:
                                (k % 2) * D + (i + 1) * 128]),
                        deps=[srnd[k], slot_gate], dma=True)
                strn[k] = t
                if pos == 0 and p >= 1:
                    po = ((p - 1) % 3) * NDC * HWS
                    hgate = mm1last.get((call, p - 4))
                    t = None
                    for i in range(NDC):
                        t = S.step(
                            "scalar",
                            lambda e, k=k, i=i, po=po:
                            e.dma_start_transpose(
                                n1Tr[:, po + i * HWS + 256:
                                     po + i * HWS + 272],
                                n1r[0:16, (k % 2) * D + i * 128:
                                    (k % 2) * D + (i + 1) * 128]),
                            deps=[srnd[k], hgate], dma=True)
                    shalo[p - 1] = t

                sf1t[k] = S.step("tensor", lambda e: e.transpose(
                    psX[0:1, 0:128], f1c[:, :], idf[:, :]),
                    deps=[sqsd[k], st["psx"]])
                sf1e[k] = S.step("vector", lambda e, k=k: e.tensor_copy(
                    f1row[:, k * 128:(k + 1) * 128], psX[0:1, 0:128]),
                    deps=[sf1t[k]], small=True)
                st["psx"] = sf1e[k]

            with ExitStack() as wes:
                Gwin = wes.enter_context(
                    nc.sbuf_tensor("Gw" + tg, [128, NHC * WIN], F32))
                n2win = wes.enter_context(
                    nc.sbuf_tensor("n2w" + tg, [128, NHC * WIN], BF16))
                Hb = wes.enter_context(
                    nc.sbuf_tensor("Hb" + tg, [128, 2 * HW], F32))
                Cb = wes.enter_context(
                    nc.sbuf_tensor("Cb" + tg, [128, 2 * WIN], F32))
                Qb = wes.enter_context(
                    nc.sbuf_tensor("Qb" + tg, [128, 2 * WIN], F32))
                f1bs = wes.enter_context(
                    nc.sbuf_tensor("f1b" + tg, [128, HW], F32))
                s2b = wes.enter_context(
                    nc.sbuf_tensor("s2b" + tg, [128, WIN], F32))
                s2rows = wes.enter_context(
                    nc.sbuf_tensor("s2r" + tg, [1, 256], F32))
                Macc = wes.enter_context(
                    nc.sbuf_tensor("Ma" + tg, [128, WIN], F32))
                rm2c = wes.enter_context(
                    nc.sbuf_tensor("rm2" + tg, [128, 2], F32))
                s2c = wes.enter_context(
                    nc.sbuf_tensor("s2c" + tg, [128, 2], F32))
                f2c = wes.enter_context(
                    nc.sbuf_tensor("f2c" + tg, [128, nbt], F32))
                xrb = wes.enter_context(
                    nc.sbuf_tensor("xrb" + tg, [128, 2 * D], F32))
                ln2b = wes.enter_context(
                    nc.sbuf_tensor("ln2b" + tg, [128, D], F32))
                odb = wes.enter_context(
                    nc.sbuf_tensor("odb" + tg, [128, 2 * D], F32))
                bns2 = wes.enter_context(
                    nc.sbuf_tensor("bns2" + tg, [128, 12], F32))
                agg2 = wes.enter_context(
                    nc.sbuf_tensor("agg2" + tg, [128, 2], F32))
                rstd2 = wes.enter_context(
                    nc.sbuf_tensor("rstd2" + tg, [128, 1], F32))
                nrs2 = wes.enter_context(
                    nc.sbuf_tensor("nrs2" + tg, [128, 1], F32))
                nmr2 = wes.enter_context(
                    nc.sbuf_tensor("nmr2" + tg, [128, 1], F32))

                deq, a1s, a23s, sil, mxs, qml, rnds = \
                    {}, {}, {}, {}, {}, {}, {}
                mm1s, mm2s, t2ld, f1be = {}, {}, {}, {}
                xldb, x1sb, x1db, statb, ln2gb, msumb = \
                    {}, {}, {}, {}, {}, {}
                f2done, s2done, maccfree, qdrs = {}, {}, {}, {}

                def emit_consume(wp):
                    W = wins[wp]
                    gl, bks = W["gl"], W["bts"]
                    for j in range(NHC):
                        t2ld[(wp, j)] = S.step(
                            "sync", lambda e, j=j: e.dma_start(
                                out=t2s[:, (j % 3) * D:(j % 3 + 1) * D],
                                in_=t2hbm[j * 128:(j + 1) * 128, :]),
                            deps=[st["t2last"] if wp == 0 else
                                  mm2s.get((wp - 1, j)),
                                  mm2s.get((wp, j - 3))], dma=True)

                        def _m2(e, j=j, gl=gl, bks=bks):
                            last = None
                            for t in range(len(bks)):
                                tw = min(128, gl - t * 128)
                                stat = n2win[:, j * WIN + t * 128:
                                             j * WIN + t * 128 + tw]
                                for h in range(2):
                                    last = e.matmul(
                                        psO[0:tw, t * 1024 + h * 512:
                                            t * 1024 + (h + 1) * 512],
                                        stat,
                                        t2s[:, (j % 3) * D + h * 512:
                                            (j % 3) * D + (h + 1) * 512],
                                        start=(j == 0), stop=(j == NHC - 1))
                            return last
                        mm2s[(wp, j)] = S.step(
                            "tensor", _m2,
                            deps=[t2ld[(wp, j)], rnds[(wp, j)],
                                  rnds.get((wp, j + 1))]
                            + (st.get("psO_free") or [] if j == 0 else []),
                            drain=(j == NHC - 1))

                    for t, bk in enumerate(bks):
                        sb0, tw = btlist[bk]
                        xldb[bk] = S.step(
                            "sync", lambda e, bk=bk, sb0=sb0, tw=tw:
                            e.dma_start(
                                out=xrb[:, (bk % 2) * D:
                                        (bk % 2 + 1) * D][0:tw, :],
                                in_=resid[sb0:sb0 + tw, :]),
                            deps=[x1db.get(bk - 2), msumb.get(bk - 2),
                                  st["t2last"] if bk < 2 else None]
                            + ([sx2p.get(bk), sx2p.get(bk + 1)]
                               if call == 2 else []),
                            dma=True)

                        def _od(e, bk=bk, t=t, tw=tw):
                            return e.activation(
                                odb[:, t * D:(t + 1) * D][0:tw, :],
                                psO[0:tw, t * 1024:(t + 1) * 1024],
                                AF.Identity, scale=f2c[0:tw, bk:bk + 1])
                        odst = S.step(
                            "scalar", _od,
                            deps=[mm2s[(wp, NHC - 1)], f2done[wp],
                                  qdrs.get(wp)],
                            drain=True)

                        def _x1(e, bk=bk, t=t, tw=tw):
                            xv = xrb[:, (bk % 2) * D:
                                     (bk % 2 + 1) * D][0:tw, :]
                            return e.tensor_add(
                                xv, xv, odb[:, t * D:(t + 1) * D][0:tw, :])
                        x1sb[bk] = S.step(
                            "vector", _x1,
                            deps=[odst, xldb[bk]], small=(gl < 128),
                            drain=(call == 2))

                        if call == 1:
                            def _bn(e, bk=bk, tw=tw):
                                xv = xrb[:, (bk % 2) * D:
                                         (bk % 2 + 1) * D][0:tw, :]
                                e.bn_stats(bns2[0:tw, 0:6], xv[:, 0:512])
                                e.bn_stats(bns2[0:tw, 6:12],
                                           xv[:, 512:1024])
                                e.drain()
                                return e.bn_aggr(agg2[0:tw, :],
                                                 bns2[0:tw, :])
                            statb[bk] = S.step(
                                "vector", _bn,
                                deps=[x1sb[bk], ln2gb.get(bk - 1)],
                                small=True)

                            sq2 = S.step("scalar", lambda e: e.activation(
                                rstd2[:, :], agg2[:, 1:2], AF.Sqrt,
                                bias=epsc[:, :]),
                                deps=[statb[bk], msumb.get(bk - 1)],
                                small=True)

                            def _rv2(e):
                                e.reciprocal(rstd2[:, :], rstd2[:, :])
                                e.drain()
                                e.tensor_scalar(nrs2[:, :], rstd2[:, :],
                                                -1.0, None, ALU.mult)
                                return e.drain()
                            rv2 = S.step("vector", _rv2, deps=[sq2])

                            def _l2(e, bk=bk, tw=tw):
                                e.activation(nmr2[:, :], agg2[:, 0:1],
                                             AF.Identity, scale=nrs2[:, :])
                                e.drain()
                                return e.activation(
                                    ln2b[0:tw, :],
                                    xrb[:, (bk % 2) * D:
                                        (bk % 2 + 1) * D][0:tw, :],
                                    AF.Identity, scale=rstd2[0:tw, :],
                                    bias=nmr2[0:tw, :])
                            ln2gb[bk] = S.step("scalar", _l2, deps=[rv2],
                                               drain=True)

                            msa = S.step(
                                "tensor", lambda e, bk=bk, tw=tw: e.matmul(
                                    psX[0:1, 0:512], maskc[0:tw, bk:bk + 1],
                                    ln2b[0:tw, 0:512],
                                    start=True, stop=True),
                                deps=[ln2gb[bk], st["psx"]])

                            def _va(e, bk=bk):
                                if bk == 0:
                                    return e.tensor_copy(
                                        sacc[0:1, 0:512], psX[0:1, 0:512])
                                return e.tensor_add(
                                    sacc[0:1, 0:512], sacc[0:1, 0:512],
                                    psX[0:1, 0:512])
                            va = S.step("vector", _va, deps=[msa],
                                        small=True)

                            msb = S.step(
                                "tensor", lambda e, bk=bk, tw=tw: e.matmul(
                                    psX[0:1, 0:512], maskc[0:tw, bk:bk + 1],
                                    ln2b[0:tw, 512:1024],
                                    start=True, stop=True), deps=[va])

                            def _vb(e, bk=bk):
                                if bk == 0:
                                    return e.tensor_copy(
                                        sacc[0:1, 512:1024],
                                        psX[0:1, 0:512])
                                return e.tensor_add(
                                    sacc[0:1, 512:1024],
                                    sacc[0:1, 512:1024], psX[0:1, 0:512])
                            vb = S.step("vector", _vb, deps=[msb],
                                        small=True)
                            msumb[bk] = vb
                            st["psx"] = vb

                        def _dma(e, bk=bk, sb0=sb0, tw=tw):
                            xv = xrb[:, (bk % 2) * D:(bk % 2 + 1) * D]
                            if call == 1:
                                return e.dma_start(
                                    out=x1hbm[sb0:sb0 + tw, :],
                                    in_=xv[0:tw, :])
                            lo = max(sb0, 2)
                            hi = min(sb0 + tw, T + 2)
                            return e.dma_start(
                                out=outd[lo - 2:hi - 2, :],
                                in_=xv[lo - sb0:hi - sb0, :])
                        x1db[bk] = S.step(
                            "sync", _dma,
                            deps=[x1sb[bk], statb.get(bk)], dma=True)
                        x1w_steps.append(x1db[bk])
                    st["psO_free"] = [x1sb[bk] for bk in bks]

                def emit_window(w):
                    W = wins[w]
                    gl, h0, hw, p = W["gl"], W["h0"], W["hw"], W["pair"]
                    sml = gl < 128
                    soff = (p % 3) * NDC * HWS
                    ktiles = sorted(set(
                        range(h0 // 128, (h0 + hw - 1) // 128 + 1)))
                    f1deps = [sf1e[kk] for kk in ktiles if kk in sf1e]
                    trdeps = [strn.get(2 * p), strn.get(2 * p + 1),
                              shalo.get(p)]

                    f1bc = S.step("tensor", lambda e, h0=h0, hw=hw:
                                  e.matmul(psm[:, 0:hw], onec[:, :],
                                           f1row[:, h0:h0 + hw],
                                           start=True, stop=True),
                                  deps=f1deps + [st["psm"],
                                                 st["t2last"] if w == 0
                                                 else None])
                    f1be[w] = S.step(
                        "vector", lambda e, hw=hw: e.tensor_copy(
                            f1bs[:, 0:hw], psm[:, 0:hw]),
                        deps=[f1bc, deq.get((w - 1, NHC - 1))], small=sml)
                    st["psm"] = f1be[w]

                    for j in range(NHC):
                        def _mm(e, j=j, hw=hw, soff=soff):
                            last = None
                            for i in range(NDC):
                                last = e.matmul(
                                    psH[:, (j % 2) * 512:(j % 2) * 512 + hw],
                                    t1T[:, i * DH + j * 128:
                                        i * DH + (j + 1) * 128],
                                    n1Tr[:, soff + i * HWS:
                                         soff + i * HWS + hw],
                                    start=(i == 0), stop=(i == NDC - 1))
                            return last
                        mm1s[(w, j)] = S.step(
                            "tensor", _mm,
                            deps=trdeps + [
                                deq.get((w, j - 2)) if j >= 2
                                else deq.get((w - 1, NHC + j - 2)),
                                st["t1last"] if w == 0 else None])

                        deq[(w, j)] = S.step(
                            "vector", lambda e, j=j, hw=hw: e.tensor_mul(
                                Hb[:, (j % 2) * HW:(j % 2) * HW + hw],
                                psH[:, (j % 2) * 512:(j % 2) * 512 + hw],
                                f1bs[:, 0:hw]),
                            deps=[mm1s[(w, j)], f1be[w],
                                  a23s.get((w, j - 2)) if j >= 2
                                  else a23s.get((w - 1, NHC + j - 2))],
                            small=sml)

                        def _a1(e, j=j, gl=gl):
                            H0 = Hb[:, (j % 2) * HW:]
                            return e.tensor_scalar(
                                Cb[:, (j % 2) * WIN:(j % 2) * WIN + gl],
                                H0[:, 0:gl], wcs[:, 3 * j:3 * j + 1],
                                None, ALU.mult)
                        a1s[(w, j)] = S.step(
                            "vector", _a1,
                            deps=[deq[(w, j)],
                                  sil.get((w, j - 2)) if j >= 2
                                  else sil.get((w - 1, NHC + j - 2))],
                            small=sml)

                        def _a23(e, j=j, gl=gl):
                            H0 = Hb[:, (j % 2) * HW:]
                            Cj = Cb[:, (j % 2) * WIN:]
                            e.scalar_tensor_tensor(
                                Cj[:, 0:gl], H0[:, 1:1 + gl],
                                wcs[:, 3 * j + 1:3 * j + 2], Cj[:, 0:gl],
                                ALU.mult, ALU.add)
                            return e.scalar_tensor_tensor(
                                Cj[:, 0:gl], H0[:, 2:2 + gl],
                                wcs[:, 3 * j + 2:3 * j + 3], Cj[:, 0:gl],
                                ALU.mult, ALU.add)
                        a23s[(w, j)] = S.step("vector", _a23,
                                              deps=[a1s[(w, j)]], small=sml)

                        def _sil(e, j=j, gl=gl):
                            return e.activation(
                                Gwin[:, j * WIN:j * WIN + gl],
                                Cb[:, (j % 2) * WIN:(j % 2) * WIN + gl],
                                AF.Silu, bias=wbs[:, j:j + 1])
                        if j >= 1:
                            jj = j - 1
                            sil[(w, jj)] = S.step(
                                "scalar", _sil_mk(jj, gl, Gwin, Cb, wbs),
                                deps=[a23s[(w, jj)], a23s[(w, j)],
                                      qml.get((w - 1, jj))],
                                drain=sml)

                            def _mx(e, jj=jj, gl=gl, w=w):
                                G = Gwin[:, jj * WIN:jj * WIN + gl]
                                if jj == 0:
                                    return e.scalar_tensor_tensor(
                                        Macc[:, 0:gl], G, -1.0, G,
                                        ALU.mult, ALU.max)
                                e.scalar_tensor_tensor(
                                    Macc[:, 0:gl], G, -1.0, Macc[:, 0:gl],
                                    ALU.mult, ALU.max)
                                return e.scalar_tensor_tensor(
                                    Macc[:, 0:gl], G, 1.0, Macc[:, 0:gl],
                                    ALU.mult, ALU.max)
                            mxs[(w, jj)] = S.step(
                                "vector", _mx,
                                deps=[sil[(w, jj)],
                                      maccfree.get(w) if jj == 0 else None],
                                small=sml)

                        if j == 10 and 2 * w + 4 <= tiles - 1:
                            emit_s1_tile(2 * w + 4)
                        if j == 20 and 2 * w + 5 <= tiles - 1:
                            emit_s1_tile(2 * w + 5)

                    sil[(w, NHC - 1)] = S.step(
                        "scalar", _sil_mk(NHC - 1, gl, Gwin, Cb, wbs),
                        deps=[a23s[(w, NHC - 1)], qml.get((w - 1, NHC - 1))],
                        drain=True)

                    def _mxl(e, gl=gl, w=w):
                        G = Gwin[:, (NHC - 1) * WIN:(NHC - 1) * WIN + gl]
                        e.scalar_tensor_tensor(
                            Macc[:, 0:gl], G, -1.0, Macc[:, 0:gl],
                            ALU.mult, ALU.max)
                        return e.scalar_tensor_tensor(
                            Macc[:, 0:gl], G, 1.0, Macc[:, 0:gl],
                            ALU.mult, ALU.max)
                    mxs[(w, NHC - 1)] = S.step(
                        "vector", _mxl, deps=[sil[(w, NHC - 1)]], drain=True,
                        small=sml)

                    bks = W["bts"]
                    rmrs = []
                    for t in range(len(bks)):
                        tw = min(128, gl - t * 128)
                        trM = S.step(
                            "tensor", lambda e, t=t, tw=tw: e.transpose(
                                psm[0:tw, 260:388],
                                Macc[:, t * 128:t * 128 + tw], idf[:, :]),
                            deps=[mxs[(w, NHC - 1)], st["psm"]], drain=True)
                        rmr = S.step(
                            "vector", lambda e, t=t, tw=tw: e.reduce_max(
                                rm2c[0:tw, t:t + 1], psm[0:tw, 260:388],
                                AX.X), deps=[trM], small=True)
                        st["psm"] = rmr
                        rmrs.append(rmr)

                        def _sc2(e, t=t, bk=bks[t], tw=tw):
                            e.tensor_scalar(rm2c[0:tw, t:t + 1],
                                            rm2c[0:tw, t:t + 1], 1e-5,
                                            None, ALU.max)
                            e.drain()
                            e.tensor_tensor(f2c[0:tw, bk:bk + 1],
                                            rm2c[0:tw, t:t + 1],
                                            m2d[0:tw, :], ALU.mult)
                            e.reciprocal(s2c[0:tw, t:t + 1],
                                         rm2c[0:tw, t:t + 1])
                            e.drain()
                            e.tensor_scalar(s2c[0:tw, t:t + 1],
                                            s2c[0:tw, t:t + 1], 127.0,
                                            None, ALU.mult)
                            return e.drain()
                        s2done[(w, t)] = S.step("vector", _sc2, deps=[rmr])
                    f2done[w] = s2done[(w, len(bks) - 1)]
                    maccfree[w + 1] = rmrs[-1]

                    s2res = []
                    for t in range(len(bks)):
                        trS = S.step(
                            "tensor", lambda e, t=t: e.transpose(
                                psm[0:1, 260:388], s2c[:, t:t + 1],
                                idf[:, :]),
                            deps=[s2done[(w, t)], st["psm"]])
                        s2re = S.step(
                            "vector", lambda e, t=t: e.tensor_copy(
                                s2rows[0:1, t * 128:(t + 1) * 128],
                                psm[0:1, 260:388]),
                            deps=[trS], small=True)
                        st["psm"] = s2re
                        s2res.append(s2re)
                    s2bcs = []
                    for t in range(len(bks)):
                        tw = min(128, gl - t * 128)
                        bcst = S.step(
                            "tensor", lambda e, t=t, tw=tw: e.matmul(
                                psX[0:128, t * 128:t * 128 + tw],
                                onec[:, :],
                                s2rows[0:1, t * 128:t * 128 + tw],
                                start=True, stop=True),
                            deps=[s2res[t], st["psx"]])
                        cpst = S.step(
                            "vector", lambda e, t=t, tw=tw: e.tensor_copy(
                                s2b[:, t * 128:t * 128 + tw],
                                psX[:, t * 128:t * 128 + tw]),
                            deps=[bcst], small=sml)
                        s2bcs.append(cpst)
                        st["psx"] = cpst

                    st["s2bcs"] = s2bcs
                    mm1last[(call, w)] = mm1s[(w, NHC - 1)]

                def emit_window_q(w):
                    W = wins[w]
                    gl = W["gl"]
                    sml = gl < 128
                    s2bcs = st["s2bcs"]
                    def _qm_mk(j, gl):
                        def _f(e):
                            return e.tensor_mul(
                                Qb[:, (j % 2) * WIN:(j % 2) * WIN + gl],
                                Gwin[:, j * WIN:j * WIN + gl],
                                s2b[:, 0:gl])
                        return _f

                    def _rn_mk(j, gl):
                        def _f(e):
                            Qj = Qb[:, (j % 2) * WIN:(j % 2) * WIN + gl]
                            e.activation(Qj, Qj, AF.Identity,
                                         bias=cstar[:, :])
                            return e.activation(
                                n2win[:, j * WIN:j * WIN + gl], Qj,
                                AF.Identity, bias=ncstar[:, :])
                        return _f

                    for j in range(NHC):
                        qml[(w, j)] = S.step(
                            "vector", _qm_mk(j, gl),
                            deps=s2bcs + [sil[(w, NHC - 1)],
                                          rnds.get((w, j - 2)) if j >= 2
                                          else rnds.get((w - 1,
                                                         NHC + j - 2))],
                            small=sml)
                        if j >= 1:
                            rnds[(w, j - 1)] = S.step(
                                "scalar", _rn_mk(j - 1, gl),
                                deps=[qml[(w, j - 1)], qml[(w, j)],
                                      mm2s.get((w - 1, j - 1))],
                                drain=sml)
                    qdr = S.step("vector", lambda e: e.drain(),
                                 deps=[qml[(w, NHC - 1)]])
                    qdrs[w] = qdr
                    rnds[(w, NHC - 1)] = S.step(
                        "scalar", _rn_mk(NHC - 1, gl),
                        deps=[qml[(w, NHC - 1)], qdr,
                              mm2s.get((w - 1, NHC - 1))],
                        drain=True)

                for k in range(min(4, tiles)):
                    emit_s1_tile(k)
                for w in range(nw + 1):
                    emit_window(w)
                    if w > 0:
                        emit_consume(w - 1)
                    emit_window_q(w)
                emit_consume(nw)

                if call == 1:
                    st["pss_done"] = msumb[nbt - 1]
                    st["x1w"] = x1w_steps[-1]
                else:
                    st["outs"] = [x1db[bk] for bk in range(nbt)]

        t1_done = ternarize(w1Td, NDC, DH, 0, t1T, None)
        t2_done = ternarize(w2Td, NHC, D, 1, None, t2hbm, gate=t1_done[-1])
        st["t1last"] = t1_done[-1]
        st["t2last"] = t2_done[-1]

        zsteps = []
        with nc.sbuf_tensor("zt", [128, D], F32) as zt:
            zld = S.step("sync", lambda e: e.dma_start(out=zt[:, :],
                                                       in_=zpad),
                         deps=[t2_done[-1]], dma=True)
            zs = [(0, 1)]
            r = T + 3
            while r < slab:
                h = min(128, slab - r)
                zs.append((r, h))
                r += h
            for (r, h) in zs:
                zsteps.append(S.step(
                    "gpsimd", lambda e, r=r, h=h: e.dma_start(
                        out=x1hbm[r:r + h, :], in_=zt[0:h, :]),
                    deps=[zld], dma=True))
        st["zs"] = zsteps

        emit_call(1)
        if last_call == 1:
            S.step("gpsimd", lambda e: e.engine_nop(), deps=[st["x1w"]])
            S.flush()
            return nc, P

        with (
            nc.sbuf_tensor("wch", [128, 2 * D], F32) as wch,
            nc.sbuf_tensor("vsrow", [1, D], F32) as vsrow,
            nc.sbuf_tensor("avrow", [1, D], F32) as avrow,
            nc.sbuf_tensor("sTrow", [1, D], F32) as sTrow,
        ):
            d1 = S.step("gpsimd", lambda e: e.dma_start(
                out=ccin[:, :], in_=sacc[0:1, 0:D]),
                deps=[st["pss_done"]], dma=True)
            sdbg_step = None
            if saccdbg is not None:
                sdbg_step = S.step("gpsimd", lambda e: e.dma_start(
                    out=saccdbg, in_=sacc[0:1, 0:D]),
                    deps=[st["pss_done"]], dma=True)
            if use_cc:
                cc = S.step("gpsimd", lambda e: e.collective_compute(
                    "AllReduce", ALU.add, replica_groups=groups,
                    ins=[ccin.ap().opt()], outs=[ccout.ap().opt()]),
                    deps=[d1], async_c=True)
                d2 = S.step("gpsimd", lambda e: e.dma_start(
                    out=sTrow[0:1, :], in_=ccout[:, :]), deps=[cc],
                    dma=True)
            else:
                d2 = S.step("gpsimd", lambda e: e.dma_start(
                    out=sTrow[0:1, :], in_=ccin[:, :]), deps=[d1], dma=True)
            stt = {}
            for i in range(NDC):
                stt[i] = S.step("tensor", lambda e, i=i: e.transpose(
                    psX[0:128, i:i + 1],
                    sTrow[0:1, i * 128:(i + 1) * 128], idf[0:1, 0:1]),
                    deps=[d2, st["psx"]])
            ste = S.step("vector", lambda e: e.tensor_copy(
                sTc[:, :], psX[0:128, 0:NDC]), deps=[stt[NDC - 1]],
                small=True)
            if stdbg is not None:
                S.step("scalar", lambda e: e.dma_start(
                    out=stdbg, in_=sTrow[0:1, :]), deps=[d2], dma=True)
            st["psx"] = ste

            wld, vmm = {}, {}
            for i in range(NDC):
                wld[i] = S.step("sync", lambda e, i=i: e.dma_start(
                    out=wch[:, (i % 2) * D:(i % 2 + 1) * D],
                    in_=wvTd[i * 128:(i + 1) * 128, :]),
                    deps=[d2, vmm.get(i - 2), st["x1w"]], dma=True)

                def _v(e, i=i):
                    e.matmul(psH[0:1, 0:512], sTc[:, i:i + 1],
                             wch[:, (i % 2) * D:(i % 2) * D + 512],
                             start=(i == 0), stop=(i == NDC - 1))
                    return e.matmul(
                        psH[0:1, 512:1024], sTc[:, i:i + 1],
                        wch[:, (i % 2) * D + 512:(i % 2 + 1) * D],
                        start=(i == 0), stop=(i == NDC - 1))
                vmm[i] = S.step("tensor", _v, deps=[wld[i], ste])

            def _vse(e):
                e.tensor_copy(vsrow[0:1, 0:512], psH[0:1, 0:512])
                e.tensor_copy(vsrow[0:1, 512:1024], psH[0:1, 512:1024])
                return e.drain()
            vse = S.step("vector", _vse, deps=[vmm[NDC - 1]])

            vst = {}
            for i in range(NDC):
                vst[i] = S.step("tensor", lambda e, i=i: e.transpose(
                    psX[0:128, i:i + 1],
                    vsrow[0:1, i * 128:(i + 1) * 128], idf[0:1, 0:1]),
                    deps=[vse, st["psx"]])
            vste = S.step("vector", lambda e: e.tensor_copy(
                vsT[:, :], psX[0:128, 0:NDC]), deps=[vst[NDC - 1]],
                small=True)
            st["psx"] = vste

            wld2, amm = {}, {}
            for i in range(NDC):
                wld2[i] = S.step("sync", lambda e, i=i: e.dma_start(
                    out=wch[:, (i % 2) * D:(i % 2 + 1) * D],
                    in_=woTd[i * 128:(i + 1) * 128, :]),
                    deps=[vmm[NDC - 1], amm.get(i - 2)], dma=True)

                def _a(e, i=i):
                    e.matmul(psH[0:1, 0:512], vsT[:, i:i + 1],
                             wch[:, (i % 2) * D:(i % 2) * D + 512],
                             start=(i == 0), stop=(i == NDC - 1))
                    return e.matmul(
                        psH[0:1, 512:1024], vsT[:, i:i + 1],
                        wch[:, (i % 2) * D + 512:(i % 2 + 1) * D],
                        start=(i == 0), stop=(i == NDC - 1))
                amm[i] = S.step("tensor", _a, deps=[wld2[i], vste, vse])

            def _ave(e):
                e.tensor_copy(avrow[0:1, 0:512], psH[0:1, 0:512])
                return e.tensor_copy(avrow[0:1, 512:1024],
                                     psH[0:1, 512:1024])
            ave = S.step("vector", _ave, deps=[amm[NDC - 1]], drain=True)

            def _avbc(e):
                e.matmul(psO[:, 0:512], onec[:, :], avrow[0:1, 0:512],
                         start=True, stop=True)
                return e.matmul(psO[:, 512:1024], onec[:, :],
                                avrow[0:1, 512:1024], start=True, stop=True)
            avbc = S.step("tensor", _avbc,
                          deps=[ave] + (st.get("psO_free") or []))
            avt = S.step("vector", lambda e: e.tensor_copy(
                avtile[:, :], psO[:, 0:1024]), deps=[avbc, sdbg_step])
            if avdbg is not None:
                S.step("gpsimd", lambda e: e.dma_start(
                    out=avdbg, in_=avtile[:, :]), deps=[avt], dma=True)
            attn_done["av"] = avt
            st["psO_free"] = [avt]

        emit_call(2)

        S.step("gpsimd", lambda e: e.engine_nop(), deps=st["outs"])
        S.flush()

    return nc, P





def make_core_inputs(c, xflat, Sq, shared, P, cores_per_batch):
    T, slab, tiles = P["T"], P["slab"], P["tiles"]
    batch = c // cores_per_batch
    tok0 = (c % cores_per_batch) * T
    xs = np.zeros((slab, D), dtype=np.float32)
    lo = max(tok0 - 2, 0)
    hi = min(tok0 + T + 2, Sq)
    xs[(lo - tok0) + 2:(hi - tok0) + 2] = \
        xflat[batch * Sq + lo: batch * Sq + hi]
    masks = np.zeros((128, tiles), dtype=np.float32)
    for k, (sb0, tw) in enumerate(P.get("bts") or P["bt1"]):
        for p in range(tw):
            if 2 <= sb0 + p < T + 2:
                masks[p, k] = 1.0
    hm = np.ones((128, tiles), dtype=np.float32)
    if tok0 == 0:
        hm[1, 0] = 0.0
    if tok0 + T == Sq:
        hm[(T + 2) % 128, (T + 2) // 128] = 0.0
    d = dict(shared)
    d["xslab"] = xs
    d["masks"] = masks
    d["halomask"] = hm
    return d


def kernel(L0_attn_x, L0_attn_w_qkv, L0_attn_w_o, L0_ffn_w1, L0_ffn_conv_w,
           L0_ffn_conv_b, L0_ffn_w2):
    x = np.ascontiguousarray(np.asarray(L0_attn_x, dtype=np.float32))
    B, Sq, _ = x.shape
    n_cores = 8
    cores_per_batch = n_cores // B
    T = Sq // cores_per_batch
    xflat = x.reshape(B * Sq, D)

    w1T = np.ascontiguousarray(np.asarray(L0_ffn_w1, np.float32).T)
    w2T = np.ascontiguousarray(np.asarray(L0_ffn_w2, np.float32).T)
    cwr = np.asarray(L0_ffn_conv_w, np.float32).reshape(DH, 3)
    cw = np.zeros((128, 3 * NHC), np.float32)
    cb = np.zeros((128, NHC), np.float32)
    cbin = np.asarray(L0_ffn_conv_b, np.float32)
    for j in range(NHC):
        cw[:, 3 * j:3 * j + 3] = cwr[j * 128:(j + 1) * 128, :]
        cb[:, j] = cbin[j * 128:(j + 1) * 128]
    wqkv = np.asarray(L0_attn_w_qkv, np.float32)
    shared = {
        "w1T": w1T, "w2T": w2T, "convw": cw, "convb": cb,
        "wvT": np.ascontiguousarray(wqkv[2 * D:3 * D, :].T),
        "woT": np.ascontiguousarray(np.asarray(L0_attn_w_o, np.float32).T),
        "onecol": np.ones((1, 128), np.float32),
        "onescol": np.ones((128, 1), np.float32),
        "identf": np.eye(128, dtype=np.float32),
        "zpad": np.zeros((128, D), np.float32),
    }

    nc, P = build_program2(T, n_cores, cores_per_batch, use_cc=True)
    in_maps = [make_core_inputs(c, xflat, Sq, shared, P, cores_per_batch)
               for c in range(n_cores)]
    res = run_bass_kernel_spmd(nc, in_maps, core_ids=list(range(n_cores)))
    out = np.concatenate([res.results[c]["out"] for c in range(n_cores)],
                         axis=0)
    return out.reshape(B, Sq, D)


# version 09
# speedup vs baseline: 1.0651x; 1.0651x over previous
import os as _os
import numpy as np
import ml_dtypes
import concourse.bass as bass
import concourse.mybir as mybir
from concourse.bass_utils import run_bass_kernel_spmd

F32 = mybir.dt.float32
BF16 = mybir.dt.bfloat16
ALU = mybir.AluOpType
AF = mybir.ActivationFunctionType
AX = mybir.AxisListType

C1 = float(3.0 * 2.0 ** 22)
EPS_G = float(2.0 ** -26)
SC_G = float(2.0 ** 26)
M29 = float(2.0 ** 29 - 1)
WMAX = 354

D = 1024
DH = 4096
NDC = D // 128
NHC = DH // 128

ENGINES = ("sync", "vector", "scalar", "tensor", "gpsimd")


def plan(T):
    slab = ((T + 4) + 127) // 128 * 128
    tiles = slab // 128

    def windows(h0, h1):
        res = []
        a = h0
        while True:
            w = min(WMAX, h1 - a)
            res.append((a, w))
            if a + w >= h1:
                break
            a = a + w - 2
        return res

    def btiles(s0, s1):
        res = []
        s = s0
        while s < s1:
            tw = min(128, s1 - s)
            res.append((s, tw))
            s += tw
        return res

    return {
        "T": T, "slab": slab, "tiles": tiles,
        "win1": windows(0, T + 4), "win2": windows(1, T + 3),
        "bt1": btiles(1, T + 3), "bt2": btiles(2, T + 2),
        "gcols": T + 2,
    }


class Step:
    __slots__ = ("eng", "kind", "fn", "deps", "inc", "seq", "small")

    def __init__(self, eng, kind, fn, deps, inc):
        self.eng, self.kind, self.fn, self.deps, self.inc = \
            eng, kind, fn, deps, inc
        self.seq = None


class Sched:
    def __init__(self, nc):
        self.nc = nc
        self.sems = {}
        for e in ENGINES:
            self.sems[(e, "c")] = nc.ctx.enter_context(nc.semaphore(f"{e}_c"))
            self.sems[(e, "q")] = nc.ctx.enter_context(nc.semaphore(f"{e}_q"))
        self.sems[("gpsimd", "x")] = nc.ctx.enter_context(nc.semaphore("gp_x"))
        self.count = {k: 0 for k in self.sems}
        self.water = {e: {k: 0 for k in self.sems} for e in ENGINES}
        self.steps = []

    def step(self, eng, fn, deps=(), dma=False, small=False, async_c=False):
        kind = "x" if async_c else ("q" if dma else "c")
        st = Step(eng, kind, fn,
                  [d for d in deps if d is not None], 16 if dma else 1)
        st.small = small and not dma
        self.steps.append(st)
        return st

    def flush(self):
        steps, self.steps = self.steps, []
        if not steps:
            return
        for st in steps:
            key = (st.eng, st.kind)
            self.count[key] += st.inc
            st.seq = self.count[key]
        with self.nc.Block() as blk:
            for e in ENGINES:
                mine = [s for s in steps if s.eng == e]
                if not mine:
                    continue

                def body(engine, mine=mine, e=e):
                    for st in mine:
                        for d in st.deps:
                            dk = (d.eng, d.kind)
                            if d.eng == e and d.kind == "c":
                                continue
                            if d.seq > self.water[e][dk]:
                                engine.wait_ge(self.sems[dk], d.seq)
                                self.water[e][dk] = d.seq
                        inst = st.fn(engine)
                        if st.small or (st.kind == "c" and e == "scalar"):
                            inst = engine.drain()
                        inst.then_inc(self.sems[(st.eng, st.kind)], st.inc)

                getattr(blk, e)(body)


def build_program(T, n_cores, cores_per_batch, use_cc=True, last_call=2):
    P = plan(T)
    tiles, slab, gcols = P["tiles"], P["slab"], P["gcols"]
    nc = bass.Bass("TRN2", target_bir_lowering=False,
                   detect_race_conditions=False)

    xslab = nc.dram_tensor("xslab", [slab, D], F32, kind="ExternalInput").ap()
    w1Td = nc.dram_tensor("w1T", [D, DH], F32, kind="ExternalInput").ap()
    w2Td = nc.dram_tensor("w2T", [DH, D], F32, kind="ExternalInput").ap()
    convw = nc.dram_tensor("convw", [128, 3 * NHC], F32, kind="ExternalInput").ap()
    convb = nc.dram_tensor("convb", [128, NHC], F32, kind="ExternalInput").ap()
    wvTd = nc.dram_tensor("wvT", [D, D], F32, kind="ExternalInput").ap()
    woTd = nc.dram_tensor("woT", [D, D], F32, kind="ExternalInput").ap()
    onecol = nc.dram_tensor("onecol", [1, 128], F32, kind="ExternalInput").ap()
    onescol = nc.dram_tensor("onescol", [128, 1], F32, kind="ExternalInput").ap()
    masksd = nc.dram_tensor("masks", [128, tiles], F32, kind="ExternalInput").ap()
    identf = nc.dram_tensor("identf", [128, 128], F32, kind="ExternalInput").ap()
    zpad = nc.dram_tensor("zpad", [128, D], F32, kind="ExternalInput").ap()
    halom = nc.dram_tensor("halomask", [128, tiles], F32, kind="ExternalInput").ap()
    mdin = (nc.dram_tensor("mdin", [128, 4], F32, kind="ExternalInput").ap()
            if last_call == -1 else None)
    outd = nc.dram_tensor("out", [T, D], F32, kind="ExternalOutput").ap()

    _dbg = dict(kind="ExternalOutput") if _os.environ.get("KDBG") else {}
    t2hbm = nc.dram_tensor("t2hbm", [DH, D], BF16)
    n2hbm = nc.dram_tensor("n2hbm", [128, NHC, gcols], BF16, **_dbg)
    x1hbm = nc.dram_tensor("x1hbm", [slab, D], F32, **_dbg)
    x2hbm = nc.dram_tensor("x2hbm", [slab, D], F32, **_dbg)
    ccin = nc.dram_tensor("ccin", [1, D], F32)
    ccout = nc.dram_tensor("ccout", [1, D], F32)
    avdbg = (nc.dram_tensor("avdbg", [128, D], F32, kind="ExternalOutput").ap()
             if _dbg else None)
    t1dbg = (nc.dram_tensor("t1dbg", [128, NDC * DH], BF16, kind="ExternalOutput").ap()
             if _dbg else None)
    n1dbg = (nc.dram_tensor("n1dbg", [128, NDC * slab], BF16, kind="ExternalOutput").ap()
             if _dbg else None)
    f1dbg = (nc.dram_tensor("f1dbg", [1, slab], F32, kind="ExternalOutput").ap()
             if _dbg else None)
    csdbg = (nc.dram_tensor("csdbg", [128, 4], F32, kind="ExternalOutput").ap()
             if _dbg else None)
    saccdbg = (nc.dram_tensor("saccdbg", [1, D], F32, kind="ExternalOutput").ap()
               if _dbg else None)
    stdbg = (nc.dram_tensor("stdbg", [1, D], F32, kind="ExternalOutput").ap()
             if _dbg else None)
    vsdbg = (nc.dram_tensor("vsdbg", [1, D], F32, kind="ExternalOutput").ap()
             if _dbg else None)
    s1dbg = (nc.dram_tensor("s1dbg", [128, D + 16], F32, kind="ExternalOutput").ap()
             if _dbg else None)

    groups = [list(range(b * cores_per_batch, (b + 1) * cores_per_batch))
              for b in range(max(1, n_cores // cores_per_batch))]

    S = Sched(nc)
    st = {}

    from contextlib import ExitStack
    with ExitStack() as _es:
        t1T = _es.enter_context(nc.sbuf_tensor("t1T", [128, NDC * DH], BF16))
        onec = _es.enter_context(nc.sbuf_tensor("onec", [1, 128], F32))
        onesc = _es.enter_context(nc.sbuf_tensor("onesc", [128, 1], F32))
        maskc = _es.enter_context(nc.sbuf_tensor("maskc", [128, tiles], F32))
        hmc = _es.enter_context(nc.sbuf_tensor("hmc", [128, tiles], F32))
        idf = _es.enter_context(nc.sbuf_tensor("idf", [128, 128], F32))
        wcs = _es.enter_context(nc.sbuf_tensor("wcs", [128, 3 * NHC], F32))
        wbs = _es.enter_context(nc.sbuf_tensor("wbs", [128, NHC], F32))
        cstar = _es.enter_context(nc.sbuf_tensor("cstar", [128, 1], F32))
        ncstar = _es.enter_context(nc.sbuf_tensor("ncstar", [128, 1], F32))
        epsc = _es.enter_context(nc.sbuf_tensor("epsc", [128, 1], F32))
        m1d = _es.enter_context(nc.sbuf_tensor("m1d", [128, 1], F32))
        m2d = _es.enter_context(nc.sbuf_tensor("m2d", [128, 1], F32))
        swb = _es.enter_context(nc.sbuf_tensor("swb", [128, 2], F32))
        f1row = _es.enter_context(nc.sbuf_tensor("f1row", [1, slab], F32))
        f2row = _es.enter_context(nc.sbuf_tensor("f2row", [1, slab + 128], F32))
        rm2row = _es.enter_context(nc.sbuf_tensor("rm2row", [1, WMAX], F32))
        s2row = _es.enter_context(nc.sbuf_tensor("s2row", [1, WMAX], F32))
        avtile = _es.enter_context(nc.sbuf_tensor("avtile", [128, D], F32))
        sTc = _es.enter_context(nc.sbuf_tensor("sTc", [128, NDC], F32))
        sacc = _es.enter_context(nc.sbuf_tensor("sacc", [1, D], F32))
        csstage = _es.enter_context(nc.sbuf_tensor("csstage", [128, 4], F32))
        dsmall = _es.enter_context(nc.sbuf_tensor("dsmall", [128, 4], F32))
        dstage = _es.enter_context(nc.sbuf_tensor("dstage", [128, D + 16], F32))
        qdbg = _es.enter_context(nc.sbuf_tensor("qdbg", [128, 2], F32))
        vsT = _es.enter_context(nc.sbuf_tensor("vsT", [128, NDC], F32))
        psA = _es.enter_context(nc.psum_tensor("psA", [128, 512], F32))
        psB = _es.enter_context(nc.psum_tensor("psB", [128, 512], F32))
        psX = _es.enter_context(nc.psum_tensor("psX", [128, 512], F32))
        psM0 = _es.enter_context(nc.psum_tensor("psM0", [128, 1024], F32))
        psM1 = _es.enter_context(nc.psum_tensor("psM1", [128, 1024], F32))
        consts = []
        for dst_sb, src_hbm in [(onec, onecol), (onesc, onescol),
                                (maskc, masksd), (idf, identf), (wcs, convw),
                                (wbs, convb), (hmc, halom)]:
            consts.append(S.step(
                "sync", lambda e, d=dst_sb, s=src_hbm:
                e.dma_start(out=d[:, :], in_=s), dma=True))

        def _cstar(e):
            e.tensor_scalar(cstar[:, :], onesc[:, :], EPS_G, 1.0,
                            ALU.add, ALU.subtract)
            e.drain()
            e.tensor_scalar(cstar[:, :], cstar[:, :], SC_G, None, ALU.mult)
            e.drain()
            e.tensor_scalar(cstar[:, :], cstar[:, :], M29, 1.0,
                            ALU.mult, ALU.add)
            e.drain()
            e.tensor_scalar(cstar[:, :], cstar[:, :], C1, None, ALU.mult)
            e.drain()
            e.tensor_scalar(ncstar[:, :], cstar[:, :], -1.0, None, ALU.mult)
            e.tensor_scalar(epsc[:, :], onesc[:, :], 1e-5, None, ALU.mult)
            e.drain()
            return e.memset(f2row[:, :], 0.0)
        st["cstar"] = S.step("vector", _cstar, deps=consts)
        S.flush()

        def ternarize(wTd, nchunks, ccols, widx, tdst_sb, spill_hbm, gate=None):
            tg = f"_t{widx}"
            done = []
            nsub = ccols // 1024
            with (
                nc.sbuf_tensor("wfa" + tg, [128, 2 * ccols], F32) as wfa,
                nc.sbuf_tensor("ub" + tg, [128, 2 * 1024], F32) as ub,
                nc.sbuf_tensor("tch" + tg, [128, 2 * ccols], BF16) as tch,
                nc.sbuf_tensor("cacc" + tg, [128, 1], F32) as cacc,
                nc.sbuf_tensor("cx" + tg, [128, 1], F32) as cx,
                nc.sbuf_tensor("mt" + tg, [1, 1], F32) as mt,
            ):
                ld1 = {}
                sm = {}
                for c in range(nchunks):
                    ld1[c] = S.step("sync", lambda e, c=c: e.dma_start(
                        out=wfa[:, (c % 2) * ccols:(c % 2 + 1) * ccols],
                        in_=wTd[c * 128:(c + 1) * 128, :]),
                        deps=[sm.get(c - 2), gate if c < 2 else None],
                        dma=True)

                    def _s(e, c=c):
                        e.reduce_sum(cx[:, :],
                                     wfa[:, (c % 2) * ccols:(c % 2 + 1) * ccols],
                                     AX.X, apply_absolute_value=True)
                        e.drain()
                        if c == 0:
                            return e.tensor_copy(cacc[:, :], cx[:, :])
                        return e.tensor_add(cacc[:, :], cacc[:, :], cx[:, :])
                    sm[c] = S.step("vector", _s, deps=[ld1[c]], small=True)

                cs = S.step("tensor", lambda e: e.matmul(
                    psA[0:1, 0:1], cacc[:, :], onesc[:, :],
                    start=True, stop=True), deps=[sm[nchunks - 1]] + consts)

                def _mean(e):
                    e.tensor_scalar(mt[:, :], psA[0:1, 0:1],
                                    1.0 / (nchunks * 128 * ccols), None,
                                    ALU.mult)
                    e.drain()
                    return e.tensor_scalar(mt[:, :], mt[:, :], 1e-5, None,
                                           ALU.max)
                mean = S.step("vector", _mean, deps=[cs], small=True)

                bc = S.step("tensor", lambda e: e.matmul(
                    psB[:, 0:1], onec[:, :], mt[:, :], start=True, stop=True),
                    deps=[mean])

                def _mc(e):
                    e.tensor_copy(cx[:, :], psB[:, 0:1])
                    e.drain()
                    e.reciprocal(swb[:, widx:widx + 1], cx[:, :])
                    md = m1d if widx == 0 else m2d
                    sc = (1.0 / 127.0) if widx == 0 else (0.5 / 127.0)
                    e.tensor_scalar(md[:, :], cx[:, :], sc, None, ALU.mult)
                    return e.drain()
                mc = S.step("vector", _mc, deps=[bc, st["cstar"]])

                ld2 = {}
                clip = {}
                spills = {}
                for c in range(nchunks):
                    lastclip = clip.get((c - 2, nsub - 1))
                    ld2[c] = S.step("sync", lambda e, c=c: e.dma_start(
                        out=wfa[:, (c % 2) * ccols:(c % 2 + 1) * ccols],
                        in_=wTd[c * 128:(c + 1) * 128, :]),
                        deps=[mc, lastclip,
                              sm.get(nchunks - 1) if c < 2 else None],
                        dma=True)
                    for u in range(nsub):
                        L = c * nsub + u
                        pL = L - 2
                        pclip = (clip.get((pL // nsub, pL % nsub))
                                 if pL >= 0 else None)

                        def _sc(e, c=c, u=u, L=L):
                            src = wfa[:, (c % 2) * ccols + u * 1024:
                                      (c % 2) * ccols + (u + 1) * 1024]
                            uu = ub[:, (L % 2) * 1024:(L % 2 + 1) * 1024]
                            e.activation(uu, src, AF.Identity,
                                         scale=swb[:, widx:widx + 1],
                                         bias=cstar[:, :])
                            return e.activation(uu, uu, AF.Identity,
                                                bias=ncstar[:, :])
                        scs = S.step("scalar", _sc, deps=[ld2[c], mc, pclip])

                        def _cl(e, c=c, u=u, L=L):
                            uu = ub[:, (L % 2) * 1024:(L % 2 + 1) * 1024]
                            if tdst_sb is not None:
                                dstv = tdst_sb[:, c * ccols + u * 1024:
                                               c * ccols + (u + 1) * 1024]
                            else:
                                dstv = tch[:, (c % 2) * ccols + u * 1024:
                                           (c % 2) * ccols + (u + 1) * 1024]
                            return e.tensor_scalar(dstv, uu, 1.0, -1.0,
                                                   ALU.min, ALU.max)
                        clip[(c, u)] = S.step(
                            "vector", _cl,
                            deps=[scs, spills.get(c - 2)])

                    if spill_hbm is not None:
                        spills[c] = S.step("scalar", lambda e, c=c:
                                           e.dma_start(
                                               out=spill_hbm[c * 128:
                                                             (c + 1) * 128, :],
                                               in_=tch[:, (c % 2) * ccols:
                                                       (c % 2 + 1) * ccols]),
                                           deps=[clip[(c, nsub - 1)]],
                                           dma=True)
                        done.append(spills[c])
                    else:
                        done.append(clip[(c, nsub - 1)])
                S.flush()
            return done

        if last_call == -1:
            def _md(e):
                return e.dma_start(out=dsmall[:, :], in_=mdin)
            _l = S.step("sync", _md, dma=True)
            def _md2(e):
                e.tensor_copy(m1d[:, :], dsmall[:, 0:1])
                e.tensor_copy(m2d[:, :], dsmall[:, 1:2])
                e.tensor_copy(swb[:, :], dsmall[:, 2:4])
                return e.drain()
            _c = S.step("vector", _md2, deps=[_l, st["cstar"]])
            S.flush()
            t1_done = [_c]
            t2_done = [_c]
        else:
            t1_done = ternarize(w1Td, NDC, DH, 0, t1T, None)
            t2_done = ternarize(w2Td, NHC, D, 1, None, t2hbm,
                                gate=t1_done[-1])

        zsteps = []
        with nc.sbuf_tensor("zt", [128, D], F32) as zt:
            zld = S.step("sync", lambda e: e.dma_start(out=zt[:, :], in_=zpad),
                         deps=[t2_done[-1]], dma=True)
            zs = [(0, 1)]
            r = T + 3
            while r < slab:
                h = min(128, slab - r)
                zs.append((r, h))
                r += h
            for (r, h) in zs:
                zsteps.append(S.step("gpsimd", lambda e, r=r, h=h: e.dma_start(
                    out=x1hbm[r:r + h, :], in_=zt[0:h, :]),
                    deps=[zld], dma=True))
            S.flush()

        st["psx_free"] = st["cstar"]

        def s1_phase(call, n1T):
            tg = f"_s{call}"
            src = xslab if call == 1 else x1hbm
            with (
                nc.sbuf_tensor("xr" + tg, [128, 3 * D], F32) as xr,
                nc.sbuf_tensor("x2r" + tg, [128, 2 * D], F32) as x2r,
                nc.sbuf_tensor("lnb" + tg, [128, D], F32) as lnb,
                nc.sbuf_tensor("tqb" + tg, [128, D], F32) as tqb,
                nc.sbuf_tensor("n1r" + tg, [128, 2 * D], BF16) as n1r,
                nc.sbuf_tensor("bns" + tg, [128, 12], F32) as bns,
                nc.sbuf_tensor("agg" + tg, [128, 2], F32) as agg,
                nc.sbuf_tensor("rstd" + tg, [128, 1], F32) as rstd,
                nc.sbuf_tensor("nrs" + tg, [128, 1], F32) as nrs,
                nc.sbuf_tensor("nmr" + tg, [128, 1], F32) as nmr,
                nc.sbuf_tensor("rmc" + tg, [128, 1], F32) as rmc,
                nc.sbuf_tensor("s1c" + tg, [128, 1], F32) as s1c,
                nc.sbuf_tensor("f1c" + tg, [128, 1], F32) as f1c,
            ):
                ld, x2s, x2sp, stat, lng, pres, rnd, trn, f1t, f1e, qsd = \
                    {}, {}, {}, {}, {}, {}, {}, {}, {}, {}, {}
                extra = ([st["x1w"], st["av_done"]] + zsteps
                         if call == 2 else [t2_done[-1]])
                for k in range(tiles):
                    ld[k] = S.step("sync", lambda e, k=k: e.dma_start(
                        out=xr[:, (k % 3) * D:(k % 3 + 1) * D],
                        in_=src[k * 128:(k + 1) * 128, :]),
                        deps=[(lng.get(k - 3) if call == 1 else x2s.get(k - 3))]
                        + (extra if k < 3 else []), dma=True)

                    if call == 2:
                        def _x2(e, k=k):
                            dst = x2r[:, (k % 2) * D:(k % 2 + 1) * D]
                            e.tensor_add(dst,
                                         xr[:, (k % 3) * D:(k % 3 + 1) * D],
                                         avtile[:, :])
                            return e.tensor_scalar(dst, dst, hmc[:, k:k + 1],
                                                   None, ALU.mult)
                        x2s[k] = S.step("vector", _x2,
                                        deps=[ld[k], st["av_done"],
                                              x2sp.get(k - 2), lng.get(k - 2)])
                        x2sp[k] = S.step("scalar", lambda e, k=k: e.dma_start(
                            out=x2hbm[k * 128:(k + 1) * 128, :],
                            in_=x2r[:, (k % 2) * D:(k % 2 + 1) * D]),
                            deps=[x2s[k]], dma=True)

                    def xt(k):
                        return (x2r[:, (k % 2) * D:(k % 2 + 1) * D]
                                if call == 2
                                else xr[:, (k % 3) * D:(k % 3 + 1) * D])

                    def _st(e, k=k):
                        e.bn_stats(bns[:, 0:6], xt(k)[:, 0:512])
                        e.bn_stats(bns[:, 6:12], xt(k)[:, 512:1024])
                        e.drain()
                        return e.bn_aggr(agg[:, :], bns[:, :])
                    stat[k] = S.step("vector", _st,
                                     deps=[(ld[k] if call == 1 else x2s[k]),
                                           lng.get(k - 1)], small=True)

                    sq = S.step("scalar", lambda e: e.activation(
                        rstd[:, :], agg[:, 1:2], AF.Sqrt, bias=epsc[:, :]),
                        deps=[stat[k], pres.get(k - 1)], small=True)

                    def _rv(e):
                        e.reciprocal(rstd[:, :], rstd[:, :])
                        e.drain()
                        e.tensor_scalar(nrs[:, :], rstd[:, :], -1.0,
                                        None, ALU.mult)
                        return e.drain()
                    rv = S.step("vector", _rv, deps=[sq])

                    def _ln(e, k=k):
                        e.activation(nmr[:, :], agg[:, 0:1], AF.Identity,
                                     scale=nrs[:, :])
                        e.drain()
                        return e.activation(lnb[:, :], xt(k), AF.Identity,
                                            scale=rstd[:, :], bias=nmr[:, :])
                    lng[k] = S.step("scalar", _ln, deps=[rv])

                    def _qs(e, k=k):
                        e.reduce_max(rmc[:, :], lnb[:, :], AX.X,
                                     apply_absolute_value=True)
                        e.drain()
                        e.tensor_scalar(rmc[:, :], rmc[:, :], 1e-5, None,
                                        ALU.max)
                        e.drain()
                        e.reciprocal(s1c[:, :], rmc[:, :])
                        e.drain()
                        e.tensor_scalar(s1c[:, :], s1c[:, :], 127.0, None,
                                        ALU.mult)
                        e.tensor_tensor(f1c[:, :], rmc[:, :],
                                        m1d[:, :], ALU.mult)
                        return e.drain()
                    qsd[k] = S.step("vector", _qs,
                                    deps=[lng[k], f1t.get(k - 1)])

                    pres[k] = S.step("scalar", lambda e, k=k: e.activation(
                        tqb[:, :], lnb[:, :], AF.Identity, scale=s1c[:, :]),
                        deps=[qsd[k], rnd.get(k - 1)])

                    def _rnd(e, k=k):
                        e.tensor_scalar(tqb[:, :], tqb[:, :], cstar[:, :],
                                        None, ALU.add)
                        return e.tensor_scalar(
                            n1r[:, (k % 2) * D:(k % 2 + 1) * D], tqb[:, :],
                            cstar[:, :], None, ALU.subtract)
                    rnd[k] = S.step("vector", _rnd,
                                    deps=[pres[k], trn.get(k - 2)])

                    for i in range(NDC):
                        t = S.step(
                            "scalar", lambda e, k=k, i=i:
                            e.dma_start_transpose(
                                n1T[:, i * slab + k * 128:
                                    i * slab + (k + 1) * 128],
                                n1r[:, (k % 2) * D + i * 128:
                                    (k % 2) * D + (i + 1) * 128]),
                            deps=[rnd[k]], dma=True)
                    trn[k] = t

                    f1t[k] = S.step("tensor", lambda e: e.transpose(
                        psX[0:1, 0:128], f1c[:, :], idf[:, :]),
                        deps=[qsd[k], f1e.get(k - 1), st["psx_free"]])
                    f1e[k] = S.step("vector", lambda e, k=k: e.tensor_copy(
                        f1row[:, k * 128:(k + 1) * 128], psX[0:1, 0:128]),
                        deps=[f1t[k]], small=True)
                S.flush()
                st["f1row_done"] = f1e[tiles - 1]
                st["n1T_done"] = trn[tiles - 1]
                st["psx_free"] = f1e[tiles - 1]
                if call == 2:
                    st["x2sp_last"] = x2sp[tiles - 1]

        def a_phase(call, n1T):
            tg = f"_a{call}"
            wins = P["win1"] if call == 1 else P["win2"]
            coff = 1 if call == 1 else 2
            gw = WMAX - 2
            spills = []
            with (
                nc.sbuf_tensor("Gwin" + tg, [128, NHC * gw], F32) as Gwin,
                nc.sbuf_tensor("n2w" + tg, [128, 2 * gw], BF16) as n2w,
                nc.sbuf_tensor("Hb" + tg, [128, 2 * WMAX], F32) as Hb,
                nc.sbuf_tensor("Cb" + tg, [128, 2 * WMAX], F32) as Cb,
                nc.sbuf_tensor("absb" + tg, [128, WMAX], F32) as absb,
                nc.sbuf_tensor("n2tmp" + tg, [128, WMAX], F32) as n2tmp,
                nc.sbuf_tensor("Macc" + tg, [128, WMAX], F32) as Macc,
                nc.sbuf_tensor("f1b" + tg, [128, WMAX], F32) as f1b,
                nc.sbuf_tensor("s2b" + tg, [128, WMAX], F32) as s2b,
                nc.sbuf_tensor("rm2c" + tg, [128, 1], F32) as rm2c2,
            ):
                deq, a1, a23, sab, mx, mul, rnds, spl = \
                    {}, {}, {}, {}, {}, {}, {}, {}
                prev_s2e = None
                for w, (a, wd) in enumerate(wins):
                    gl = wd - 2
                    bc = S.step("tensor", lambda e, a=a, wd=wd: e.matmul(
                        psX[:, 0:wd], onec[:, :], f1row[:, a:a + wd],
                        start=True, stop=True),
                        deps=[st["f1row_done"], st["psx_free"], prev_s2e])
                    f1be = S.step("vector", lambda e, wd=wd: e.tensor_copy(
                        f1b[:, 0:wd], psX[:, 0:wd]), deps=[bc])

                    for j in range(NHC):
                        ps = psA if j % 2 == 0 else psB
                        pdq = (deq.get((w, j - 2)) if j >= 2
                               else deq.get((w - 1, NHC + j - 2)))

                        def _mm(e, a=a, wd=wd, j=j, ps=ps):
                            last = None
                            for i in range(NDC):
                                last = e.matmul(
                                    ps[:, 0:wd],
                                    t1T[:, i * DH + j * 128:
                                        i * DH + (j + 1) * 128],
                                    n1T[:, i * slab + a:i * slab + a + wd],
                                    start=(i == 0), stop=(i == NDC - 1))
                            return last
                        mm = S.step("tensor", _mm,
                                    deps=[st["n1T_done"], t1_done[-1], pdq])

                        deq[(w, j)] = S.step(
                            "vector", lambda e, wd=wd, j=j, ps=ps:
                            e.tensor_mul(
                                Hb[:, (j % 2) * WMAX:(j % 2) * WMAX + wd],
                                ps[:, 0:wd], f1b[:, 0:wd]),
                            deps=[mm, f1be, a23.get((w, j - 2)) if j >= 2
                                  else a23.get((w - 1, NHC + j - 2))])

                        a1[(w, j)] = S.step(
                            "scalar", lambda e, gl=gl, j=j: e.activation(
                                Cb[:, (j % 2) * WMAX:(j % 2) * WMAX + gl],
                                Hb[:, (j % 2) * WMAX:(j % 2) * WMAX + gl],
                                AF.Identity, scale=wcs[:, 3 * j:3 * j + 1]),
                            deps=[deq[(w, j)],
                                  sab.get((w, j - 2)) if j >= 2
                                  else sab.get((w - 1, NHC + j - 2))])

                        def _a23(e, gl=gl, j=j):
                            H0 = Hb[:, (j % 2) * WMAX:]
                            Cj = Cb[:, (j % 2) * WMAX:]
                            e.scalar_tensor_tensor(
                                Cj[:, 0:gl], H0[:, 1:1 + gl],
                                wcs[:, 3 * j + 1:3 * j + 2], Cj[:, 0:gl],
                                ALU.mult, ALU.add)
                            return e.scalar_tensor_tensor(
                                Cj[:, 0:gl], H0[:, 2:2 + gl],
                                wcs[:, 3 * j + 2:3 * j + 3], Cj[:, 0:gl],
                                ALU.mult, ALU.add)
                        a23[(w, j)] = S.step("vector", _a23, deps=[a1[(w, j)]])

                        def _sab(e, gl=gl, j=j):
                            e.activation(Gwin[:, j * gw:j * gw + gl],
                                         Cb[:, (j % 2) * WMAX:
                                            (j % 2) * WMAX + gl],
                                         AF.Silu, bias=wbs[:, j:j + 1])
                            return e.activation(absb[:, 0:gl],
                                                Gwin[:, j * gw:j * gw + gl],
                                                AF.Abs)
                        sab[(w, j)] = S.step(
                            "scalar", _sab,
                            deps=[a23[(w, j)], mul.get((w - 1, j)),
                                  mx.get((w, j - 1))])

                        def _mx(e, gl=gl, j=j):
                            if j == 0:
                                return e.tensor_copy(Macc[:, 0:gl],
                                                     absb[:, 0:gl])
                            return e.tensor_max(Macc[:, 0:gl], Macc[:, 0:gl],
                                                absb[:, 0:gl])
                        mx[(w, j)] = S.step("vector", _mx, deps=[sab[(w, j)]])

                    cpr = None
                    for tt in range((gl + 127) // 128):
                        twc = min(128, gl - tt * 128)
                        trM = S.step(
                            "tensor", lambda e, tt=tt, twc=twc: e.transpose(
                                psX[0:twc, 384:512],
                                Macc[:, tt * 128:tt * 128 + twc], idf[:, :]),
                            deps=[mx[(w, NHC - 1)], f1be, cpr,
                                  st["psx_free"]])
                        rmr = S.step(
                            "vector", lambda e, twc=twc: e.reduce_max(
                                rm2c2[0:twc, 0:1], psX[0:twc, 384:512],
                                AX.X), deps=[trM], small=True)
                        trC = S.step(
                            "tensor", lambda e: e.transpose(
                                psX[0:1, 384:512], rm2c2[:, 0:1], idf[:, :]),
                            deps=[rmr])
                        cpr = S.step(
                            "vector", lambda e, tt=tt, twc=twc: e.tensor_copy(
                                rm2row[0:1, tt * 128:tt * 128 + twc],
                                psX[0:1, 384:384 + twc]),
                            deps=[trC], small=True)
                    rm = cpr

                    def _s2(e, a=a, gl=gl):
                        e.tensor_scalar(rm2row[:, 0:gl], rm2row[:, 0:gl],
                                        1e-5, None, ALU.max)
                        e.drain()
                        e.reciprocal(s2row[:, 0:gl], rm2row[:, 0:gl])
                        e.drain()
                        e.tensor_scalar(s2row[:, 0:gl], s2row[:, 0:gl],
                                        127.0, None, ALU.mult)
                        e.tensor_scalar(f2row[:, a + 1:a + 1 + gl],
                                        rm2row[:, 0:gl],
                                        m2d[0:1, 0:1], None, ALU.mult)
                        return e.drain()
                    s2st = S.step("vector", _s2, deps=[rm])

                    s2bc = S.step("tensor", lambda e, gl=gl: e.matmul(
                        psX[:, 0:gl], onec[:, :], s2row[:, 0:gl],
                        start=True, stop=True), deps=[s2st, f1be])
                    s2e = S.step("vector", lambda e, gl=gl: e.tensor_copy(
                        s2b[:, 0:gl], psX[:, 0:gl]), deps=[s2bc])
                    prev_s2e = s2e
                    st["psx_free"] = s2e

                    for j in range(NHC):
                        mul[(w, j)] = S.step(
                            "vector", lambda e, gl=gl, j=j: e.tensor_mul(
                                n2tmp[:, 0:gl], Gwin[:, j * gw:j * gw + gl],
                                s2b[:, 0:gl]),
                            deps=[s2e, sab[(w, j)], rnds.get((w, j - 1))])

                        def _rn(e, gl=gl, j=j):
                            e.activation(n2tmp[:, 0:gl], n2tmp[:, 0:gl],
                                         AF.Identity, bias=cstar[:, :])
                            return e.activation(
                                n2w[:, (j % 2) * gw:(j % 2) * gw + gl],
                                n2tmp[:, 0:gl], AF.Identity,
                                bias=ncstar[:, :])
                        rnds[(w, j)] = S.step(
                            "scalar", _rn,
                            deps=[mul[(w, j)], spl.get((w, j - 2)) if j >= 2
                                  else spl.get((w - 1, NHC + j - 2))])

                        spl[(w, j)] = S.step(
                            "scalar", lambda e, a=a, gl=gl, j=j: e.dma_start(
                                out=n2hbm[:, j, a + 1 - coff:
                                          a + 1 - coff + gl],
                                in_=n2w[:, (j % 2) * gw:(j % 2) * gw + gl]),
                            deps=[rnds[(w, j)]], dma=True)
                        spills.append(spl[(w, j)])
                S.flush()
            st["spills_last"] = spills[-1]
            st["f2row_done"] = spills[-1]

        def b_phase(call):
            tg = f"_b{call}"
            bts = P["bt1"] if call == 1 else P["bt2"]
            coff = 1 if call == 1 else 2
            resid = xslab if call == 1 else x2hbm
            nbt = len(bts)
            with (
                nc.sbuf_tensor("t2T" + tg, [128, NHC * D], BF16) as t2T,
                nc.sbuf_tensor("n2t" + tg, [128, 2 * NHC * 128], BF16) as n2t,
                nc.sbuf_tensor("xrb" + tg, [128, 2 * D], F32) as xrb,
                nc.sbuf_tensor("odb" + tg, [128, D], F32) as odb,
                nc.sbuf_tensor("x1b" + tg, [128, 2 * D], F32) as x1b,
                nc.sbuf_tensor("f2c" + tg, [128, len(bts)], F32) as f2c,
                nc.sbuf_tensor("ln2b" + tg, [128, D], F32) as ln2b,
                nc.sbuf_tensor("bns2" + tg, [128, 12], F32) as bns2,
                nc.sbuf_tensor("agg2" + tg, [128, 2], F32) as agg2,
                nc.sbuf_tensor("rstd2" + tg, [128, 1], F32) as rstd2,
                nc.sbuf_tensor("nrs2" + tg, [128, 1], F32) as nrs2,
                nc.sbuf_tensor("nmr2" + tg, [128, 1], F32) as nmr2,
            ):
                t2ld = []
                for c in range(NHC):
                    t2ld.append(S.step("sync", lambda e, c=c: e.dma_start(
                        out=t2T[:, c * D:(c + 1) * D],
                        in_=t2hbm[c * 128:(c + 1) * 128, :]),
                        deps=[t2_done[-1], st["spills_last"]], dma=True))

                n2ld, xld, f2t, f2e, mm2, od, x1s, ln2g, msum, x1dma = \
                    {}, {}, {}, {}, {}, {}, {}, {}, {}, {}
                vacc = {}
                for k, (sb0, tw) in enumerate(bts):
                    f2t[k] = S.step("tensor", lambda e, sb0=sb0: e.transpose(
                        psX[0:128, 0:1], f2row[:, sb0:sb0 + 128],
                        idf[0:1, 0:1]),
                        deps=[st["f2row_done"], st["psx_free"],
                              f2e.get(k - 1)])
                    f2e[k] = S.step("scalar", lambda e, k=k: e.copy(
                        f2c[:, k:k + 1], psX[0:128, 0:1]), deps=[f2t[k]],
                        small=True)
                    st["psx_free"] = f2e[k]
                for k, (sb0, tw) in enumerate(bts):
                    n2ld[k] = S.step(
                        "gpsimd", lambda e, k=k, sb0=sb0, tw=tw: e.dma_start(
                            out=n2t[:, (k % 2) * NHC * 128:
                                    (k % 2 + 1) * NHC * 128]
                            .rearrange("p (c t) -> p c t", c=NHC)[:, :, 0:tw],
                            in_=n2hbm[:, :, sb0 - coff:sb0 - coff + tw]),
                        deps=[st["spills_last"], mm2.get(k - 2)], dma=True)

                    xld[k] = S.step(
                        "sync", lambda e, k=k, sb0=sb0, tw=tw: e.dma_start(
                            out=xrb[:, (k % 2) * D:(k % 2 + 1) * D][0:tw, :],
                            in_=resid[sb0:sb0 + tw, :]),
                        deps=[x1s.get(k - 2)] +
                        (([st["x2sp_last"]] if call == 2
                          else [st["spills_last"]]) if k < 2 else []),
                        dma=True)

                    def _m2(e, k=k, tw=tw):
                        psM = psM0 if k % 2 == 0 else psM1
                        last = None
                        for j in range(NHC):
                            stt = n2t[:, (k % 2) * NHC * 128 + j * 128:
                                      (k % 2) * NHC * 128 + j * 128 + tw]
                            for h in range(2):
                                last = e.matmul(
                                    psM[0:tw, h * 512:(h + 1) * 512], stt,
                                    t2T[:, j * D + h * 512:
                                        j * D + (h + 1) * 512],
                                    start=(j == 0), stop=(j == NHC - 1))
                        return last
                    mm2[k] = S.step("tensor", _m2,
                                    deps=[n2ld[k], od.get(k - 2), t2ld[-1]])

                    od[k] = S.step("scalar", lambda e, k=k, tw=tw:
                                   e.activation(
                                       odb[0:tw, :],
                                       (psM0 if k % 2 == 0 else psM1)[0:tw, :],
                                       AF.Identity,
                                       scale=f2c[0:tw, k:k + 1]),
                                   deps=[mm2[k], f2e[len(bts) - 1],
                                         x1s.get(k - 1)])

                    def _x1(e, k=k, tw=tw):
                        xv = xrb[:, (k % 2) * D:(k % 2 + 1) * D][0:tw, :]
                        x1v = x1b[:, (k % 2) * D:(k % 2 + 1) * D][0:tw, :]
                        inst = e.tensor_add(x1v, xv, odb[0:tw, :])
                        if call == 1:
                            e.bn_stats(bns2[0:tw, 0:6], x1v[:, 0:512])
                            e.bn_stats(bns2[0:tw, 6:12], x1v[:, 512:1024])
                            e.drain()
                            inst = e.bn_aggr(agg2[0:tw, :], bns2[0:tw, :])
                        return inst
                    x1s[k] = S.step("vector", _x1,
                                    deps=[od[k], xld[k], x1dma.get(k - 2),
                                          ln2g.get(k - 1)],
                                    small=(call == 1))

                    if call == 1:
                        sq2 = S.step("scalar", lambda e: e.activation(
                            rstd2[:, :], agg2[:, 1:2], AF.Sqrt, bias=epsc[:, :]),
                            deps=[x1s[k], msum.get(k - 1)], small=True)

                        def _rv2(e):
                            e.reciprocal(rstd2[:, :], rstd2[:, :])
                            e.drain()
                            e.tensor_scalar(nrs2[:, :], rstd2[:, :],
                                            -1.0, None, ALU.mult)
                            return e.drain()
                        rv2 = S.step("vector", _rv2, deps=[sq2])

                        def _l2(e, k=k, tw=tw):
                            e.activation(nmr2[:, :], agg2[:, 0:1], AF.Identity,
                                         scale=nrs2[:, :])
                            e.drain()
                            return e.activation(
                                ln2b[0:tw, :],
                                x1b[:, (k % 2) * D:(k % 2 + 1) * D][0:tw, :],
                                AF.Identity, scale=rstd2[0:tw, :],
                                bias=nmr2[0:tw, :])
                        ln2g[k] = S.step("scalar", _l2, deps=[rv2])

                        def _msa(e, k=k, tw=tw):
                            return e.matmul(
                                psX[0:1, 0:512], maskc[0:tw, k:k + 1],
                                ln2b[0:tw, 0:512], start=True, stop=True)
                        msa = S.step("tensor", _msa,
                                     deps=[ln2g[k], vacc.get(k - 1),
                                           f2e[len(bts) - 1]])

                        def _va(e, k=k):
                            if k == 0:
                                return e.tensor_copy(sacc[0:1, 0:512],
                                                     psX[0:1, 0:512])
                            return e.tensor_add(sacc[0:1, 0:512],
                                                sacc[0:1, 0:512],
                                                psX[0:1, 0:512])
                        va = S.step("vector", _va, deps=[msa])

                        def _msb(e, k=k, tw=tw):
                            return e.matmul(
                                psX[0:1, 0:512], maskc[0:tw, k:k + 1],
                                ln2b[0:tw, 512:1024], start=True, stop=True)
                        msb = S.step("tensor", _msb, deps=[va])

                        def _vb(e, k=k):
                            if k == 0:
                                return e.tensor_copy(sacc[0:1, 512:1024],
                                                     psX[0:1, 0:512])
                            return e.tensor_add(sacc[0:1, 512:1024],
                                                sacc[0:1, 512:1024],
                                                psX[0:1, 0:512])
                        vacc[k] = S.step("vector", _vb, deps=[msb])
                        msum[k] = vacc[k]

                    def _dma(e, k=k, sb0=sb0, tw=tw):
                        if call == 1:
                            return e.dma_start(
                                out=x1hbm[sb0:sb0 + tw, :],
                                in_=x1b[:, (k % 2) * D:(k % 2 + 1) * D]
                                [0:tw, :])
                        return e.dma_start(
                            out=outd[sb0 - 2:sb0 - 2 + tw, :],
                            in_=x1b[:, (k % 2) * D:(k % 2 + 1) * D][0:tw, :])
                    x1dma[k] = S.step("gpsimd", _dma, deps=[x1s[k]], dma=True)
                S.flush()
                st["x1w"] = x1dma[nbt - 1]
                st["outs"] = [x1dma[k] for k in range(nbt)]
                if call == 1:
                    st["pss_done"] = msum[nbt - 1]
                    st["psx_free"] = msum[nbt - 1]

        def attn_phase():
            with (
                nc.sbuf_tensor("wch", [128, 2 * D], F32) as wch,
                nc.sbuf_tensor("vsrow", [1, D], F32) as vsrow,
                nc.sbuf_tensor("avrow", [1, D], F32) as avrow,
                nc.sbuf_tensor("sTrow", [1, D], F32) as sTrow,
            ):
                d1 = S.step("gpsimd", lambda e: e.dma_start(
                    out=ccin[:, :], in_=sacc[0:1, :]),
                    deps=[st["pss_done"]], dma=True)
                if saccdbg is not None:
                    S.step("gpsimd", lambda e: e.dma_start(
                        out=saccdbg, in_=sacc[0:1, :]),
                        deps=[st["pss_done"]], dma=True)
                if use_cc:
                    cc = S.step("gpsimd", lambda e: e.collective_compute(
                        "AllReduce", ALU.add, replica_groups=groups,
                        ins=[ccin.ap().opt()], outs=[ccout.ap().opt()]),
                        deps=[d1], async_c=True)
                    d2 = S.step("gpsimd", lambda e: e.dma_start(
                        out=sTrow[0:1, :], in_=ccout[:, :]), deps=[cc],
                        dma=True)
                else:
                    d2 = S.step("gpsimd", lambda e: e.dma_start(
                        out=sTrow[0:1, :], in_=ccin[:, :]), deps=[d1],
                        dma=True)
                stt = {}
                for i in range(NDC):
                    stt[i] = S.step("tensor", lambda e, i=i: e.transpose(
                        psX[0:128, i:i + 1],
                        sTrow[0:1, i * 128:(i + 1) * 128], idf[0:1, 0:1]),
                        deps=[d2, st["psx_free"]])
                ste = S.step("vector", lambda e: e.tensor_copy(
                    sTc[:, :], psX[0:128, 0:NDC]), deps=[stt[NDC - 1]],
                    small=True)
                if stdbg is not None:
                    S.step("scalar", lambda e: e.dma_start(
                        out=stdbg, in_=sTrow[0:1, :]), deps=[d2], dma=True)
                st["psx_free"] = ste

                wld, vmm = {}, {}
                for i in range(NDC):
                    wld[i] = S.step("sync", lambda e, i=i: e.dma_start(
                        out=wch[:, (i % 2) * D:(i % 2 + 1) * D],
                        in_=wvTd[i * 128:(i + 1) * 128, :]),
                        deps=[d2, vmm.get(i - 2), st["x1w"]], dma=True)

                    def _v(e, i=i):
                        e.matmul(psA[0:1, 0:512], sTc[:, i:i + 1],
                                 wch[:, (i % 2) * D:(i % 2) * D + 512],
                                 start=(i == 0), stop=(i == NDC - 1))
                        return e.matmul(
                            psB[0:1, 0:512], sTc[:, i:i + 1],
                            wch[:, (i % 2) * D + 512:(i % 2 + 1) * D],
                            start=(i == 0), stop=(i == NDC - 1))
                    vmm[i] = S.step("tensor", _v, deps=[wld[i], ste])

                def _vse(e):
                    e.tensor_copy(vsrow[0:1, 0:512], psA[0:1, 0:512])
                    e.tensor_copy(vsrow[0:1, 512:1024],
                                  psB[0:1, 0:512])
                    return e.drain()
                vse = S.step("vector", _vse, deps=[vmm[NDC - 1]])
                if vsdbg is not None:
                    S.step("scalar", lambda e: e.dma_start(
                        out=vsdbg, in_=vsrow[0:1, :]), deps=[vse], dma=True)

                vst = {}
                for i in range(NDC):
                    vst[i] = S.step("tensor", lambda e, i=i: e.transpose(
                        psX[0:128, i:i + 1],
                        vsrow[0:1, i * 128:(i + 1) * 128], idf[0:1, 0:1]),
                        deps=[vse, st["psx_free"]])
                vste = S.step("vector", lambda e: e.tensor_copy(
                    vsT[:, :], psX[0:128, 0:NDC]), deps=[vst[NDC - 1]],
                    small=True)
                st["psx_free"] = vste

                wld2, amm = {}, {}
                for i in range(NDC):
                    wld2[i] = S.step("sync", lambda e, i=i: e.dma_start(
                        out=wch[:, (i % 2) * D:(i % 2 + 1) * D],
                        in_=woTd[i * 128:(i + 1) * 128, :]),
                        deps=[vmm[NDC - 1], amm.get(i - 2)], dma=True)

                    def _a(e, i=i):
                        e.matmul(psA[0:1, 0:512], vsT[:, i:i + 1],
                                 wch[:, (i % 2) * D:(i % 2) * D + 512],
                                 start=(i == 0), stop=(i == NDC - 1))
                        return e.matmul(
                            psB[0:1, 0:512], vsT[:, i:i + 1],
                            wch[:, (i % 2) * D + 512:(i % 2 + 1) * D],
                            start=(i == 0), stop=(i == NDC - 1))
                    amm[i] = S.step("tensor", _a, deps=[wld2[i], vste, vse])

                def _ave(e):
                    e.tensor_copy(avrow[0:1, 0:512], psA[0:1, 0:512])
                    return e.tensor_copy(avrow[0:1, 512:1024],
                                         psB[0:1, 0:512])
                ave = S.step("vector", _ave, deps=[amm[NDC - 1]])

                def _avbc(e):
                    e.matmul(psM0[:, 0:512], onec[:, :], avrow[0:1, 0:512],
                             start=True, stop=True)
                    return e.matmul(psM0[:, 512:1024], onec[:, :],
                                    avrow[0:1, 512:1024],
                                    start=True, stop=True)
                avbc = S.step("tensor", _avbc, deps=[ave])
                avt = S.step("vector", lambda e: e.tensor_copy(
                    avtile[:, :], psM0[:, :]), deps=[avbc])
                if avdbg is not None:
                    S.step("gpsimd", lambda e: e.dma_start(
                        out=avdbg, in_=avtile[:, :]), deps=[avt], dma=True)
                S.flush()
                st["av_done"] = avt

        with nc.sbuf_tensor("n1Ta", [128, NDC * slab], BF16) as n1Ta:
            s1_phase(1, n1Ta)
            if last_call <= 0:
                S.step("gpsimd", lambda e: e.dma_start(
                    out=n1dbg, in_=n1Ta[:, :]), deps=[st["n1T_done"]], dma=True)
                S.step("scalar", lambda e: e.dma_start(
                    out=f1dbg, in_=f1row[:, :]),
                    deps=[st["f1row_done"]], dma=True)
                fin0 = S.step("gpsimd", lambda e: e.dma_start(
                    out=t1dbg, in_=t1T[:, :]), deps=[t1_done[-1]], dma=True)
                sdmp = S.step("scalar", lambda e: e.dma_start(
                    out=s1dbg, in_=dstage[:, :]), deps=[st["n1T_done"]],
                    dma=True)
                S.step("gpsimd", lambda e: e.engine_nop(),
                       deps=[fin0, sdmp])
                S.flush()
            if last_call > 0 and t1dbg is not None:
                def _css(e):
                    e.tensor_copy(csstage[:, 0:1], cstar[:, :])
                    e.tensor_copy(csstage[:, 1:2], ncstar[:, :])
                    e.tensor_copy(csstage[:, 2:3], epsc[:, :])
                    return e.tensor_copy(csstage[:, 3:4], m1d[:, :])
                _c1 = S.step("vector", _css, deps=[t1_done[-1]])
                S.step("scalar", lambda e: e.dma_start(out=csdbg,
                                                       in_=csstage[:, :]),
                       deps=[_c1], dma=True)
                S.step("gpsimd", lambda e: e.dma_start(out=t1dbg, in_=t1T[:, :]),
                       deps=[t1_done[-1]], dma=True)
                S.step("gpsimd", lambda e: e.dma_start(out=n1dbg, in_=n1Ta[:, :]),
                       deps=[st["n1T_done"]], dma=True)
                S.step("gpsimd", lambda e: e.dma_start(out=f1dbg, in_=f1row[:, :]),
                       deps=[st["f1row_done"]], dma=True)
            a_phase(1, n1Ta)
        b_phase(1)
        if last_call <= 0:
            return nc, P
        if last_call == 2:
            attn_phase()
            with nc.sbuf_tensor("n1Tb", [128, NDC * slab], BF16) as n1Tb:
                s1_phase(2, n1Tb)
                a_phase(2, n1Tb)
            b_phase(2)

        S.step("gpsimd", lambda e: e.engine_nop(),
               deps=st["outs"] + [st["x1w"]])
        S.flush()

    return nc, P




def make_core_inputs(c, xflat, Sq, shared, P, cores_per_batch):
    T, slab, tiles = P["T"], P["slab"], P["tiles"]
    batch = c // cores_per_batch
    tok0 = (c % cores_per_batch) * T
    xs = np.zeros((slab, D), dtype=np.float32)
    lo = max(tok0 - 2, 0)
    hi = min(tok0 + T + 2, Sq)
    xs[(lo - tok0) + 2:(hi - tok0) + 2] = \
        xflat[batch * Sq + lo: batch * Sq + hi]
    masks = np.zeros((128, tiles), dtype=np.float32)
    for k, (sb0, tw) in enumerate(P["bt1"]):
        for p in range(tw):
            if 2 <= sb0 + p < T + 2:
                masks[p, k] = 1.0
    hm = np.ones((128, tiles), dtype=np.float32)
    if tok0 == 0:
        hm[1, 0] = 0.0
    if tok0 + T == Sq:
        hm[(T + 2) % 128, (T + 2) // 128] = 0.0
    d = dict(shared)
    d["xslab"] = xs
    d["masks"] = masks
    d["halomask"] = hm
    return d


def kernel(L0_attn_x, L0_attn_w_qkv, L0_attn_w_o, L0_ffn_w1, L0_ffn_conv_w,
           L0_ffn_conv_b, L0_ffn_w2):
    x = np.ascontiguousarray(np.asarray(L0_attn_x, dtype=np.float32))
    B, Sq, _ = x.shape
    n_cores = 8
    cores_per_batch = n_cores // B
    T = Sq // cores_per_batch
    xflat = x.reshape(B * Sq, D)

    w1T = np.ascontiguousarray(np.asarray(L0_ffn_w1, np.float32).T)
    w2T = np.ascontiguousarray(np.asarray(L0_ffn_w2, np.float32).T)
    cwr = np.asarray(L0_ffn_conv_w, np.float32).reshape(DH, 3)
    cw = np.zeros((128, 3 * NHC), np.float32)
    cb = np.zeros((128, NHC), np.float32)
    cbin = np.asarray(L0_ffn_conv_b, np.float32)
    for j in range(NHC):
        cw[:, 3 * j:3 * j + 3] = cwr[j * 128:(j + 1) * 128, :]
        cb[:, j] = cbin[j * 128:(j + 1) * 128]
    wqkv = np.asarray(L0_attn_w_qkv, np.float32)
    shared = {
        "w1T": w1T, "w2T": w2T, "convw": cw, "convb": cb,
        "wvT": np.ascontiguousarray(wqkv[2 * D:3 * D, :].T),
        "woT": np.ascontiguousarray(np.asarray(L0_attn_w_o, np.float32).T),
        "onecol": np.ones((1, 128), np.float32),
        "onescol": np.ones((128, 1), np.float32),
        "identf": np.eye(128, dtype=np.float32),
        "zpad": np.zeros((128, D), np.float32),
    }

    nc, P = build_program(T, n_cores, cores_per_batch, use_cc=True)
    in_maps = [make_core_inputs(c, xflat, Sq, shared, P, cores_per_batch)
               for c in range(n_cores)]
    res = run_bass_kernel_spmd(nc, in_maps, core_ids=list(range(n_cores)))
    out = np.concatenate([res.results[c]["out"] for c in range(n_cores)],
                         axis=0)
    return out.reshape(B, Sq, D)


# version 10 (final)
# speedup vs baseline: 1.0718x; 1.0063x over previous
import os as _os
import numpy as np
import ml_dtypes
import concourse.bass as bass
import concourse.mybir as mybir
from concourse.bass_utils import run_bass_kernel_spmd

F32 = mybir.dt.float32
BF16 = mybir.dt.bfloat16
ALU = mybir.AluOpType
AF = mybir.ActivationFunctionType
AX = mybir.AxisListType

C1 = float(3.0 * 2.0 ** 22)
EPS_G = float(2.0 ** -26)
SC_G = float(2.0 ** 26)
M29 = float(2.0 ** 29 - 1)
WMAX = 354

D = 1024
DH = 4096
NDC = D // 128
NHC = DH // 128

ENGINES = ("sync", "vector", "scalar", "tensor", "gpsimd")


def plan(T):
    slab = ((T + 4) + 127) // 128 * 128
    tiles = slab // 128

    def windows(h0, h1):
        res = []
        a = h0
        while True:
            w = min(WMAX, h1 - a)
            res.append((a, w))
            if a + w >= h1:
                break
            a = a + w - 2
        return res

    def btiles(s0, s1):
        res = []
        s = s0
        while s < s1:
            tw = min(128, s1 - s)
            res.append((s, tw))
            s += tw
        return res

    return {
        "T": T, "slab": slab, "tiles": tiles,
        "win1": windows(0, T + 4), "win2": windows(1, T + 3),
        "bt1": btiles(1, T + 3), "bt2": btiles(2, T + 2),
        "gcols": T + 2,
    }


class Step:
    __slots__ = ("eng", "kind", "fn", "deps", "inc", "seq", "small")

    def __init__(self, eng, kind, fn, deps, inc):
        self.eng, self.kind, self.fn, self.deps, self.inc = \
            eng, kind, fn, deps, inc
        self.seq = None


class Sched:
    def __init__(self, nc):
        self.nc = nc
        self.sems = {}
        for e in ENGINES:
            self.sems[(e, "c")] = nc.ctx.enter_context(nc.semaphore(f"{e}_c"))
            self.sems[(e, "q")] = nc.ctx.enter_context(nc.semaphore(f"{e}_q"))
        self.sems[("gpsimd", "x")] = nc.ctx.enter_context(nc.semaphore("gp_x"))
        self.count = {k: 0 for k in self.sems}
        self.water = {e: {k: 0 for k in self.sems} for e in ENGINES}
        self.steps = []

    def step(self, eng, fn, deps=(), dma=False, small=False, async_c=False):
        kind = "x" if async_c else ("q" if dma else "c")
        st = Step(eng, kind, fn,
                  [d for d in deps if d is not None], 16 if dma else 1)
        st.small = small and not dma
        self.steps.append(st)
        return st

    def flush(self):
        steps, self.steps = self.steps, []
        if not steps:
            return
        for st in steps:
            key = (st.eng, st.kind)
            self.count[key] += st.inc
            st.seq = self.count[key]
        with self.nc.Block() as blk:
            for e in ENGINES:
                mine = [s for s in steps if s.eng == e]
                if not mine:
                    continue

                def body(engine, mine=mine, e=e):
                    for st in mine:
                        for d in st.deps:
                            dk = (d.eng, d.kind)
                            if d.eng == e and d.kind == "c":
                                continue
                            if d.seq > self.water[e][dk]:
                                engine.wait_ge(self.sems[dk], d.seq)
                                self.water[e][dk] = d.seq
                        inst = st.fn(engine)
                        if st.small or (st.kind == "c" and e == "scalar"):
                            inst = engine.drain()
                        inst.then_inc(self.sems[(st.eng, st.kind)], st.inc)

                getattr(blk, e)(body)


def build_program(T, n_cores, cores_per_batch, use_cc=True, last_call=2):
    P = plan(T)
    tiles, slab, gcols = P["tiles"], P["slab"], P["gcols"]
    nc = bass.Bass("TRN2", target_bir_lowering=False,
                   detect_race_conditions=False)

    xslab = nc.dram_tensor("xslab", [slab, D], F32, kind="ExternalInput").ap()
    w1Td = nc.dram_tensor("w1T", [D, DH], F32, kind="ExternalInput").ap()
    w2Td = nc.dram_tensor("w2T", [DH, D], F32, kind="ExternalInput").ap()
    convw = nc.dram_tensor("convw", [128, 3 * NHC], F32, kind="ExternalInput").ap()
    convb = nc.dram_tensor("convb", [128, NHC], F32, kind="ExternalInput").ap()
    wvTd = nc.dram_tensor("wvT", [D, D], F32, kind="ExternalInput").ap()
    woTd = nc.dram_tensor("woT", [D, D], F32, kind="ExternalInput").ap()
    onecol = nc.dram_tensor("onecol", [1, 128], F32, kind="ExternalInput").ap()
    onescol = nc.dram_tensor("onescol", [128, 1], F32, kind="ExternalInput").ap()
    masksd = nc.dram_tensor("masks", [128, tiles], F32, kind="ExternalInput").ap()
    identf = nc.dram_tensor("identf", [128, 128], F32, kind="ExternalInput").ap()
    zpad = nc.dram_tensor("zpad", [128, D], F32, kind="ExternalInput").ap()
    halom = nc.dram_tensor("halomask", [128, tiles], F32, kind="ExternalInput").ap()
    mdin = (nc.dram_tensor("mdin", [128, 4], F32, kind="ExternalInput").ap()
            if last_call == -1 else None)
    outd = nc.dram_tensor("out", [T, D], F32, kind="ExternalOutput").ap()

    _dbg = dict(kind="ExternalOutput") if _os.environ.get("KDBG") else {}
    t2hbm = nc.dram_tensor("t2hbm", [DH, D], BF16)
    n2hbm = nc.dram_tensor("n2hbm", [128, NHC, gcols], BF16, **_dbg)
    x1hbm = nc.dram_tensor("x1hbm", [slab, D], F32, **_dbg)
    x2hbm = nc.dram_tensor("x2hbm", [slab, D], F32, **_dbg)
    ccin = nc.dram_tensor("ccin", [1, D], F32)
    ccout = nc.dram_tensor("ccout", [1, D], F32)
    avdbg = (nc.dram_tensor("avdbg", [128, D], F32, kind="ExternalOutput").ap()
             if _dbg else None)
    t1dbg = (nc.dram_tensor("t1dbg", [128, NDC * DH], BF16, kind="ExternalOutput").ap()
             if _dbg else None)
    n1dbg = (nc.dram_tensor("n1dbg", [128, NDC * slab], BF16, kind="ExternalOutput").ap()
             if _dbg else None)
    f1dbg = (nc.dram_tensor("f1dbg", [1, slab], F32, kind="ExternalOutput").ap()
             if _dbg else None)
    csdbg = (nc.dram_tensor("csdbg", [128, 4], F32, kind="ExternalOutput").ap()
             if _dbg else None)
    saccdbg = (nc.dram_tensor("saccdbg", [1, D], F32, kind="ExternalOutput").ap()
               if _dbg else None)
    stdbg = (nc.dram_tensor("stdbg", [1, D], F32, kind="ExternalOutput").ap()
             if _dbg else None)
    vsdbg = (nc.dram_tensor("vsdbg", [1, D], F32, kind="ExternalOutput").ap()
             if _dbg else None)
    s1dbg = (nc.dram_tensor("s1dbg", [128, D + 16], F32, kind="ExternalOutput").ap()
             if _dbg else None)

    groups = [list(range(b * cores_per_batch, (b + 1) * cores_per_batch))
              for b in range(max(1, n_cores // cores_per_batch))]

    S = Sched(nc)
    st = {}

    from contextlib import ExitStack
    with ExitStack() as _es:
        t1T = _es.enter_context(nc.sbuf_tensor("t1T", [128, NDC * DH], BF16))
        onec = _es.enter_context(nc.sbuf_tensor("onec", [1, 128], F32))
        onesc = _es.enter_context(nc.sbuf_tensor("onesc", [128, 1], F32))
        maskc = _es.enter_context(nc.sbuf_tensor("maskc", [128, tiles], F32))
        hmc = _es.enter_context(nc.sbuf_tensor("hmc", [128, tiles], F32))
        idf = _es.enter_context(nc.sbuf_tensor("idf", [128, 128], F32))
        wcs = _es.enter_context(nc.sbuf_tensor("wcs", [128, 3 * NHC], F32))
        wbs = _es.enter_context(nc.sbuf_tensor("wbs", [128, NHC], F32))
        cstar = _es.enter_context(nc.sbuf_tensor("cstar", [128, 1], F32))
        ncstar = _es.enter_context(nc.sbuf_tensor("ncstar", [128, 1], F32))
        epsc = _es.enter_context(nc.sbuf_tensor("epsc", [128, 1], F32))
        m1d = _es.enter_context(nc.sbuf_tensor("m1d", [128, 1], F32))
        m2d = _es.enter_context(nc.sbuf_tensor("m2d", [128, 1], F32))
        swb = _es.enter_context(nc.sbuf_tensor("swb", [128, 2], F32))
        f1row = _es.enter_context(nc.sbuf_tensor("f1row", [1, slab], F32))
        f2row = _es.enter_context(nc.sbuf_tensor("f2row", [1, slab + 128], F32))
        rm2row = _es.enter_context(nc.sbuf_tensor("rm2row", [1, WMAX], F32))
        s2row = _es.enter_context(nc.sbuf_tensor("s2row", [1, WMAX], F32))
        avtile = _es.enter_context(nc.sbuf_tensor("avtile", [128, D], F32))
        sTc = _es.enter_context(nc.sbuf_tensor("sTc", [128, NDC], F32))
        sacc = _es.enter_context(nc.sbuf_tensor("sacc", [1, D], F32))
        csstage = _es.enter_context(nc.sbuf_tensor("csstage", [128, 4], F32))
        dsmall = _es.enter_context(nc.sbuf_tensor("dsmall", [128, 4], F32))
        dstage = _es.enter_context(nc.sbuf_tensor("dstage", [128, D + 16], F32))
        qdbg = _es.enter_context(nc.sbuf_tensor("qdbg", [128, 2], F32))
        vsT = _es.enter_context(nc.sbuf_tensor("vsT", [128, NDC], F32))
        psA = _es.enter_context(nc.psum_tensor("psA", [128, 512], F32))
        psB = _es.enter_context(nc.psum_tensor("psB", [128, 512], F32))
        psX = _es.enter_context(nc.psum_tensor("psX", [128, 512], F32))
        psM0 = _es.enter_context(nc.psum_tensor("psM0", [128, 1024], F32))
        psM1 = _es.enter_context(nc.psum_tensor("psM1", [128, 1024], F32))
        consts = []
        for dst_sb, src_hbm in [(onec, onecol), (onesc, onescol),
                                (maskc, masksd), (idf, identf), (wcs, convw),
                                (wbs, convb), (hmc, halom)]:
            consts.append(S.step(
                "sync", lambda e, d=dst_sb, s=src_hbm:
                e.dma_start(out=d[:, :], in_=s), dma=True))

        def _cstar(e):
            e.tensor_scalar(cstar[:, :], onesc[:, :], EPS_G, 1.0,
                            ALU.add, ALU.subtract)
            e.drain()
            e.tensor_scalar(cstar[:, :], cstar[:, :], SC_G, None, ALU.mult)
            e.drain()
            e.tensor_scalar(cstar[:, :], cstar[:, :], M29, 1.0,
                            ALU.mult, ALU.add)
            e.drain()
            e.tensor_scalar(cstar[:, :], cstar[:, :], C1, None, ALU.mult)
            e.drain()
            e.tensor_scalar(ncstar[:, :], cstar[:, :], -1.0, None, ALU.mult)
            e.tensor_scalar(epsc[:, :], onesc[:, :], 1e-5, None, ALU.mult)
            e.drain()
            return e.memset(f2row[:, :], 0.0)
        st["cstar"] = S.step("vector", _cstar, deps=consts)
        S.flush()

        def ternarize(wTd, nchunks, ccols, widx, tdst_sb, spill_hbm, gate=None):
            tg = f"_t{widx}"
            done = []
            nsub = ccols // 1024
            with (
                nc.sbuf_tensor("wfa" + tg, [128, 2 * ccols], F32) as wfa,
                nc.sbuf_tensor("ub" + tg, [128, 2 * 1024], F32) as ub,
                nc.sbuf_tensor("tch" + tg, [128, 2 * ccols], BF16) as tch,
                nc.sbuf_tensor("cacc" + tg, [128, 1], F32) as cacc,
                nc.sbuf_tensor("cx" + tg, [128, 1], F32) as cx,
                nc.sbuf_tensor("mt" + tg, [1, 1], F32) as mt,
            ):
                ld1 = {}
                sm = {}
                for c in range(nchunks):
                    ld1[c] = S.step("sync", lambda e, c=c: e.dma_start(
                        out=wfa[:, (c % 2) * ccols:(c % 2 + 1) * ccols],
                        in_=wTd[c * 128:(c + 1) * 128, :]),
                        deps=[sm.get(c - 2), gate if c < 2 else None],
                        dma=True)

                    def _s(e, c=c):
                        e.reduce_sum(cx[:, :],
                                     wfa[:, (c % 2) * ccols:(c % 2 + 1) * ccols],
                                     AX.X, apply_absolute_value=True)
                        e.drain()
                        if c == 0:
                            return e.tensor_copy(cacc[:, :], cx[:, :])
                        return e.tensor_add(cacc[:, :], cacc[:, :], cx[:, :])
                    sm[c] = S.step("vector", _s, deps=[ld1[c]], small=True)

                cs = S.step("tensor", lambda e: e.matmul(
                    psA[0:1, 0:1], cacc[:, :], onesc[:, :],
                    start=True, stop=True), deps=[sm[nchunks - 1]] + consts)

                def _mean(e):
                    e.tensor_scalar(mt[:, :], psA[0:1, 0:1],
                                    1.0 / (nchunks * 128 * ccols), None,
                                    ALU.mult)
                    e.drain()
                    return e.tensor_scalar(mt[:, :], mt[:, :], 1e-5, None,
                                           ALU.max)
                mean = S.step("vector", _mean, deps=[cs], small=True)

                bc = S.step("tensor", lambda e: e.matmul(
                    psB[:, 0:1], onec[:, :], mt[:, :], start=True, stop=True),
                    deps=[mean])

                def _mc(e):
                    e.tensor_copy(cx[:, :], psB[:, 0:1])
                    e.drain()
                    e.reciprocal(swb[:, widx:widx + 1], cx[:, :])
                    md = m1d if widx == 0 else m2d
                    sc = (1.0 / 127.0) if widx == 0 else (0.5 / 127.0)
                    e.tensor_scalar(md[:, :], cx[:, :], sc, None, ALU.mult)
                    return e.drain()
                mc = S.step("vector", _mc, deps=[bc, st["cstar"]])

                ld2 = {}
                clip = {}
                spills = {}
                for c in range(nchunks):
                    lastclip = clip.get((c - 2, nsub - 1))
                    ld2[c] = S.step("sync", lambda e, c=c: e.dma_start(
                        out=wfa[:, (c % 2) * ccols:(c % 2 + 1) * ccols],
                        in_=wTd[c * 128:(c + 1) * 128, :]),
                        deps=[mc, lastclip,
                              sm.get(nchunks - 1) if c < 2 else None],
                        dma=True)
                    for u in range(nsub):
                        L = c * nsub + u
                        pL = L - 2
                        pclip = (clip.get((pL // nsub, pL % nsub))
                                 if pL >= 0 else None)

                        def _sc(e, c=c, u=u, L=L):
                            src = wfa[:, (c % 2) * ccols + u * 1024:
                                      (c % 2) * ccols + (u + 1) * 1024]
                            uu = ub[:, (L % 2) * 1024:(L % 2 + 1) * 1024]
                            e.activation(uu, src, AF.Identity,
                                         scale=swb[:, widx:widx + 1],
                                         bias=cstar[:, :])
                            return e.activation(uu, uu, AF.Identity,
                                                bias=ncstar[:, :])
                        scs = S.step("scalar", _sc, deps=[ld2[c], mc, pclip])

                        def _cl(e, c=c, u=u, L=L):
                            uu = ub[:, (L % 2) * 1024:(L % 2 + 1) * 1024]
                            if tdst_sb is not None:
                                dstv = tdst_sb[:, c * ccols + u * 1024:
                                               c * ccols + (u + 1) * 1024]
                            else:
                                dstv = tch[:, (c % 2) * ccols + u * 1024:
                                           (c % 2) * ccols + (u + 1) * 1024]
                            return e.tensor_scalar(dstv, uu, 1.0, -1.0,
                                                   ALU.min, ALU.max)
                        clip[(c, u)] = S.step(
                            "vector", _cl,
                            deps=[scs, spills.get(c - 2)])

                    if spill_hbm is not None:
                        spills[c] = S.step("scalar", lambda e, c=c:
                                           e.dma_start(
                                               out=spill_hbm[c * 128:
                                                             (c + 1) * 128, :],
                                               in_=tch[:, (c % 2) * ccols:
                                                       (c % 2 + 1) * ccols]),
                                           deps=[clip[(c, nsub - 1)]],
                                           dma=True)
                        done.append(spills[c])
                    else:
                        done.append(clip[(c, nsub - 1)])
                S.flush()
            return done

        if last_call == -1:
            def _md(e):
                return e.dma_start(out=dsmall[:, :], in_=mdin)
            _l = S.step("sync", _md, dma=True)
            def _md2(e):
                e.tensor_copy(m1d[:, :], dsmall[:, 0:1])
                e.tensor_copy(m2d[:, :], dsmall[:, 1:2])
                e.tensor_copy(swb[:, :], dsmall[:, 2:4])
                return e.drain()
            _c = S.step("vector", _md2, deps=[_l, st["cstar"]])
            S.flush()
            t1_done = [_c]
            t2_done = [_c]
        else:
            t1_done = ternarize(w1Td, NDC, DH, 0, t1T, None)
            t2_done = ternarize(w2Td, NHC, D, 1, None, t2hbm,
                                gate=t1_done[-1])

        zsteps = []
        with nc.sbuf_tensor("zt", [128, D], F32) as zt:
            zld = S.step("sync", lambda e: e.dma_start(out=zt[:, :], in_=zpad),
                         deps=[t2_done[-1]], dma=True)
            zs = [(0, 1)]
            r = T + 3
            while r < slab:
                h = min(128, slab - r)
                zs.append((r, h))
                r += h
            for (r, h) in zs:
                zsteps.append(S.step("gpsimd", lambda e, r=r, h=h: e.dma_start(
                    out=x1hbm[r:r + h, :], in_=zt[0:h, :]),
                    deps=[zld], dma=True))
            S.flush()

        st["psx_free"] = st["cstar"]

        def s1_phase(call, n1T):
            tg = f"_s{call}"
            src = xslab if call == 1 else x1hbm
            with (
                nc.sbuf_tensor("xr" + tg, [128, 3 * D], F32) as xr,
                nc.sbuf_tensor("x2r" + tg, [128, 2 * D], F32) as x2r,
                nc.sbuf_tensor("lnb" + tg, [128, D], F32) as lnb,
                nc.sbuf_tensor("tqb" + tg, [128, D], F32) as tqb,
                nc.sbuf_tensor("n1r" + tg, [128, 2 * D], BF16) as n1r,
                nc.sbuf_tensor("bns" + tg, [128, 12], F32) as bns,
                nc.sbuf_tensor("agg" + tg, [128, 2], F32) as agg,
                nc.sbuf_tensor("rstd" + tg, [128, 1], F32) as rstd,
                nc.sbuf_tensor("nrs" + tg, [128, 1], F32) as nrs,
                nc.sbuf_tensor("nmr" + tg, [128, 1], F32) as nmr,
                nc.sbuf_tensor("rmc" + tg, [128, 1], F32) as rmc,
                nc.sbuf_tensor("s1c" + tg, [128, 1], F32) as s1c,
                nc.sbuf_tensor("f1c" + tg, [128, 1], F32) as f1c,
            ):
                ld, x2s, x2sp, stat, lng, pres, rnd, trn, f1t, f1e, qsd = \
                    {}, {}, {}, {}, {}, {}, {}, {}, {}, {}, {}
                extra = ([st["x1w"], st["av_done"]] + zsteps
                         if call == 2 else [t2_done[-1]])
                for k in range(tiles):
                    ld[k] = S.step("sync", lambda e, k=k: e.dma_start(
                        out=xr[:, (k % 3) * D:(k % 3 + 1) * D],
                        in_=src[k * 128:(k + 1) * 128, :]),
                        deps=[(lng.get(k - 3) if call == 1 else x2s.get(k - 3))]
                        + (extra if k < 3 else []), dma=True)

                    if call == 2:
                        def _x2(e, k=k):
                            dst = x2r[:, (k % 2) * D:(k % 2 + 1) * D]
                            e.tensor_add(dst,
                                         xr[:, (k % 3) * D:(k % 3 + 1) * D],
                                         avtile[:, :])
                            return e.tensor_scalar(dst, dst, hmc[:, k:k + 1],
                                                   None, ALU.mult)
                        x2s[k] = S.step("vector", _x2,
                                        deps=[ld[k], st["av_done"],
                                              x2sp.get(k - 2), lng.get(k - 2)])
                        x2sp[k] = S.step("scalar", lambda e, k=k: e.dma_start(
                            out=x2hbm[k * 128:(k + 1) * 128, :],
                            in_=x2r[:, (k % 2) * D:(k % 2 + 1) * D]),
                            deps=[x2s[k]], dma=True)

                    def xt(k):
                        return (x2r[:, (k % 2) * D:(k % 2 + 1) * D]
                                if call == 2
                                else xr[:, (k % 3) * D:(k % 3 + 1) * D])

                    def _st(e, k=k):
                        e.bn_stats(bns[:, 0:6], xt(k)[:, 0:512])
                        e.bn_stats(bns[:, 6:12], xt(k)[:, 512:1024])
                        e.drain()
                        return e.bn_aggr(agg[:, :], bns[:, :])
                    stat[k] = S.step("vector", _st,
                                     deps=[(ld[k] if call == 1 else x2s[k]),
                                           lng.get(k - 1)], small=True)

                    sq = S.step("scalar", lambda e: e.activation(
                        rstd[:, :], agg[:, 1:2], AF.Sqrt, bias=epsc[:, :]),
                        deps=[stat[k], pres.get(k - 1)], small=True)

                    def _rv(e):
                        e.reciprocal(rstd[:, :], rstd[:, :])
                        e.drain()
                        e.tensor_scalar(nrs[:, :], rstd[:, :], -1.0,
                                        None, ALU.mult)
                        return e.drain()
                    rv = S.step("vector", _rv, deps=[sq])

                    def _ln(e, k=k):
                        e.activation(nmr[:, :], agg[:, 0:1], AF.Identity,
                                     scale=nrs[:, :])
                        e.drain()
                        return e.activation(lnb[:, :], xt(k), AF.Identity,
                                            scale=rstd[:, :], bias=nmr[:, :])
                    lng[k] = S.step("scalar", _ln, deps=[rv])

                    def _qs(e, k=k):
                        e.reduce_max(rmc[:, :], lnb[:, :], AX.X,
                                     apply_absolute_value=True)
                        e.drain()
                        e.tensor_scalar(rmc[:, :], rmc[:, :], 1e-5, None,
                                        ALU.max)
                        e.drain()
                        e.reciprocal(s1c[:, :], rmc[:, :])
                        e.drain()
                        e.tensor_scalar(s1c[:, :], s1c[:, :], 127.0, None,
                                        ALU.mult)
                        e.tensor_tensor(f1c[:, :], rmc[:, :],
                                        m1d[:, :], ALU.mult)
                        return e.drain()
                    qsd[k] = S.step("vector", _qs,
                                    deps=[lng[k], f1t.get(k - 1)])

                    pres[k] = S.step("scalar", lambda e, k=k: e.activation(
                        tqb[:, :], lnb[:, :], AF.Identity, scale=s1c[:, :]),
                        deps=[qsd[k], rnd.get(k - 1)])

                    def _rnd(e, k=k):
                        e.tensor_scalar(tqb[:, :], tqb[:, :], cstar[:, :],
                                        None, ALU.add)
                        return e.tensor_scalar(
                            n1r[:, (k % 2) * D:(k % 2 + 1) * D], tqb[:, :],
                            cstar[:, :], None, ALU.subtract)
                    rnd[k] = S.step("vector", _rnd,
                                    deps=[pres[k], trn.get(k - 2)])

                    for i in range(NDC):
                        t = S.step(
                            "scalar", lambda e, k=k, i=i:
                            e.dma_start_transpose(
                                n1T[:, i * slab + k * 128:
                                    i * slab + (k + 1) * 128],
                                n1r[:, (k % 2) * D + i * 128:
                                    (k % 2) * D + (i + 1) * 128]),
                            deps=[rnd[k]], dma=True)
                    trn[k] = t

                    f1t[k] = S.step("tensor", lambda e: e.transpose(
                        psX[0:1, 0:128], f1c[:, :], idf[:, :]),
                        deps=[qsd[k], f1e.get(k - 1), st["psx_free"]])
                    f1e[k] = S.step("vector", lambda e, k=k: e.tensor_copy(
                        f1row[:, k * 128:(k + 1) * 128], psX[0:1, 0:128]),
                        deps=[f1t[k]], small=True)
                S.flush()
                st["f1row_done"] = f1e[tiles - 1]
                st["n1T_done"] = trn[tiles - 1]
                st["psx_free"] = f1e[tiles - 1]
                if call == 2:
                    st["x2sp_last"] = x2sp[tiles - 1]

        def a_phase(call, n1T):
            tg = f"_a{call}"
            wins = P["win1"] if call == 1 else P["win2"]
            coff = 1 if call == 1 else 2
            gw = WMAX - 2
            spills = []
            with (
                nc.sbuf_tensor("Gwin" + tg, [128, NHC * gw], F32) as Gwin,
                nc.sbuf_tensor("n2w" + tg, [128, 2 * gw], BF16) as n2w,
                nc.sbuf_tensor("Hb" + tg, [128, 2 * WMAX], F32) as Hb,
                nc.sbuf_tensor("Cb" + tg, [128, 2 * WMAX], F32) as Cb,
                nc.sbuf_tensor("absb" + tg, [128, WMAX], F32) as absb,
                nc.sbuf_tensor("n2tmp" + tg, [128, WMAX], F32) as n2tmp,
                nc.sbuf_tensor("Macc" + tg, [128, WMAX], F32) as Macc,
                nc.sbuf_tensor("f1b" + tg, [128, WMAX], F32) as f1b,
                nc.sbuf_tensor("s2b" + tg, [128, WMAX], F32) as s2b,
                nc.sbuf_tensor("rm2c" + tg, [128, 1], F32) as rm2c2,
            ):
                deq, a1, a23, sab, mx, mul, rnds, spl = \
                    {}, {}, {}, {}, {}, {}, {}, {}
                prev_s2e = None
                for w, (a, wd) in enumerate(wins):
                    gl = wd - 2
                    bc = S.step("tensor", lambda e, a=a, wd=wd: e.matmul(
                        psX[:, 0:wd], onec[:, :], f1row[:, a:a + wd],
                        start=True, stop=True),
                        deps=[st["f1row_done"], st["psx_free"], prev_s2e])
                    f1be = S.step("vector", lambda e, wd=wd: e.tensor_copy(
                        f1b[:, 0:wd], psX[:, 0:wd]), deps=[bc])

                    for j in range(NHC):
                        ps = psA if j % 2 == 0 else psB
                        pdq = (deq.get((w, j - 2)) if j >= 2
                               else deq.get((w - 1, NHC + j - 2)))

                        def _mm(e, a=a, wd=wd, j=j, ps=ps):
                            last = None
                            for i in range(NDC):
                                last = e.matmul(
                                    ps[:, 0:wd],
                                    t1T[:, i * DH + j * 128:
                                        i * DH + (j + 1) * 128],
                                    n1T[:, i * slab + a:i * slab + a + wd],
                                    start=(i == 0), stop=(i == NDC - 1))
                            return last
                        mm = S.step("tensor", _mm,
                                    deps=[st["n1T_done"], t1_done[-1], pdq])

                        deq[(w, j)] = S.step(
                            "vector", lambda e, wd=wd, j=j, ps=ps:
                            e.tensor_mul(
                                Hb[:, (j % 2) * WMAX:(j % 2) * WMAX + wd],
                                ps[:, 0:wd], f1b[:, 0:wd]),
                            deps=[mm, f1be, a23.get((w, j - 2)) if j >= 2
                                  else a23.get((w - 1, NHC + j - 2))])

                        a1[(w, j)] = S.step(
                            "scalar", lambda e, gl=gl, j=j: e.activation(
                                Cb[:, (j % 2) * WMAX:(j % 2) * WMAX + gl],
                                Hb[:, (j % 2) * WMAX:(j % 2) * WMAX + gl],
                                AF.Identity, scale=wcs[:, 3 * j:3 * j + 1]),
                            deps=[deq[(w, j)],
                                  sab.get((w, j - 2)) if j >= 2
                                  else sab.get((w - 1, NHC + j - 2))])

                        def _a23(e, gl=gl, j=j):
                            H0 = Hb[:, (j % 2) * WMAX:]
                            Cj = Cb[:, (j % 2) * WMAX:]
                            e.scalar_tensor_tensor(
                                Cj[:, 0:gl], H0[:, 1:1 + gl],
                                wcs[:, 3 * j + 1:3 * j + 2], Cj[:, 0:gl],
                                ALU.mult, ALU.add)
                            return e.scalar_tensor_tensor(
                                Cj[:, 0:gl], H0[:, 2:2 + gl],
                                wcs[:, 3 * j + 2:3 * j + 3], Cj[:, 0:gl],
                                ALU.mult, ALU.add)
                        a23[(w, j)] = S.step("vector", _a23, deps=[a1[(w, j)]])

                        def _sab(e, gl=gl, j=j):
                            return e.activation(Gwin[:, j * gw:j * gw + gl],
                                                Cb[:, (j % 2) * WMAX:
                                                   (j % 2) * WMAX + gl],
                                                AF.Silu, bias=wbs[:, j:j + 1])
                        sab[(w, j)] = S.step(
                            "scalar", _sab,
                            deps=[a23[(w, j)], mul.get((w - 1, j)),
                                  mx.get((w, j - 1))])

                        def _mx(e, gl=gl, j=j):
                            G = Gwin[:, j * gw:j * gw + gl]
                            if j == 0:
                                return e.scalar_tensor_tensor(
                                    Macc[:, 0:gl], G, -1.0, G,
                                    ALU.mult, ALU.max)
                            e.scalar_tensor_tensor(
                                Macc[:, 0:gl], G, -1.0, Macc[:, 0:gl],
                                ALU.mult, ALU.max)
                            return e.scalar_tensor_tensor(
                                Macc[:, 0:gl], G, 1.0, Macc[:, 0:gl],
                                ALU.mult, ALU.max)
                        mx[(w, j)] = S.step("vector", _mx, deps=[sab[(w, j)]])

                    cpr = None
                    for tt in range((gl + 127) // 128):
                        twc = min(128, gl - tt * 128)
                        trM = S.step(
                            "tensor", lambda e, tt=tt, twc=twc: e.transpose(
                                psX[0:twc, 384:512],
                                Macc[:, tt * 128:tt * 128 + twc], idf[:, :]),
                            deps=[mx[(w, NHC - 1)], f1be, cpr,
                                  st["psx_free"]])
                        rmr = S.step(
                            "vector", lambda e, twc=twc: e.reduce_max(
                                rm2c2[0:twc, 0:1], psX[0:twc, 384:512],
                                AX.X), deps=[trM], small=True)
                        trC = S.step(
                            "tensor", lambda e: e.transpose(
                                psX[0:1, 384:512], rm2c2[:, 0:1], idf[:, :]),
                            deps=[rmr])
                        cpr = S.step(
                            "vector", lambda e, tt=tt, twc=twc: e.tensor_copy(
                                rm2row[0:1, tt * 128:tt * 128 + twc],
                                psX[0:1, 384:384 + twc]),
                            deps=[trC], small=True)
                    rm = cpr

                    def _s2(e, a=a, gl=gl):
                        e.tensor_scalar(rm2row[:, 0:gl], rm2row[:, 0:gl],
                                        1e-5, None, ALU.max)
                        e.drain()
                        e.reciprocal(s2row[:, 0:gl], rm2row[:, 0:gl])
                        e.drain()
                        e.tensor_scalar(s2row[:, 0:gl], s2row[:, 0:gl],
                                        127.0, None, ALU.mult)
                        e.tensor_scalar(f2row[:, a + 1:a + 1 + gl],
                                        rm2row[:, 0:gl],
                                        m2d[0:1, 0:1], None, ALU.mult)
                        return e.drain()
                    s2st = S.step("vector", _s2, deps=[rm])

                    s2bc = S.step("tensor", lambda e, gl=gl: e.matmul(
                        psX[:, 0:gl], onec[:, :], s2row[:, 0:gl],
                        start=True, stop=True), deps=[s2st, f1be])
                    s2e = S.step("vector", lambda e, gl=gl: e.tensor_copy(
                        s2b[:, 0:gl], psX[:, 0:gl]), deps=[s2bc])
                    prev_s2e = s2e
                    st["psx_free"] = s2e

                    for j in range(NHC):
                        mul[(w, j)] = S.step(
                            "vector", lambda e, gl=gl, j=j: e.tensor_mul(
                                n2tmp[:, 0:gl], Gwin[:, j * gw:j * gw + gl],
                                s2b[:, 0:gl]),
                            deps=[s2e, sab[(w, j)], rnds.get((w, j - 1))])

                        def _rn(e, gl=gl, j=j):
                            e.activation(n2tmp[:, 0:gl], n2tmp[:, 0:gl],
                                         AF.Identity, bias=cstar[:, :])
                            return e.activation(
                                n2w[:, (j % 2) * gw:(j % 2) * gw + gl],
                                n2tmp[:, 0:gl], AF.Identity,
                                bias=ncstar[:, :])
                        rnds[(w, j)] = S.step(
                            "scalar", _rn,
                            deps=[mul[(w, j)], spl.get((w, j - 2)) if j >= 2
                                  else spl.get((w - 1, NHC + j - 2))])

                        spl[(w, j)] = S.step(
                            "scalar", lambda e, a=a, gl=gl, j=j: e.dma_start(
                                out=n2hbm[:, j, a + 1 - coff:
                                          a + 1 - coff + gl],
                                in_=n2w[:, (j % 2) * gw:(j % 2) * gw + gl]),
                            deps=[rnds[(w, j)]], dma=True)
                        spills.append(spl[(w, j)])
                S.flush()
            st["spills_last"] = spills[-1]
            st["f2row_done"] = spills[-1]

        def b_phase(call):
            tg = f"_b{call}"
            bts = P["bt1"] if call == 1 else P["bt2"]
            coff = 1 if call == 1 else 2
            resid = xslab if call == 1 else x2hbm
            nbt = len(bts)
            with (
                nc.sbuf_tensor("t2T" + tg, [128, NHC * D], BF16) as t2T,
                nc.sbuf_tensor("n2t" + tg, [128, 2 * NHC * 128], BF16) as n2t,
                nc.sbuf_tensor("xrb" + tg, [128, 2 * D], F32) as xrb,
                nc.sbuf_tensor("odb" + tg, [128, D], F32) as odb,
                nc.sbuf_tensor("x1b" + tg, [128, 2 * D], F32) as x1b,
                nc.sbuf_tensor("f2c" + tg, [128, len(bts)], F32) as f2c,
                nc.sbuf_tensor("ln2b" + tg, [128, D], F32) as ln2b,
                nc.sbuf_tensor("bns2" + tg, [128, 12], F32) as bns2,
                nc.sbuf_tensor("agg2" + tg, [128, 2], F32) as agg2,
                nc.sbuf_tensor("rstd2" + tg, [128, 1], F32) as rstd2,
                nc.sbuf_tensor("nrs2" + tg, [128, 1], F32) as nrs2,
                nc.sbuf_tensor("nmr2" + tg, [128, 1], F32) as nmr2,
            ):
                t2ld = []
                for c in range(NHC):
                    t2ld.append(S.step("sync", lambda e, c=c: e.dma_start(
                        out=t2T[:, c * D:(c + 1) * D],
                        in_=t2hbm[c * 128:(c + 1) * 128, :]),
                        deps=[t2_done[-1], st["spills_last"]], dma=True))

                n2ld, xld, f2t, f2e, mm2, od, x1s, ln2g, msum, x1dma = \
                    {}, {}, {}, {}, {}, {}, {}, {}, {}, {}
                vacc = {}
                for k, (sb0, tw) in enumerate(bts):
                    f2t[k] = S.step("tensor", lambda e, sb0=sb0: e.transpose(
                        psX[0:128, 0:1], f2row[:, sb0:sb0 + 128],
                        idf[0:1, 0:1]),
                        deps=[st["f2row_done"], st["psx_free"],
                              f2e.get(k - 1)])
                    f2e[k] = S.step("scalar", lambda e, k=k: e.copy(
                        f2c[:, k:k + 1], psX[0:128, 0:1]), deps=[f2t[k]],
                        small=True)
                    st["psx_free"] = f2e[k]
                for k, (sb0, tw) in enumerate(bts):
                    n2ld[k] = S.step(
                        "gpsimd", lambda e, k=k, sb0=sb0, tw=tw: e.dma_start(
                            out=n2t[:, (k % 2) * NHC * 128:
                                    (k % 2 + 1) * NHC * 128]
                            .rearrange("p (c t) -> p c t", c=NHC)[:, :, 0:tw],
                            in_=n2hbm[:, :, sb0 - coff:sb0 - coff + tw]),
                        deps=[st["spills_last"], mm2.get(k - 2)], dma=True)

                    xld[k] = S.step(
                        "sync", lambda e, k=k, sb0=sb0, tw=tw: e.dma_start(
                            out=xrb[:, (k % 2) * D:(k % 2 + 1) * D][0:tw, :],
                            in_=resid[sb0:sb0 + tw, :]),
                        deps=[x1s.get(k - 2)] +
                        (([st["x2sp_last"]] if call == 2
                          else [st["spills_last"]]) if k < 2 else []),
                        dma=True)

                    def _m2(e, k=k, tw=tw):
                        psM = psM0 if k % 2 == 0 else psM1
                        last = None
                        for j in range(NHC):
                            stt = n2t[:, (k % 2) * NHC * 128 + j * 128:
                                      (k % 2) * NHC * 128 + j * 128 + tw]
                            for h in range(2):
                                last = e.matmul(
                                    psM[0:tw, h * 512:(h + 1) * 512], stt,
                                    t2T[:, j * D + h * 512:
                                        j * D + (h + 1) * 512],
                                    start=(j == 0), stop=(j == NHC - 1))
                        return last
                    mm2[k] = S.step("tensor", _m2,
                                    deps=[n2ld[k], od.get(k - 2), t2ld[-1]])

                    od[k] = S.step("scalar", lambda e, k=k, tw=tw:
                                   e.activation(
                                       odb[0:tw, :],
                                       (psM0 if k % 2 == 0 else psM1)[0:tw, :],
                                       AF.Identity,
                                       scale=f2c[0:tw, k:k + 1]),
                                   deps=[mm2[k], f2e[len(bts) - 1],
                                         x1s.get(k - 1)])

                    def _x1(e, k=k, tw=tw):
                        xv = xrb[:, (k % 2) * D:(k % 2 + 1) * D][0:tw, :]
                        x1v = x1b[:, (k % 2) * D:(k % 2 + 1) * D][0:tw, :]
                        inst = e.tensor_add(x1v, xv, odb[0:tw, :])
                        if call == 1:
                            e.bn_stats(bns2[0:tw, 0:6], x1v[:, 0:512])
                            e.bn_stats(bns2[0:tw, 6:12], x1v[:, 512:1024])
                            e.drain()
                            inst = e.bn_aggr(agg2[0:tw, :], bns2[0:tw, :])
                        return inst
                    x1s[k] = S.step("vector", _x1,
                                    deps=[od[k], xld[k], x1dma.get(k - 2),
                                          ln2g.get(k - 1)],
                                    small=(call == 1))

                    if call == 1:
                        sq2 = S.step("scalar", lambda e: e.activation(
                            rstd2[:, :], agg2[:, 1:2], AF.Sqrt, bias=epsc[:, :]),
                            deps=[x1s[k], msum.get(k - 1)], small=True)

                        def _rv2(e):
                            e.reciprocal(rstd2[:, :], rstd2[:, :])
                            e.drain()
                            e.tensor_scalar(nrs2[:, :], rstd2[:, :],
                                            -1.0, None, ALU.mult)
                            return e.drain()
                        rv2 = S.step("vector", _rv2, deps=[sq2])

                        def _l2(e, k=k, tw=tw):
                            e.activation(nmr2[:, :], agg2[:, 0:1], AF.Identity,
                                         scale=nrs2[:, :])
                            e.drain()
                            return e.activation(
                                ln2b[0:tw, :],
                                x1b[:, (k % 2) * D:(k % 2 + 1) * D][0:tw, :],
                                AF.Identity, scale=rstd2[0:tw, :],
                                bias=nmr2[0:tw, :])
                        ln2g[k] = S.step("scalar", _l2, deps=[rv2])

                        def _msa(e, k=k, tw=tw):
                            return e.matmul(
                                psX[0:1, 0:512], maskc[0:tw, k:k + 1],
                                ln2b[0:tw, 0:512], start=True, stop=True)
                        msa = S.step("tensor", _msa,
                                     deps=[ln2g[k], vacc.get(k - 1),
                                           f2e[len(bts) - 1]])

                        def _va(e, k=k):
                            if k == 0:
                                return e.tensor_copy(sacc[0:1, 0:512],
                                                     psX[0:1, 0:512])
                            return e.tensor_add(sacc[0:1, 0:512],
                                                sacc[0:1, 0:512],
                                                psX[0:1, 0:512])
                        va = S.step("vector", _va, deps=[msa])

                        def _msb(e, k=k, tw=tw):
                            return e.matmul(
                                psX[0:1, 0:512], maskc[0:tw, k:k + 1],
                                ln2b[0:tw, 512:1024], start=True, stop=True)
                        msb = S.step("tensor", _msb, deps=[va])

                        def _vb(e, k=k):
                            if k == 0:
                                return e.tensor_copy(sacc[0:1, 512:1024],
                                                     psX[0:1, 0:512])
                            return e.tensor_add(sacc[0:1, 512:1024],
                                                sacc[0:1, 512:1024],
                                                psX[0:1, 0:512])
                        vacc[k] = S.step("vector", _vb, deps=[msb])
                        msum[k] = vacc[k]

                    def _dma(e, k=k, sb0=sb0, tw=tw):
                        if call == 1:
                            return e.dma_start(
                                out=x1hbm[sb0:sb0 + tw, :],
                                in_=x1b[:, (k % 2) * D:(k % 2 + 1) * D]
                                [0:tw, :])
                        return e.dma_start(
                            out=outd[sb0 - 2:sb0 - 2 + tw, :],
                            in_=x1b[:, (k % 2) * D:(k % 2 + 1) * D][0:tw, :])
                    x1dma[k] = S.step("gpsimd", _dma, deps=[x1s[k]], dma=True)
                S.flush()
                st["x1w"] = x1dma[nbt - 1]
                st["outs"] = [x1dma[k] for k in range(nbt)]
                if call == 1:
                    st["pss_done"] = msum[nbt - 1]
                    st["psx_free"] = msum[nbt - 1]

        def attn_phase():
            with (
                nc.sbuf_tensor("wch", [128, 2 * D], F32) as wch,
                nc.sbuf_tensor("vsrow", [1, D], F32) as vsrow,
                nc.sbuf_tensor("avrow", [1, D], F32) as avrow,
                nc.sbuf_tensor("sTrow", [1, D], F32) as sTrow,
            ):
                d1 = S.step("gpsimd", lambda e: e.dma_start(
                    out=ccin[:, :], in_=sacc[0:1, :]),
                    deps=[st["pss_done"]], dma=True)
                if saccdbg is not None:
                    S.step("gpsimd", lambda e: e.dma_start(
                        out=saccdbg, in_=sacc[0:1, :]),
                        deps=[st["pss_done"]], dma=True)
                if use_cc:
                    cc = S.step("gpsimd", lambda e: e.collective_compute(
                        "AllReduce", ALU.add, replica_groups=groups,
                        ins=[ccin.ap().opt()], outs=[ccout.ap().opt()]),
                        deps=[d1], async_c=True)
                    d2 = S.step("gpsimd", lambda e: e.dma_start(
                        out=sTrow[0:1, :], in_=ccout[:, :]), deps=[cc],
                        dma=True)
                else:
                    d2 = S.step("gpsimd", lambda e: e.dma_start(
                        out=sTrow[0:1, :], in_=ccin[:, :]), deps=[d1],
                        dma=True)
                stt = {}
                for i in range(NDC):
                    stt[i] = S.step("tensor", lambda e, i=i: e.transpose(
                        psX[0:128, i:i + 1],
                        sTrow[0:1, i * 128:(i + 1) * 128], idf[0:1, 0:1]),
                        deps=[d2, st["psx_free"]])
                ste = S.step("vector", lambda e: e.tensor_copy(
                    sTc[:, :], psX[0:128, 0:NDC]), deps=[stt[NDC - 1]],
                    small=True)
                if stdbg is not None:
                    S.step("scalar", lambda e: e.dma_start(
                        out=stdbg, in_=sTrow[0:1, :]), deps=[d2], dma=True)
                st["psx_free"] = ste

                wld, vmm = {}, {}
                for i in range(NDC):
                    wld[i] = S.step("sync", lambda e, i=i: e.dma_start(
                        out=wch[:, (i % 2) * D:(i % 2 + 1) * D],
                        in_=wvTd[i * 128:(i + 1) * 128, :]),
                        deps=[d2, vmm.get(i - 2), st["x1w"]], dma=True)

                    def _v(e, i=i):
                        e.matmul(psA[0:1, 0:512], sTc[:, i:i + 1],
                                 wch[:, (i % 2) * D:(i % 2) * D + 512],
                                 start=(i == 0), stop=(i == NDC - 1))
                        return e.matmul(
                            psB[0:1, 0:512], sTc[:, i:i + 1],
                            wch[:, (i % 2) * D + 512:(i % 2 + 1) * D],
                            start=(i == 0), stop=(i == NDC - 1))
                    vmm[i] = S.step("tensor", _v, deps=[wld[i], ste])

                def _vse(e):
                    e.tensor_copy(vsrow[0:1, 0:512], psA[0:1, 0:512])
                    e.tensor_copy(vsrow[0:1, 512:1024],
                                  psB[0:1, 0:512])
                    return e.drain()
                vse = S.step("vector", _vse, deps=[vmm[NDC - 1]])
                if vsdbg is not None:
                    S.step("scalar", lambda e: e.dma_start(
                        out=vsdbg, in_=vsrow[0:1, :]), deps=[vse], dma=True)

                vst = {}
                for i in range(NDC):
                    vst[i] = S.step("tensor", lambda e, i=i: e.transpose(
                        psX[0:128, i:i + 1],
                        vsrow[0:1, i * 128:(i + 1) * 128], idf[0:1, 0:1]),
                        deps=[vse, st["psx_free"]])
                vste = S.step("vector", lambda e: e.tensor_copy(
                    vsT[:, :], psX[0:128, 0:NDC]), deps=[vst[NDC - 1]],
                    small=True)
                st["psx_free"] = vste

                wld2, amm = {}, {}
                for i in range(NDC):
                    wld2[i] = S.step("sync", lambda e, i=i: e.dma_start(
                        out=wch[:, (i % 2) * D:(i % 2 + 1) * D],
                        in_=woTd[i * 128:(i + 1) * 128, :]),
                        deps=[vmm[NDC - 1], amm.get(i - 2)], dma=True)

                    def _a(e, i=i):
                        e.matmul(psA[0:1, 0:512], vsT[:, i:i + 1],
                                 wch[:, (i % 2) * D:(i % 2) * D + 512],
                                 start=(i == 0), stop=(i == NDC - 1))
                        return e.matmul(
                            psB[0:1, 0:512], vsT[:, i:i + 1],
                            wch[:, (i % 2) * D + 512:(i % 2 + 1) * D],
                            start=(i == 0), stop=(i == NDC - 1))
                    amm[i] = S.step("tensor", _a, deps=[wld2[i], vste, vse])

                def _ave(e):
                    e.tensor_copy(avrow[0:1, 0:512], psA[0:1, 0:512])
                    return e.tensor_copy(avrow[0:1, 512:1024],
                                         psB[0:1, 0:512])
                ave = S.step("vector", _ave, deps=[amm[NDC - 1]])

                def _avbc(e):
                    e.matmul(psM0[:, 0:512], onec[:, :], avrow[0:1, 0:512],
                             start=True, stop=True)
                    return e.matmul(psM0[:, 512:1024], onec[:, :],
                                    avrow[0:1, 512:1024],
                                    start=True, stop=True)
                avbc = S.step("tensor", _avbc, deps=[ave])
                avt = S.step("vector", lambda e: e.tensor_copy(
                    avtile[:, :], psM0[:, :]), deps=[avbc])
                if avdbg is not None:
                    S.step("gpsimd", lambda e: e.dma_start(
                        out=avdbg, in_=avtile[:, :]), deps=[avt], dma=True)
                S.flush()
                st["av_done"] = avt

        with nc.sbuf_tensor("n1Ta", [128, NDC * slab], BF16) as n1Ta:
            s1_phase(1, n1Ta)
            if last_call <= 0:
                S.step("gpsimd", lambda e: e.dma_start(
                    out=n1dbg, in_=n1Ta[:, :]), deps=[st["n1T_done"]], dma=True)
                S.step("scalar", lambda e: e.dma_start(
                    out=f1dbg, in_=f1row[:, :]),
                    deps=[st["f1row_done"]], dma=True)
                fin0 = S.step("gpsimd", lambda e: e.dma_start(
                    out=t1dbg, in_=t1T[:, :]), deps=[t1_done[-1]], dma=True)
                sdmp = S.step("scalar", lambda e: e.dma_start(
                    out=s1dbg, in_=dstage[:, :]), deps=[st["n1T_done"]],
                    dma=True)
                S.step("gpsimd", lambda e: e.engine_nop(),
                       deps=[fin0, sdmp])
                S.flush()
            if last_call > 0 and t1dbg is not None:
                def _css(e):
                    e.tensor_copy(csstage[:, 0:1], cstar[:, :])
                    e.tensor_copy(csstage[:, 1:2], ncstar[:, :])
                    e.tensor_copy(csstage[:, 2:3], epsc[:, :])
                    return e.tensor_copy(csstage[:, 3:4], m1d[:, :])
                _c1 = S.step("vector", _css, deps=[t1_done[-1]])
                S.step("scalar", lambda e: e.dma_start(out=csdbg,
                                                       in_=csstage[:, :]),
                       deps=[_c1], dma=True)
                S.step("gpsimd", lambda e: e.dma_start(out=t1dbg, in_=t1T[:, :]),
                       deps=[t1_done[-1]], dma=True)
                S.step("gpsimd", lambda e: e.dma_start(out=n1dbg, in_=n1Ta[:, :]),
                       deps=[st["n1T_done"]], dma=True)
                S.step("gpsimd", lambda e: e.dma_start(out=f1dbg, in_=f1row[:, :]),
                       deps=[st["f1row_done"]], dma=True)
            a_phase(1, n1Ta)
        b_phase(1)
        if last_call <= 0:
            return nc, P
        if last_call == 2:
            attn_phase()
            with nc.sbuf_tensor("n1Tb", [128, NDC * slab], BF16) as n1Tb:
                s1_phase(2, n1Tb)
                a_phase(2, n1Tb)
            b_phase(2)

        S.step("gpsimd", lambda e: e.engine_nop(),
               deps=st["outs"] + [st["x1w"]])
        S.flush()

    return nc, P




def make_core_inputs(c, xflat, Sq, shared, P, cores_per_batch):
    T, slab, tiles = P["T"], P["slab"], P["tiles"]
    batch = c // cores_per_batch
    tok0 = (c % cores_per_batch) * T
    xs = np.zeros((slab, D), dtype=np.float32)
    lo = max(tok0 - 2, 0)
    hi = min(tok0 + T + 2, Sq)
    xs[(lo - tok0) + 2:(hi - tok0) + 2] = \
        xflat[batch * Sq + lo: batch * Sq + hi]
    masks = np.zeros((128, tiles), dtype=np.float32)
    for k, (sb0, tw) in enumerate(P["bt1"]):
        for p in range(tw):
            if 2 <= sb0 + p < T + 2:
                masks[p, k] = 1.0
    hm = np.ones((128, tiles), dtype=np.float32)
    if tok0 == 0:
        hm[1, 0] = 0.0
    if tok0 + T == Sq:
        hm[(T + 2) % 128, (T + 2) // 128] = 0.0
    d = dict(shared)
    d["xslab"] = xs
    d["masks"] = masks
    d["halomask"] = hm
    return d


def kernel(L0_attn_x, L0_attn_w_qkv, L0_attn_w_o, L0_ffn_w1, L0_ffn_conv_w,
           L0_ffn_conv_b, L0_ffn_w2):
    x = np.ascontiguousarray(np.asarray(L0_attn_x, dtype=np.float32))
    B, Sq, _ = x.shape
    n_cores = 8
    cores_per_batch = n_cores // B
    T = Sq // cores_per_batch
    xflat = x.reshape(B * Sq, D)

    w1T = np.ascontiguousarray(np.asarray(L0_ffn_w1, np.float32).T)
    w2T = np.ascontiguousarray(np.asarray(L0_ffn_w2, np.float32).T)
    cwr = np.asarray(L0_ffn_conv_w, np.float32).reshape(DH, 3)
    cw = np.zeros((128, 3 * NHC), np.float32)
    cb = np.zeros((128, NHC), np.float32)
    cbin = np.asarray(L0_ffn_conv_b, np.float32)
    for j in range(NHC):
        cw[:, 3 * j:3 * j + 3] = cwr[j * 128:(j + 1) * 128, :]
        cb[:, j] = cbin[j * 128:(j + 1) * 128]
    wqkv = np.asarray(L0_attn_w_qkv, np.float32)
    shared = {
        "w1T": w1T, "w2T": w2T, "convw": cw, "convb": cb,
        "wvT": np.ascontiguousarray(wqkv[2 * D:3 * D, :].T),
        "woT": np.ascontiguousarray(np.asarray(L0_attn_w_o, np.float32).T),
        "onecol": np.ones((1, 128), np.float32),
        "onescol": np.ones((128, 1), np.float32),
        "identf": np.eye(128, dtype=np.float32),
        "zpad": np.zeros((128, D), np.float32),
    }

    nc, P = build_program(T, n_cores, cores_per_batch, use_cc=True)
    in_maps = [make_core_inputs(c, xflat, Sq, shared, P, cores_per_batch)
               for c in range(n_cores)]
    res = run_bass_kernel_spmd(nc, in_maps, core_ids=list(range(n_cores)))
    out = np.concatenate([res.results[c]["out"] for c in range(n_cores)],
                         axis=0)
    return out.reshape(B, Sq, D)
